# version 01
# baseline (speedup 1.0000x reference)
import numpy as np
import ml_dtypes
import concourse.bass as bass
import concourse.mybir as mybir
from concourse.bass_utils import run_bass_kernel_spmd
from contextlib import ExitStack

F32 = mybir.dt.float32
F16 = mybir.dt.float16
BF16 = mybir.dt.bfloat16
ALU = mybir.AluOpType
ACT = mybir.ActivationFunctionType
AX = mybir.AxisListType.X

NCORES = 8
DBG_CHUNK = -1
PARANOID = False
B, L, D = 16, 2048, 1024
BPC = B // NCORES
DI = 2048
DTR = 64
NH, DH = 16, 64
NE, DFF = 8, 2048
TC1 = 256
TC2 = 512
MMDT = F16

ENGINES = ("sync", "vector", "scalar", "tensor", "gpsimd")
DTB = {F32: 4, F16: 2, BF16: 2, mybir.dt.int32: 4}


class Prog:
    def __init__(self, nc, es):
        self.nc = nc
        self.es = es
        self.items = {e: [] for e in ENGINES}
        self.cnt = {}
        self.sems = {}

    def sem(self, name):
        if name not in self.sems:
            self.sems[name] = self.es.enter_context(self.nc.semaphore(name))
            self.cnt[name] = 0
        return name

    HAZARD_OPS = ("reduce_sum", "reduce_max", "tensor_reduce",
                  "tensor_tensor_scan", "reciprocal", "bn_stats", "bn_aggr")
    ENG_SEM = {"vector": "tv", "gpsimd": "tg", "scalar": "tsl"}

    def _small_out(self, meth, a, kw):
        if kw.get("accum_out") is not None:
            return True
        obj = kw.get("out")
        if obj is None and a:
            obj = a[0]
        shp = getattr(obj, "shape", None)
        if not shp or len(shp) < 2:
            return False
        free = 1
        for d in list(shp)[1:]:
            free *= int(d)
        return free < 64

    def op(self, eng, meth, *a, inc=None, **kw):
        hazard = eng in ("vector", "gpsimd", "scalar") and (
            meth in self.HAZARD_OPS or self._small_out(meth, a, kw))
        if hazard and inc is None:
            inc = (self.ENG_SEM[eng], 1)
        if inc is not None:
            self.cnt[inc[0]] += inc[1]
            semname = inc[0]
            inc = (self.sems[inc[0]], inc[1])
        self.items[eng].append((meth, a, kw, inc))
        if hazard:
            self.items[eng].append(("wait_ge", (inc[0], self.cnt[semname]), {}, None))

    def inc_last(self, eng, semname, n=1):
        idx = len(self.items[eng]) - 1
        while idx >= 0 and self.items[eng][idx][0] == "wait_ge":
            idx -= 1
        meth, a, kw, inc = self.items[eng][idx]
        if inc is not None:
            assert inc[0] is self.sems[semname], f"{meth} inc on other sem"
            return
        self.cnt[semname] += n
        self.items[eng][idx] = (meth, a, kw, (self.sems[semname], n))

    def mark(self, semname):
        return self.cnt[semname]

    def ensure_inc(self, eng, semname):
        self.inc_last(eng, semname)

    def wait(self, eng, semname, val=None):
        v = self.cnt[semname] if val is None else val
        if v > 0:
            self.items[eng].append(("wait_ge", (self.sems[semname], v), {}, None))

    def dma(self, eng, qsem, out, in_, **kw):
        self.op(eng, "dma_start", inc=(qsem, 16), out=out, in_=in_, **kw)

    def build(self, block):
        for eng in ENGINES:
            if not self.items[eng]:
                continue
            items = self.items[eng]

            def body(h, items=items):
                for meth, a, kw, inc in items:
                    r = getattr(h, meth)(*a, **kw)
                    if inc is not None:
                        r.then_inc(inc[0], inc[1])
            getattr(block, eng)(body)

    def nops(self):
        return {e: len(v) for e, v in self.items.items()}

    def fence(self, engines=ENGINES, sems=("tv", "tt", "tsl", "qs", "qg")):
        for eng in engines:
            if not self.items[eng]:
                continue
            meth, a, kw, inc = self.items[eng][-1]
            if inc is None and meth != "wait_ge":
                sm = {"sync": "qs", "gpsimd": "tg", "vector": "tv",
                      "scalar": "tsl", "tensor": "tt"}[eng]
                if meth == "dma_start":
                    sm = "qs" if eng == "sync" else "qg"
                    self.inc_last(eng, sm, 16)
                else:
                    self.inc_last(eng, sm)
        vals = {s: self.cnt[s] for s in list(sems) + ["tg"]}
        for eng in engines:
            for s, v in vals.items():
                if v > 0:
                    self.items[eng].append(("wait_ge", (self.sems[s], v), {}, None))


class Alloc:
    def __init__(self, nc, es, label=""):
        self.nc = nc
        self.es = es
        self.used = 0
        self.label = label

    def sb(self, name, shape, dtype):
        t = self.es.enter_context(self.nc.sbuf_tensor(name, shape, dtype))
        self.used += int(np.prod(shape[1:])) * DTB[dtype]
        return t

    def ps(self, name, shape, dtype=F32):
        return self.es.enter_context(self.nc.psum_tensor(name, shape, dtype))


def _tm(dram2d, t0, nt, w=D):
    return dram2d[t0:t0 + nt * 128, :].rearrange("(n p) c -> p n c", p=128)


def build_core(nc, dbg=False, stop_after=None):
    dr = {}

    def din(name, shape, dtype=F32):
        dr[name] = nc.dram_tensor(name, list(shape), dtype, kind="ExternalInput").ap()
        return dr[name]

    x_in = din("x", (BPC, L, D))
    w_in_t = din("w_in_t", (D, 2 * DI))
    w_x_t = din("w_x_t", (DI, 96))
    w_dt_t = din("w_dt_t", (DTR, DI))
    w_out_t = din("w_out_t", (DI, D))
    gamma_in = din("gamma", (D,))
    step_in = din("step", (1,))
    convw_in = din("convw", (DI, 4))
    convb_in = din("convb", (DI,))
    bdt_in = din("bdt", (DI,))
    ident_in = din("ident", (128, 128))
    n1w_in = din("n1w", (D,))
    n2w_in = din("n2w", (D,))
    wq_in = din("wq", (D, D)); wk_in = din("wk", (D, D))
    wv_in = din("wv", (D, D)); wo_in = din("wo", (D, D))
    rw_in = din("rw", (D, NE))
    shg_in = din("shg", (D, DFF)); shu_in = din("shu", (D, DFF))
    shd_in = din("shd", (DFF, D))
    eg_in = din("eg", (NE, D, DFF)); eu_in = din("eu", (NE, D, DFF))
    ed_in = din("ed", (NE, DFF, D))

    out_d = nc.dram_tensor("out", [BPC, L, D], F32, kind="ExternalOutput").ap()
    sk = "ExternalOutput" if dbg else "Internal"
    ys_d = nc.dram_tensor("ys", [BPC, DI, L], MMDT, kind=sk).ap()
    dbg_d = {}
    if dbg:
        for nm, sh, dt_ in [("d_xn", [128, 2048], MMDT), ("d_xc", [128, 4096], F32),
                            ("d_z", [128, 4096], F32), ("d_cv", [128, 4096], F32),
                            ("d_u", [128, 4096], MMDT), ("d_spz", [128, 4096], MMDT),
                            ("d_dec", [128, 4096], F32), ("d_h", [128, 4096], F32),
                            ("d_spy", [128, 4096], F32), ("d_spr", [128, 4096], F32),
                            ("d_sq", [128, 4096], F32), ("d_sqin", [128, 4096], F32),
                            ("d_wq", [128, 512], MMDT), ("d_smc", [128, 16], F32),
                            ("d_rt", [128, 64], F32), ("d_h2", [128, 128], MMDT),
                            ("d_ctx", [128, 512], F32), ("d_den", [128, 512], F32),
                            ("d_pr", [128, 512], F32), ("d_ctxn", [128, 512], F32),
                            ("d_sc2", [128, 512], F32), ("d_qs", [128, 512], MMDT),
                            ("d_kc", [128, 512], MMDT),
                            ("d_xncm", [128, 2048], MMDT)]:
            dbg_d[nm] = nc.dram_tensor(nm, sh, dt_, kind="ExternalOutput").ap()
    x1_d = nc.dram_tensor("x1s", [BPC, L, D], F16, kind=sk).ap()
    x2_d = nc.dram_tensor("x2s", [BPC, L, D], F16, kind=sk).ap()
    ss_d = nc.dram_tensor("sss", [BPC, L, D], F16, kind=sk).ap()
    den_d = nc.dram_tensor("dens", [NH, TC2], F32, kind="Internal").ap()
    ce_d = nc.dram_tensor("ces", [NE, 2048], F16, kind="Internal").ap()

    es0 = ExitStack()
    nc._mk_es0 = es0
    P = Prog(nc, es0)
    A0 = Alloc(nc, es0)

    for s in ("qs", "qg", "tv", "tt", "tsl", "tg", "qx", "qw", "qyl",
              "qys", "qx1", "qo", "qdbg"):
        P.sem(s)

    ident_b = A0.sb("ident_b", [128, 128], MMDT)
    ident_f = A0.sb("ident_f", [128, 128], F32)
    gamma_sb = A0.sb("gamma_sb", [128, 8], F32)
    n1w_sb = A0.sb("n1w_sb", [128, 8], F32)
    n2w_sb = A0.sb("n2w_sb", [128, 8], F32)
    convw_sb = A0.sb("convw_sb", [128, 64], F32)
    convb_sb = A0.sb("convb_sb", [128, 16], F32)
    bdt_sb = A0.sb("bdt_sb", [128, 16], F32)
    step_sb = A0.sb("step_sb", [1, 1], F32)
    ones_sb = A0.sb("ones_sb", [128, 1], F32)
    ones_mat = A0.sb("ones_mat", [128, 128], MMDT)
    stepv = A0.sb("stepv", [128, 1], F32)
    stepv16 = A0.sb("stepv16", [128, 1], MMDT)
    acc16 = A0.sb("acc16", [128, 1], MMDT)
    ones_b = A0.sb("ones_b", [128, 1], MMDT)
    flag_sb = A0.sb("flag_sb", [128, 1], F32)
    eps5_sb = A0.sb("eps5_sb", [128, 1], F32)
    eps6_sb = A0.sb("eps6_sb", [128, 1], F32)
    smc = A0.sb("smc", [128, 16], F32)
    ident_h = A0.sb("ident_h", [128, 128], F16)

    P.dma("sync", "qs", out=ident_f[:], in_=ident_in)
    P.dma("sync", "qs", out=gamma_sb[:], in_=gamma_in.rearrange("(k p) -> p k", p=128), allow_slow_non_contiguous=True)
    P.dma("sync", "qs", out=n1w_sb[:], in_=n1w_in.rearrange("(k p) -> p k", p=128), allow_slow_non_contiguous=True)
    P.dma("sync", "qs", out=n2w_sb[:], in_=n2w_in.rearrange("(k p) -> p k", p=128), allow_slow_non_contiguous=True)
    P.dma("sync", "qs", out=convw_sb[:].rearrange("p (c k) -> p c k", k=4),
          in_=convw_in.rearrange("(c p) k -> p c k", p=128), allow_slow_non_contiguous=True)
    P.dma("sync", "qs", out=convb_sb[:], in_=convb_in.rearrange("(c p) -> p c", p=128), allow_slow_non_contiguous=True)
    P.dma("sync", "qs", out=bdt_sb[:], in_=bdt_in.rearrange("(c p) -> p c", p=128), allow_slow_non_contiguous=True)
    P.dma("sync", "qs", out=step_sb[:, 0:1], in_=step_in.rearrange("(a o) -> a o", a=1))
    ld0 = P.mark("qs")

    P.wait("vector", "qs", ld0)
    P.op("vector", "memset", ones_sb[:], 1.0)
    P.op("vector", "memset", ones_mat[:], 1.0)
    P.op("vector", "memset", smc[:], 0.0)
    P.op("vector", "memset", ones_b[:], 1.0)
    P.op("vector", "memset", eps5_sb[:], 1e-5)
    P.op("vector", "memset", eps6_sb[:], 1e-6)
    P.op("vector", "tensor_copy", out=ident_b[:], in_=ident_f[:])
    P.op("vector", "tensor_copy", out=ident_h[:], in_=ident_f[:])
    P.op("vector", "tensor_scalar", out=flag_sb[:], in0=ones_sb[:],
         scalar1=2.0 ** -30, scalar2=1.0, op0=ALU.add, op1=ALU.subtract)
    P.op("vector", "tensor_scalar", out=flag_sb[:], in0=flag_sb[:],
         scalar1=0.0, scalar2=None, op0=ALU.is_gt, inc=("tv", 1))
    P.op("vector", "memset", stepv[:], 0.0)
    P.op("vector", "tensor_copy", out=stepv[0:1, 0:1], in_=step_sb[:, 0:1])
    P.op("vector", "tensor_copy", out=stepv16[:], in_=stepv[:], inc=("tv", 1))

    def quantize(w_dram, K, M, wq_sb, fold_const, fold_gamma, wld, wmask, ps_sm, scol):
        nkt = (K + 127) // 128
        kp = min(K, 128)
        acc = smc[:, scol:scol + 1]
        red = smc[:, scol + 1:scol + 2]
        for kt in range(nkt):
            P.wait("sync", "tv", P.mark("tv"))
            P.dma("sync", "qw", out=wld[:kp, :M], in_=w_dram[kt * 128:kt * 128 + kp, :])
            P.wait("vector", "qw")
            P.op("vector", "reduce_sum", out=red[:kp, :], in_=wld[:kp, :M], axis=AX,
                 apply_absolute_value=True)
            if kt == 0:
                P.op("vector", "memset", acc[:], 0.0)
            P.op("vector", "tensor_add", out=acc[:kp, :], in0=acc[:kp, :],
                 in1=red[:kp, :], inc=("tv", 1))
        P.op("vector", "tensor_copy", out=acc16[:], in_=acc[:], inc=("tv", 1))
        P.wait("tensor", "tv", P.mark("tv"))
        P.op("tensor", "matmul", ps_sm[:, 0:1], ones_mat[:], acc16[:],
             start=True, stop=True, inc=("tt", 1))
        P.wait("vector", "tt", P.mark("tt"))
        thr = smc[:, scol + 2:scol + 3]
        fold = smc[:, scol + 3:scol + 4]
        nthr = smc[:, scol + 4:scol + 5]
        P.op("vector", "tensor_scalar", out=thr, in0=ps_sm[:, 0:1],
             scalar1=1.0 / (K * M), scalar2=1e-8, op0=ALU.mult, op1=ALU.add)
        P.op("vector", "tensor_scalar", out=fold, in0=thr,
             scalar1=fold_const, scalar2=None, op0=ALU.mult)
        if fold_gamma:
            P.op("vector", "tensor_scalar", out=fold, in0=fold,
                 scalar1=smc[:, 15:16], scalar2=None, op0=ALU.mult)
        P.op("vector", "tensor_scalar", out=thr, in0=thr,
             scalar1=0.5, scalar2=None, op0=ALU.mult)
        P.op("vector", "tensor_scalar", out=nthr, in0=thr, scalar1=-1.0, scalar2=None,
             op0=ALU.mult, inc=("tv", 1))
        for kt in range(nkt):
            P.wait("sync", "tv", P.mark("tv"))
            P.dma("sync", "qw", out=wld[:kp, :M], in_=w_dram[kt * 128:kt * 128 + kp, :])
            P.wait("vector", "qw")
            P.op("vector", "tensor_scalar", out=wmask[:kp, :M], in0=wld[:kp, :M],
                 scalar1=thr[:kp, :], scalar2=None, op0=ALU.is_gt)
            P.op("vector", "scalar_tensor_tensor", out=wmask[:kp, :M],
                 in0=wld[:kp, :M], scalar=nthr[:kp, :], in1=wmask[:kp, :M],
                 op0=ALU.is_lt, op1=ALU.subtract)
            if fold_gamma:
                gsc = smc[:, scol + 5:scol + 6]
                P.op("vector", "tensor_scalar", out=gsc, in0=gamma_sb[:, kt:kt + 1],
                     scalar1=fold, scalar2=-1.0, op0=ALU.mult, op1=ALU.mult)
                P.op("vector", "tensor_scalar", out=wq_sb[:kp, kt * M:kt * M + M],
                     in0=wmask[:kp, :M], scalar1=gsc[:kp, :], scalar2=None,
                     op0=ALU.mult, inc=("tv", 1))
            else:
                P.op("vector", "tensor_scalar", out=wq_sb[:kp, kt * M:kt * M + M],
                     in0=wmask[:kp, :M], scalar1=fold[:kp, :], scalar2=-1.0,
                     op0=ALU.mult, op1=ALU.mult, inc=("tv", 1))

    es1 = ExitStack()
    A1 = Alloc(nc, es1, "m")
    wqin = A1.sb("wqin", [128, 8 * 4096], MMDT)
    wqx = A1.sb("wqx", [128, 16 * 96], MMDT)
    wqdt = A1.sb("wqdt", [128, DI], MMDT)
    x0b = A1.sb("x0b", [128, 2 * 2048], F32)
    y32 = A1.sb("y32", [128, 16 * TC1], F32)
    r16 = A1.sb("r16", [128, 16 * TC1], F16)
    t16 = A1.sb("t16", [128, 16 * TC1], F16)
    stats = A1.sb("stats", [128, 16], F32)
    carry = A1.sb("carry", [128, 16], F32)
    dump = A1.sb("dump", [128, 1024], F32)
    ps_sm = A1.ps("ps_sm", [128, 16], F32)

    es1a = ExitStack()
    A1a = Alloc(nc, es1a, "m1only")
    xc0 = A1a.sb("xc0", [128, 2048], F32)
    xn_tm = A1a.sb("xn_tm", [128, 2048], MMDT)
    xn_cm = A1a.sb("xn_cm", [128, 8 * TC1], MMDT)
    xc_sb = A1a.sb("xc_sb", [128, 16 * TC1], F32)
    z_sb = A1a.sb("z_sb", [128, 16 * TC1], F32)
    spz = A1a.sb("spz", [128, 16 * TC1], MMDT)
    u16 = A1a.sb("u16", [128, 16 * TC1], MMDT)
    b16 = A1a.sb("b16", [128, 16 * TC1], F16)
    dbl_sb = A1a.sb("dbl_sb", [128, TC1], MMDT)
    hal = A1a.sb("hal", [128, 48], F32)
    ps_xz = [A1a.ps(f"ps_xz{i}", [128, TC1], F32) for i in range(2)]
    ps_tr = [A1a.ps(f"ps_tr{i}", [128, 128], MMDT) for i in range(2)]
    ps_dt = [A1a.ps(f"ps_dt{i}", [128, TC1], F32) for i in range(2)]
    dt_sb = xc_sb
    h_sb = y32

    P.wait("tensor", "tv", P.mark("tv"))
    P.op("tensor", "matmul", ps_sm[:, 0:1], ones_mat[:], stepv16[:],
         start=True, stop=True, inc=("tt", 1))
    P.wait("vector", "tt", P.mark("tt"))
    P.op("vector", "tensor_copy", out=smc[:, 15:16], in_=ps_sm[:, 0:1],
         inc=("tv", 1))

    wcast_marks = []
    a_x1_free = [("tv", 0)]
    sc_free_marks = []
    ctx_free_marks = []
    wo_ev_marks = []
    x2_store_m = [0]
    q_free = [0]
    den_prev = [0]
    dgrp_flush = [0]
    tro_free = [0]
    f_x2_free = [("tv", 0)]
    wgu_free = [0]
    wd_free = [0]
    ceb_free = [0]
    act_marks = []
    dn_marks = []
    out_store_m = [0]
    out_t_marks = []
    moet_free = [0]
    rt_free = [("tv", 0), ("tsl", 0)]
    quantize(w_in_t, D, 2 * DI, wqin, 1.0, True, y32[:, 0:4096], r16[:, 0:4096],
             ps_sm, 0)
    if dbg:
        P.ensure_inc("vector", "tv")
        P.wait("gpsimd", "tv", P.mark("tv"))
        P.dma("gpsimd", "qdbg", out=dbg_d["d_wq"], in_=wqin[:, 0:512])
        P.dma("gpsimd", "qdbg", out=dbg_d["d_smc"], in_=smc[:])
        P.wait("vector", "qdbg")
    quantize(w_x_t, DI, 96, wqx, 0.5, False, y32[:, 0:96], r16[:, 0:96], ps_sm, 6)
    quantize(w_dt_t, DTR, DI, wqdt, 1.0, False, y32[:, 0:DI], r16[:, 0:DI], ps_sm, 0)

    def sp_emit(v_in, out_f16, W, dump_init=False):
        y = y32[:, 0:W]; r = r16[:, 0:W]; t = t16[:, 0:W]
        P.ensure_inc("vector", "tv")
        P.wait("scalar", "tv", P.mark("tv"))
        P.op("scalar", "activation", out=y, in_=v_in, func=ACT.Square,
             bias=0.0, scale=1.0, inc=("tsl", 1))
        P.wait("vector", "tsl", P.mark("tsl"))
        if dump_init:
            P.wait("gpsimd", "tsl", P.mark("tsl"))
            P.dma("gpsimd", "qdbg", out=dbg_d["d_sq"], in_=y)
            P.dma("gpsimd", "qdbg", out=dbg_d["d_sqin"], in_=v_in)
            P.wait("vector", "qdbg")
            P.wait("scalar", "qdbg")
        P.op("vector", "tensor_scalar", out=y, in0=y, scalar1=4.0, scalar2=1e6,
             op0=ALU.add, op1=ALU.min)
        P.op("vector", "tensor_scalar", out=r, in0=y, scalar1=16.0, scalar2=-0.375,
             op0=ALU.is_gt, op1=ALU.mult)
        P.op("vector", "tensor_scalar", out=t, in0=y, scalar1=64.0, scalar2=-0.0625,
             op0=ALU.is_gt, op1=ALU.mult)
        P.op("vector", "tensor_add", out=r, in0=r, in1=t)
        P.op("vector", "tensor_scalar", out=t, in0=y, scalar1=256.0, scalar2=-0.03125,
             op0=ALU.is_gt, op1=ALU.mult)
        P.op("vector", "tensor_add", out=r, in0=r, in1=t)
        P.op("vector", "tensor_scalar", out=r, in0=r, scalar1=0.5, scalar2=None,
             op0=ALU.add)
        if dump_init:
            P.inc_last("vector", "tv")
            P.wait("gpsimd", "tv", P.mark("tv"))
            P.dma("gpsimd", "qdbg", out=dbg_d["d_spy"], in_=y)
            P.dma("gpsimd", "qdbg", out=dbg_d["d_spr"], in_=r)
            P.wait("vector", "qdbg")
        for _ in range(3):
            P.op("vector", "tensor_mul", out=t, in0=r, in1=r)
            P.op("vector", "tensor_mul", out=t, in0=t, in1=y)
            P.op("vector", "tensor_scalar", out=t, in0=t, scalar1=-0.5, scalar2=1.5,
                 op0=ALU.mult, op1=ALU.add)
            P.op("vector", "tensor_mul", out=r, in0=r, in1=t)
            P.op("vector", "tensor_scalar", out=r, in0=r, scalar1=1e-6, scalar2=1000.0,
                 op0=ALU.max, op1=ALU.min)
        P.op("vector", "tensor_mul", out=t, in0=y, in1=r)
        P.op("vector", "tensor_scalar", out=t, in0=t, scalar1=0.0, scalar2=1000.0,
             op0=ALU.max, op1=ALU.min)
        P.op("vector", "tensor_add", out=out_f16, in0=v_in, in1=t)

    nch1 = L // TC1
    NT1 = TC1 // 128
    x0_free = {}
    win_done = {}
    xn_ready_prev = [0]
    y_store_marks = {}
    ev_marks = []
    tr_ev_marks = []
    db_marks = []

    for bb in range(BPC):
        for ci in range(nch1):
            gi = bb * nch1 + ci
            par = gi % 2
            T0 = ci * TC1
            x0v = x0b[:, par * 2048:par * 2048 + 2048]
            if gi >= 2:
                P.wait("sync", "tv", x0_free[gi - 2])
            P.dma("sync", "qx", out=x0v.rearrange("p (n c) -> p n c", c=1024),
                  in_=_tm(x_in[bb], T0, NT1))
            P.wait("scalar", "qx")
            if gi >= 1:
                P.wait("scalar", "tv", xn_ready_prev[0])
            for tt2 in range(NT1):
                P.op("scalar", "activation", out=xc0[:, tt2 * 1024:tt2 * 1024 + 1024],
                     in_=x0v[:, tt2 * 1024:tt2 * 1024 + 1024], func=ACT.Copy,
                     bias=0.0, scale=1.0, accum_out=stats[:, tt2:tt2 + 1])
            P.inc_last("scalar", "tsl")
            P.wait("vector", "tsl", P.mark("tsl"))
            for tt2 in range(NT1):
                mean = stats[:, 4 + tt2:5 + tt2]
                P.op("vector", "tensor_scalar", out=mean, in0=stats[:, tt2:tt2 + 1],
                     scalar1=1.0 / 1024, scalar2=None, op0=ALU.mult)
                P.op("vector", "tensor_scalar", out=xc0[:, tt2 * 1024:tt2 * 1024 + 1024],
                     in0=xc0[:, tt2 * 1024:tt2 * 1024 + 1024],
                     scalar1=mean, scalar2=None, op0=ALU.subtract)
            P.inc_last("vector", "tv")
            x0_free[gi] = P.mark("tv")
            P.wait("scalar", "tv", P.mark("tv"))
            for tt2 in range(NT1):
                P.op("scalar", "activation", out=dump[:],
                     in_=xc0[:, tt2 * 1024:tt2 * 1024 + 1024], func=ACT.Square,
                     bias=0.0, scale=1.0, accum_out=stats[:, tt2:tt2 + 1])
            P.inc_last("scalar", "tsl")
            P.wait("vector", "tsl", P.mark("tsl"))
            for tt2 in range(NT1):
                var = stats[:, 4 + tt2:5 + tt2]
                inv = stats[:, 6 + tt2:7 + tt2]
                msk = stats[:, 7 + tt2:8 + tt2]
                P.op("vector", "tensor_scalar", out=var, in0=stats[:, tt2:tt2 + 1],
                     scalar1=1.0 / 1024, scalar2=1e-9, op0=ALU.mult, op1=ALU.add)
                P.op("vector", "tensor_scalar", out=inv, in0=var, scalar1=4.0,
                     scalar2=-0.5, op0=ALU.is_ge, op1=ALU.mult)
                for thv, dv in [(16.0, -0.25), (64.0, -0.125), (256.0, -0.0625),
                                (1024.0, -0.03125), (4096.0, -0.015625),
                                (16384.0, -0.0078125), (65536.0, -0.00390625)]:
                    P.op("vector", "tensor_scalar", out=msk, in0=var, scalar1=thv,
                         scalar2=dv, op0=ALU.is_ge, op1=ALU.mult)
                    P.op("vector", "tensor_add", out=inv, in0=inv, in1=msk)
                P.op("vector", "tensor_scalar", out=msk, in0=var, scalar1=0.25,
                     scalar2=1.0, op0=ALU.is_lt, op1=ALU.mult)
                P.op("vector", "tensor_add", out=inv, in0=inv, in1=msk)
                P.op("vector", "tensor_scalar", out=msk, in0=var, scalar1=0.0625,
                     scalar2=2.0, op0=ALU.is_lt, op1=ALU.mult)
                P.op("vector", "tensor_add", out=inv, in0=inv, in1=msk)
                P.op("vector", "tensor_scalar", out=inv, in0=inv, scalar1=1.0,
                     scalar2=None, op0=ALU.add)
                P.op("vector", "tensor_scalar", out=xn_tm[:, tt2 * 1024:tt2 * 1024 + 1024],
                     in0=xc0[:, tt2 * 1024:tt2 * 1024 + 1024], scalar1=inv,
                     scalar2=None, op0=ALU.mult)
            P.inc_last("vector", "tv")
            xn_ready = P.mark("tv")
            xn_ready_prev[0] = xn_ready
            if dbg and gi == DBG_CHUNK:
                P.wait("gpsimd", "tv", P.mark("tv"))
                P.dma("gpsimd", "qdbg", out=dbg_d["d_xn"], in_=xn_tm[:])
                P.wait("tensor", "qdbg")
            P.wait("tensor", "tv", xn_ready)
            if gi >= 1:
                P.wait("scalar", "tt", win_done[gi - 1])
            for idx in range(NT1 * 8):
                tt2, kt = divmod(idx, 8)
                slot = idx % 2
                if len(tr_ev_marks) >= 2:
                    P.wait("tensor", "tsl", tr_ev_marks[-2])
                P.op("tensor", "transpose", ps_tr[slot][:],
                     xn_tm[:, tt2 * 1024 + kt * 128: tt2 * 1024 + kt * 128 + 128],
                     ident_b[:], inc=("tt", 1))
                P.wait("scalar", "tt", P.mark("tt"))
                P.op("scalar", "copy",
                     out=xn_cm[:, kt * TC1 + tt2 * 128: kt * TC1 + tt2 * 128 + 128],
                     in_=ps_tr[slot][:], inc=("tsl", 1))
                tr_ev_marks.append(P.mark("tsl"))
            xncm_ready = P.mark("tsl")
            if dbg and gi == DBG_CHUNK:
                P.wait("gpsimd", "tsl", xncm_ready)
                P.dma("gpsimd", "qdbg", out=dbg_d["d_xncm"], in_=xn_cm[:])
                P.wait("tensor", "qdbg")
            P.wait("tensor", "tsl", xncm_ready)
            for jt in range(32):
                slot = jt % 2
                if len(ev_marks) >= 2:
                    P.wait("tensor", "tv", ev_marks[-2])
                for kt in range(8):
                    P.op("tensor", "matmul", ps_xz[slot][:],
                         wqin[:, kt * 4096 + jt * 128: kt * 4096 + jt * 128 + 128],
                         xn_cm[:, kt * TC1:kt * TC1 + TC1],
                         start=(kt == 0), stop=(kt == 7))
                P.inc_last("tensor", "tt")
                P.wait("vector", "tt", P.mark("tt"))
                dstt = xc_sb if jt < 16 else z_sb
                ct = jt % 16
                P.op("vector", "tensor_copy", out=dstt[:, ct * TC1:ct * TC1 + TC1],
                     in_=ps_xz[slot][:], inc=("tv", 1))
                ev_marks.append(P.mark("tv"))
            win_done[gi] = P.mark("tt")
            if dbg and gi == DBG_CHUNK:
                P.wait("gpsimd", "tv", P.mark("tv"))
                P.dma("gpsimd", "qdbg", out=dbg_d["d_xc"], in_=xc_sb[:])
                P.dma("gpsimd", "qdbg", out=dbg_d["d_z"], in_=z_sb[:])
                P.wait("vector", "qdbg")
            if gi >= 1:
                P.wait("vector", "qys", y_store_marks[gi - 1])
            sp_emit(z_sb[:], spz[:], 16 * TC1)
            if ci == 0:
                P.op("vector", "memset", hal[:], 0.0)
            for ct in range(16):
                xcv = xc_sb[:, ct * TC1:ct * TC1 + TC1]
                ov = z_sb[:, ct * TC1:ct * TC1 + TC1]
                P.op("vector", "tensor_scalar", out=ov, in0=xcv,
                     scalar1=convw_sb[:, ct * 4 + 3:ct * 4 + 4],
                     scalar2=convb_sb[:, ct:ct + 1], op0=ALU.mult, op1=ALU.add)
                for k in range(3):
                    sh = 3 - k
                    P.op("vector", "scalar_tensor_tensor",
                         out=z_sb[:, ct * TC1 + sh:ct * TC1 + TC1],
                         in0=xc_sb[:, ct * TC1:ct * TC1 + TC1 - sh],
                         scalar=convw_sb[:, ct * 4 + k:ct * 4 + k + 1],
                         in1=z_sb[:, ct * TC1 + sh:ct * TC1 + TC1],
                         op0=ALU.mult, op1=ALU.add)
                    P.op("vector", "scalar_tensor_tensor",
                         out=z_sb[:, ct * TC1:ct * TC1 + sh],
                         in0=hal[:, ct * 3 + k:ct * 3 + 3],
                         scalar=convw_sb[:, ct * 4 + k:ct * 4 + k + 1],
                         in1=z_sb[:, ct * TC1:ct * TC1 + sh],
                         op0=ALU.mult, op1=ALU.add)
            P.op("vector", "tensor_copy",
                 out=hal[:].rearrange("p (c k) -> p c k", k=3),
                 in_=xc_sb[:].rearrange("p (c t) -> p c t", t=TC1)[:, :, TC1 - 3:TC1])
            if dbg and gi == DBG_CHUNK:
                P.inc_last("vector", "tv")
                P.wait("gpsimd", "tv", P.mark("tv"))
                P.dma("gpsimd", "qdbg", out=dbg_d["d_spz"], in_=spz[:])
                P.dma("gpsimd", "qdbg", out=dbg_d["d_cv"], in_=z_sb[:])
                P.wait("vector", "qdbg")
            sp_emit(z_sb[:], u16[:], 16 * TC1, dump_init=(dbg and gi == DBG_CHUNK))
            P.inc_last("vector", "tv")
            u_ready = P.mark("tv")
            if dbg and gi == DBG_CHUNK:
                P.wait("gpsimd", "tv", P.mark("tv"))
                P.dma("gpsimd", "qdbg", out=dbg_d["d_u"], in_=u16[:])
                P.wait("vector", "qdbg")
            P.wait("tensor", "tv", u_ready)
            if len(db_marks) >= 1:
                P.wait("tensor", "tv", db_marks[-1])
            for kt in range(16):
                P.op("tensor", "matmul", ps_dt[0][0:96, :],
                     wqx[:, kt * 96:kt * 96 + 96], u16[:, kt * TC1:kt * TC1 + TC1],
                     start=(kt == 0), stop=(kt == 15))
            P.inc_last("tensor", "tt")
            P.wait("vector", "tt", P.mark("tt"))
            P.op("vector", "tensor_copy", out=dbl_sb[0:64, :],
                 in_=ps_dt[0][0:64, :], inc=("tv", 1))
            dbl_ready = P.mark("tv")
            P.wait("tensor", "tv", dbl_ready)
            for mt in range(16):
                slot = mt % 2
                if mt >= 2:
                    P.wait("tensor", "tv", db_marks[-2])
                P.op("tensor", "matmul", ps_dt[slot][:],
                     wqdt[0:64, mt * 128:mt * 128 + 128], dbl_sb[0:64, :],
                     start=True, stop=True, inc=("tt", 1))
                P.wait("vector", "tt", P.mark("tt"))
                P.op("vector", "tensor_scalar", out=dt_sb[:, mt * TC1:mt * TC1 + TC1],
                     in0=ps_dt[slot][:],
                     scalar1=bdt_sb[:, mt:mt + 1], scalar2=None, op0=ALU.add,
                     inc=("tv", 1))
                db_marks.append(P.mark("tv"))
            W = 16 * TC1
            aab = y32[:]
            P.wait("scalar", "tv", P.mark("tv"))
            P.op("scalar", "activation", out=aab, in_=dt_sb[:], func=ACT.Abs,
                 bias=0.0, scale=1.0, inc=("tsl", 1))
            P.wait("vector", "tsl", P.mark("tsl"))
            P.op("vector", "tensor_scalar", out=aab, in0=aab, scalar1=1.0,
                 scalar2=None, op0=ALU.add)
            P.op("vector", "reciprocal", out=aab, in_=aab)
            P.op("vector", "tensor_mul", out=dt_sb[:], in0=dt_sb[:], in1=aab)
            P.op("vector", "tensor_scalar", out=dt_sb[:], in0=dt_sb[:], scalar1=0.5,
                 scalar2=0.5, op0=ALU.mult, op1=ALU.add)
            P.op("vector", "tensor_scalar", out=aab, in0=dt_sb[:], scalar1=32768.0,
                 scalar2=2.0 ** 23, op0=ALU.mult, op1=ALU.add)
            P.op("vector", "tensor_scalar", out=aab, in0=aab, scalar1=2.0 ** 23,
                 scalar2=2.0 ** -15, op0=ALU.subtract, op1=ALU.mult)
            zb = z_sb[:]
            P.op("vector", "tensor_scalar", out=zb, in0=dt_sb[:], scalar1=32768.0,
                 scalar2=2.0 ** 52, op0=ALU.mult, op1=ALU.add)
            P.op("vector", "tensor_scalar", out=zb, in0=zb, scalar1=2.0 ** 52,
                 scalar2=2.0 ** -15, op0=ALU.subtract, op1=ALU.mult)
            P.op("vector", "tensor_sub", out=zb, in0=zb, in1=aab)
            P.op("vector", "scalar_tensor_tensor", out=dt_sb[:], in0=zb,
                 scalar=flag_sb[:], in1=aab, op0=ALU.mult, op1=ALU.add)
            P.op("vector", "tensor_scalar", out=aab, in0=dt_sb[:], scalar1=-1.0,
                 scalar2=1.0, op0=ALU.mult, op1=ALU.add)
            P.op("vector", "tensor_mul", out=b16[:], in0=aab, in1=u16[:])
            for ct in range(16):
                ini = 0.0 if ci == 0 else carry[:, ct:ct + 1]
                P.op("vector", "tensor_tensor_scan",
                     out=h_sb[:, ct * TC1:ct * TC1 + TC1],
                     data0=dt_sb[:, ct * TC1:ct * TC1 + TC1],
                     data1=b16[:, ct * TC1:ct * TC1 + TC1],
                     initial=ini, op0=ALU.mult, op1=ALU.add)
            if dbg and gi == DBG_CHUNK:
                P.inc_last("vector", "tv")
                P.wait("gpsimd", "tv", P.mark("tv"))
                P.dma("gpsimd", "qdbg", out=dbg_d["d_dec"], in_=dt_sb[:])
                P.dma("gpsimd", "qdbg", out=dbg_d["d_h"], in_=h_sb[:])
                P.wait("vector", "qdbg")
            P.op("vector", "tensor_copy", out=carry[:],
                 in_=h_sb[:].rearrange("p (c t) -> p c t", t=TC1)[:, :, TC1 - 1:TC1]
                 .rearrange("p c o -> p (c o)"))
            P.op("vector", "tensor_mul", out=spz[:], in0=h_sb[:], in1=spz[:])
            P.inc_last("vector", "tv")
            y_ready = P.mark("tv")
            P.wait("gpsimd", "tv", y_ready)
            P.dma("gpsimd", "qys",
                  out=ys_d[bb].rearrange("(k p) t -> p k t", p=128)[:, :, T0:T0 + TC1],
                  in_=spz[:].rearrange("p (k t) -> p k t", t=TC1))
            y_store_marks[gi] = P.mark("qys")
            if PARANOID:
                P.fence()
    ys_done = P.mark("qys")

    if stop_after == "m1":
        P.wait("gpsimd", "qys", ys_done)
        P.wait("gpsimd", "qdbg")
        return P, dr, es0

    P.wait("vector", "tt", P.mark("tt"))
    P.wait("sync", "tv", P.mark("tv"))
    es1a.close()
    quantize(w_out_t, DI, D, wqin[:, 0:16 * D], 0.25, False, y32[:, 0:1024],
             r16[:, 0:1024], ps_sm, 0)
    wqout = wqin

    NT2 = TC2 // 128
    nch2 = L // TC2
    mo_cm = None
    es2 = ExitStack()
    A2 = Alloc(nc, es2, "m2")
    ybig = A2.sb("ybig", [128, 16 * TC2], MMDT)
    mo_cm2 = A2.sb("mo_cm2", [128, 8 * TC2], F32)
    mo_tm = A2.sb("mo_tm", [128, 4 * 1024], F32)
    x1c = A2.sb("x1c", [128, 4 * 1024], F16)
    ps_mo = [A2.ps(f"ps_mo{i}", [128, TC2], F32) for i in range(2)]
    ps_tr2 = [A2.ps(f"ps_tr2{i}", [128, 128], F32) for i in range(2)]

    mo_marks = []
    tr2_marks = []
    x0_free2 = {}
    x1_vec_done = {}
    x1_store_marks = {}
    ymark_free = {}
    for bb in range(BPC):
        for ci in range(nch2):
            gi = bb * nch2 + ci
            T0 = ci * TC2
            par = gi % 2
            x0v = x0b[:, par * 2048:par * 2048 + 2048]
            if gi >= 1:
                P.wait("sync", "tv", x0_free2[gi - 1])
            else:
                P.wait("sync", "tv", P.mark("tv"))
                P.wait("sync", "qys", ys_done)
            P.dma("sync", "qx", out=x0b[:].rearrange("p (n c) -> p n c", c=1024),
                  in_=_tm(x_in[bb], T0, NT2))
            x0mark = P.mark("qx")
            if gi >= 1:
                P.wait("sync", "tv", ymark_free[gi - 1])
            P.dma("sync", "qyl",
                  out=ybig[:].rearrange("p (k t) -> p k t", t=TC2),
                  in_=ys_d[bb].rearrange("(k p) t -> p k t", p=128)[:, :, T0:T0 + TC2])
            P.wait("tensor", "qyl")
            if gi >= 1:
                P.wait("tensor", "tv", mo_marks[-1])
            for jt in range(8):
                slot = jt % 2
                if jt >= 2:
                    P.wait("tensor", "tv", mo_marks[-2])
                for kt in range(16):
                    P.op("tensor", "matmul", ps_mo[slot][:],
                         wqout[:, kt * 1024 + jt * 128: kt * 1024 + jt * 128 + 128],
                         ybig[:, kt * TC2:kt * TC2 + TC2],
                         start=(kt == 0), stop=(kt == 15))
                P.inc_last("tensor", "tt")
                P.wait("vector", "tt", P.mark("tt"))
                P.op("vector", "tensor_copy", out=mo_cm2[:, jt * TC2:jt * TC2 + TC2],
                     in_=ps_mo[slot][:], inc=("tv", 1))
                mo_marks.append(P.mark("tv"))
            ymark_free[gi] = P.mark("tv")
            P.wait("tensor", "tv", mo_marks[-1])
            if gi >= 1:
                P.wait("scalar", "tv", x1_vec_done[gi - 1])
            for idx in range(NT2 * 8):
                tt2, kt = divmod(idx, 8)
                slot = idx % 2
                if len(tr2_marks) >= 2:
                    P.wait("tensor", "tsl", tr2_marks[-2])
                P.op("tensor", "transpose", ps_tr2[slot][:],
                     mo_cm2[:, kt * TC2 + tt2 * 128: kt * TC2 + tt2 * 128 + 128],
                     ident_f[:], inc=("tt", 1))
                P.wait("scalar", "tt", P.mark("tt"))
                P.op("scalar", "copy",
                     out=mo_tm[:, tt2 * 1024 + kt * 128: tt2 * 1024 + kt * 128 + 128],
                     in_=ps_tr2[slot][:], inc=("tsl", 1))
                tr2_marks.append(P.mark("tsl"))
            P.wait("vector", "tsl", P.mark("tsl"))
            P.wait("vector", "qx", x0mark)
            P.op("vector", "scalar_tensor_tensor", out=mo_tm[:], in0=x0b[:, 0:4096],
                 scalar=2.0, in1=mo_tm[:], op0=ALU.mult, op1=ALU.add,
                 inc=("tv", 1))
            x0_free2[gi] = P.mark("tv")
            P.wait("scalar", "tv", P.mark("tv"))
            for tt2 in range(NT2):
                P.op("scalar", "activation", out=dump[:],
                     in_=mo_tm[:, tt2 * 1024:tt2 * 1024 + 1024], func=ACT.Copy,
                     bias=0.0, scale=1.0, accum_out=stats[:, tt2:tt2 + 1])
            P.inc_last("scalar", "tsl")
            P.wait("vector", "tsl", P.mark("tsl"))
            for tt2 in range(NT2):
                mean = stats[:, 4 + tt2:5 + tt2]
                P.op("vector", "tensor_scalar", out=mean, in0=stats[:, tt2:tt2 + 1],
                     scalar1=1.0 / 1024, scalar2=None, op0=ALU.mult)
                P.op("vector", "tensor_scalar",
                     out=mo_tm[:, tt2 * 1024:tt2 * 1024 + 1024],
                     in0=mo_tm[:, tt2 * 1024:tt2 * 1024 + 1024],
                     scalar1=mean, scalar2=None, op0=ALU.subtract)
            P.inc_last("vector", "tv")
            P.wait("scalar", "tv", P.mark("tv"))
            for tt2 in range(NT2):
                P.op("scalar", "activation", out=dump[:],
                     in_=mo_tm[:, tt2 * 1024:tt2 * 1024 + 1024], func=ACT.Square,
                     bias=0.0, scale=1.0, accum_out=stats[:, tt2:tt2 + 1])
            for tt2 in range(NT2):
                P.op("scalar", "activation", out=stats[:, 4 + tt2:5 + tt2],
                     in_=stats[:, tt2:tt2 + 1], func=ACT.Sqrt, bias=eps5_sb[:],
                     scale=1.0 / 1024)
            P.inc_last("scalar", "tsl")
            P.wait("vector", "tsl", P.mark("tsl"))
            if gi >= 1:
                P.wait("vector", "qx1", x1_store_marks[gi - 1])
            for tt2 in range(NT2):
                P.op("vector", "reciprocal", out=stats[:, 8 + tt2:9 + tt2],
                     in_=stats[:, 4 + tt2:5 + tt2])
                P.op("vector", "tensor_scalar", out=x1c[:, tt2 * 1024:tt2 * 1024 + 1024],
                     in0=mo_tm[:, tt2 * 1024:tt2 * 1024 + 1024],
                     scalar1=stats[:, 8 + tt2:9 + tt2], scalar2=None, op0=ALU.mult)
            P.inc_last("vector", "tv")
            x1_vec_done[gi] = P.mark("tv")
            P.wait("gpsimd", "tv", P.mark("tv"))
            P.dma("gpsimd", "qx1", out=_tm(x1_d[bb], T0, NT2),
                  in_=x1c[:].rearrange("p (n c) -> p n c", c=1024))
            x1_store_marks[gi] = P.mark("qx1")
            if PARANOID:
                P.fence()
    x1_done_q = P.mark("qx1")

    if stop_after == "m2":
        P.wait("gpsimd", "qx1", x1_done_q)
        P.wait("gpsimd", "qdbg")
        return P, dr, es0

    es2.close()
    es1.close()

    es3 = ExitStack()
    A3 = Alloc(nc, es3, "attn")
    wq_sb = A3.sb("wq_sb", [128, 8 * 1024], MMDT)
    wk_sb = A3.sb("wk_sb", [128, 8 * 1024], MMDT)
    wv_sb = A3.sb("wv_sb", [128, 8 * 1024], MMDT)
    wo_sb = A3.sb("wo_sb", [128, 8 * 1024], MMDT)
    h_cm = A3.sb("h_cm", [128, 8 * L], MMDT)
    k_cm = A3.sb("k_cm", [128, 8 * L], MMDT)
    v_buf = A3.sb("v_buf", [128, 16 * (NH * 65)], BF16)
    q_sb = A3.sb("q_sb", [128, 512], MMDT)
    x1c = A3.sb("x1c_a", [128, 4 * 1024], F16)
    pr_sb = A3.sb("pr_sb", [128, 512], BF16)
    ctx_sb = A3.sb("ctx_sb", [128, 8 * 512], F32)
    ctxn_sb = A3.sb("ctxn_sb", [128, 8 * 512], MMDT)
    ctxo_t1 = A3.sb("ctxo_t1", [128, 1024], F32)
    xsbuf = A3.sb("xsbuf", [128, 1024], F16)
    dcol = A3.sb("dcol", [128, 512], F32)
    den_b = A3.sb("den_b", [128, 512], F32)
    stats3 = A3.sb("stats3", [128, 16], F32)
    assert MMDT == F16, "in-place rmsnorm assumes x1 spill dtype == MMDT"
    htm = x1c

    p_a = A3.ps("p_a", [128, 512], F32)
    p_b = A3.ps("p_b", [128, 512], F32)
    p_c = A3.ps("p_c", [128, 512], F32)
    p_dd = A3.ps("p_dd", [128, 512], F32)
    p_e = A3.ps("p_e", [128, 512], F32)
    p_f = A3.ps("p_f", [128, 512], F32)
    p_tr3 = A3.ps("p_tr3", [128, 128], MMDT)
    p_trf = A3.ps("p_trf", [128, 128], F32)

    wlda = ctx_sb
    P.wait("sync", "qx1", x1_done_q)
    for wi, (wdr, wsb, foldw) in enumerate([(wq_in, wq_sb, True), (wk_in, wk_sb, True),
                                            (wv_in, wv_sb, True), (wo_in, wo_sb, False)]):
        for kt in range(8):
            par = (wi * 8 + kt) % 2
            if len(wcast_marks) >= 2:
                P.wait("sync", "tsl", wcast_marks[-2])
            P.dma("sync", "qw", out=wlda[:, par * 1024:par * 1024 + 1024],
                  in_=wdr[kt * 128:kt * 128 + 128, :])
            P.wait("scalar", "qw")
            P.op("scalar", "activation", out=wsb[:, kt * 1024:kt * 1024 + 1024],
                 in_=wlda[:, par * 1024:par * 1024 + 1024], func=ACT.Copy,
                 bias=0.0, scale=(n1w_sb[:, kt:kt + 1] if foldw else 1.0),
                 inc=("tsl", 1))
            wcast_marks.append(P.mark("tsl"))

    NT2 = TC2 // 128
    for bb in range(BPC):
        trh_marks = []
        kv_marks = []
        P.op("vector", "memset", dcol[:], 1.0)
        P.op("vector", "memset",
             v_buf[:].rearrange("p (tt h d) -> p tt h d", h=NH, d=65)[:, :, :, 64:65],
             1.0, inc=("tv", 1))
        for sc in range(4):
            T0 = sc * TC2
            P.wait("sync", a_x1_free[0][0], a_x1_free[0][1])
            P.dma("sync", "qx", out=x1c[:].rearrange("p (n c) -> p n c", c=1024),
                  in_=_tm(x1_d[bb], T0, NT2))
            P.wait("scalar", "qx")
            for tt2 in range(NT2):
                P.op("scalar", "activation", out=ctx_sb[:, 0:1024],
                     in_=x1c[:, tt2 * 1024:tt2 * 1024 + 1024], func=ACT.Square,
                     bias=0.0, scale=1.0, accum_out=stats3[:, tt2:tt2 + 1])
            for tt2 in range(NT2):
                P.op("scalar", "activation", out=stats3[:, 4 + tt2:5 + tt2],
                     in_=stats3[:, tt2:tt2 + 1], func=ACT.Sqrt, bias=eps6_sb[:],
                     scale=1.0 / 1024)
            P.inc_last("scalar", "tsl")
            P.wait("vector", "tsl", P.mark("tsl"))
            for tt2 in range(NT2):
                P.op("vector", "reciprocal", out=stats3[:, 8 + tt2:9 + tt2],
                     in_=stats3[:, 4 + tt2:5 + tt2])
                P.op("vector", "tensor_scalar", out=htm[:, tt2 * 1024:tt2 * 1024 + 1024],
                     in0=x1c[:, tt2 * 1024:tt2 * 1024 + 1024],
                     scalar1=stats3[:, 8 + tt2:9 + tt2], scalar2=None, op0=ALU.mult)
            P.inc_last("vector", "tv")
            htm_ready = P.mark("tv")
            P.wait("tensor", "tv", htm_ready)
            for idx in range(NT2 * 8):
                tt2, kt = divmod(idx, 8)
                if len(trh_marks) >= 1:
                    P.wait("tensor", "tsl", trh_marks[-1])
                P.op("tensor", "transpose", p_tr3[:],
                     htm[:, tt2 * 1024 + kt * 128: tt2 * 1024 + kt * 128 + 128],
                     ident_b[:], inc=("tt", 1))
                P.wait("scalar", "tt", P.mark("tt"))
                P.op("scalar", "copy",
                     out=h_cm[:, kt * L + T0 + tt2 * 128: kt * L + T0 + tt2 * 128 + 128],
                     in_=p_tr3[:], inc=("tsl", 1))
                trh_marks.append(P.mark("tsl"))
            hch_ready = P.mark("tsl")
            a_x1_free[0] = ("tsl", hch_ready)
            P.wait("tensor", "tsl", hch_ready)
            for ct in range(8):
                pk = (p_a, p_b)[len(kv_marks) % 2]
                if len(kv_marks) >= 2:
                    P.wait("tensor", "tv", kv_marks[-2])
                for kt in range(8):
                    P.op("tensor", "matmul", pk[:],
                         wk_sb[:, kt * 1024 + ct * 128: kt * 1024 + ct * 128 + 128],
                         h_cm[:, kt * L + T0: kt * L + T0 + TC2],
                         start=(kt == 0), stop=(kt == 7))
                P.inc_last("tensor", "tt")
                P.wait("vector", "tt", P.mark("tt"))
                P.op("vector", "tensor_copy",
                     out=k_cm[:, ct * L + T0: ct * L + T0 + TC2],
                     in_=pk[:], inc=("tv", 1))
                kv_marks.append(P.mark("tv"))
            for tt in range(NT2):
                for half in range(2):
                    pv = (p_c, p_dd)[len(kv_marks) % 2]
                    if len(kv_marks) >= 2:
                        P.wait("tensor", "tv", kv_marks[-2])
                    for kt in range(8):
                        P.op("tensor", "matmul", pv[:],
                             h_cm[:, kt * L + T0 + tt * 128: kt * L + T0 + tt * 128 + 128],
                             wv_sb[:, kt * 1024 + half * 512: kt * 1024 + half * 512 + 512],
                             start=(kt == 0), stop=(kt == 7))
                    P.inc_last("tensor", "tt")
                    P.wait("vector", "tt", P.mark("tt"))
                    gt = sc * NT2 + tt
                    dstv = v_buf[:, gt * 1040 + half * 520: gt * 1040 + half * 520 + 520] \
                        .rearrange("p (h d) -> p h d", d=65)[:, :, 0:64]
                    P.op("vector", "tensor_copy", out=dstv,
                         in_=pv[:].rearrange("p (h d) -> p h d", d=64), inc=("tv", 1))
                    kv_marks.append(P.mark("tv"))
        for sc in range(4):
            T0 = sc * TC2
            den_row_marks = []
            for ct in range(8):
                P.wait("tensor", "tv", q_free[0])
                for kt in range(8):
                    P.op("tensor", "matmul", p_a[:],
                         wq_sb[:, kt * 1024 + ct * 128: kt * 1024 + ct * 128 + 128],
                         h_cm[:, kt * L + T0: kt * L + T0 + TC2],
                         start=(kt == 0), stop=(kt == 7))
                P.inc_last("tensor", "tt")
                P.wait("vector", "tt", P.mark("tt"))
                P.op("vector", "tensor_copy", out=q_sb[:], in_=p_a[:], inc=("tv", 1))
                q_ready = P.mark("tv")
                P.wait("tensor", "tv", q_ready)
                for hh in range(2):
                    h = ct * 2 + hh
                    pctx = (p_c, p_dd)[h % 2]
                    if len(ctx_free_marks) >= 2:
                        P.wait("tensor", "tv", ctx_free_marks[-2])
                    if len(den_row_marks) >= 2:
                        P.wait("tensor", "tsl", den_row_marks[-2])
                    elif den_prev[0] > 0 and len(den_row_marks) == 0:
                        P.wait("tensor", "tsl", den_prev[0])
                    for tt in range(16):
                        psc = (p_b, p_e)[tt % 2]
                        if len(sc_free_marks) >= 2:
                            P.wait("tensor", "tsl", sc_free_marks[-2])
                        P.op("tensor", "matmul", psc[:],
                             k_cm[hh * 64:hh * 64 + 64,
                                  ct * L + tt * 128: ct * L + tt * 128 + 128],
                             q_sb[hh * 64:hh * 64 + 64, 0:512],
                             start=True, stop=True, inc=("tt", 1))
                        P.wait("scalar", "tt", P.mark("tt"))
                        ppr = pr_sb[:, 0:512]
                        P.op("scalar", "activation", out=ppr, in_=psc[:],
                             func=ACT.Exp, bias=0.0, scale=0.125, inc=("tsl", 1))
                        sc_free_marks.append(P.mark("tsl"))
                        if dbg and bb == 0 and sc == 0 and h == 0 and tt == 0:
                            P.wait("vector", "tsl", P.mark("tsl"))
                            P.op("vector", "tensor_copy", out=ctx_sb[:, 3584:4096],
                                 in_=psc[:], inc=("tv", 1))
                            P.wait("gpsimd", "tv", P.mark("tv"))
                            P.wait("gpsimd", "tsl", P.mark("tsl"))
                            P.dma("gpsimd", "qdbg", out=dbg_d["d_pr"], in_=ppr)
                            P.dma("gpsimd", "qdbg", out=dbg_d["d_sc2"],
                                  in_=ctx_sb[:, 3584:4096])
                            P.dma("gpsimd", "qdbg", out=dbg_d["d_qs"], in_=q_sb[:])
                            P.dma("gpsimd", "qdbg", out=dbg_d["d_kc"],
                                  in_=k_cm[:, 0:512])
                            P.wait("scalar", "qdbg")
                            P.wait("tensor", "qdbg")
                            P.wait("vector", "qdbg")
                        P.wait("tensor", "tsl", P.mark("tsl"))
                        P.op("tensor", "matmul", pctx[0:65, :],
                             v_buf[:, tt * 1040 + h * 65: tt * 1040 + h * 65 + 65],
                             ppr, start=(tt == 0), stop=(tt == 15))
                    P.inc_last("tensor", "tt")
                    ctx_done = P.mark("tt")
                    P.wait("scalar", "tt", ctx_done)
                    if h % 4 == 0 and dgrp_flush[0] > 0:
                        P.wait("scalar", "qys", dgrp_flush[0])
                    P.op("scalar", "copy",
                         out=dcol[(h % 4) * 32:(h % 4) * 32 + 1, :],
                         in_=pctx[64:65, :], inc=("tsl", 1))
                    den_row_marks.append(P.mark("tsl"))
                    if h % 4 == 3:
                        P.wait("vector", "tsl", P.mark("tsl"))
                        P.op("vector", "reciprocal", out=dcol[:], in_=dcol[:])
                        P.inc_last("vector", "tv")
                        P.wait("gpsimd", "tv", P.mark("tv"))
                        if h == 3:
                            P.wait("gpsimd", "qyl")
                        for pp in range(4):
                            P.dma("gpsimd", "qys",
                                  out=bass.AP(den_d.tensor, (h - 3 + pp) * TC2,
                                              [[1, 1], [1, 512]]),
                                  in_=dcol[pp * 32:pp * 32 + 1, :])
                        dgrp_flush[0] = P.mark("qys")
                    P.wait("vector", "tt", ctx_done)
                    P.op("vector", "tensor_copy",
                         out=ctx_sb[:, ct * 512:ct * 512 + 512][hh * 64:hh * 64 + 64, :],
                         in_=pctx[0:64, :], inc=("tv", 1))
                    ctx_free_marks.append(P.mark("tv"))
            den_prev[0] = den_row_marks[-1]
            P.wait("sync", "qys")
            for ct in range(8):
                for hh in range(2):
                    h = ct * 2 + hh
                    P.dma("sync", "qyl",
                          out=den_b[hh * 64:hh * 64 + 64, :],
                          in_=bass.AP(den_d.tensor, h * TC2, [[0, 64], [1, 512]]))
                P.wait("vector", "qyl")
                P.op("vector", "tensor_mul", out=ctxn_sb[:, ct * 512:ct * 512 + 512],
                     in0=ctx_sb[:, ct * 512:ct * 512 + 512], in1=den_b[:],
                     inc=("tv", 1))
                if dbg and bb == 0 and sc == 0 and ct == 0:
                    P.wait("gpsimd", "tv", P.mark("tv"))
                    P.dma("gpsimd", "qdbg", out=dbg_d["d_ctx"], in_=ctx_sb[:, 0:512])
                    P.dma("gpsimd", "qdbg", out=dbg_d["d_den"], in_=den_b[:])
                    cv = ctxn_sb[:, 0:512]
                    P.dma("gpsimd", "qdbg", out=dbg_d["d_ctxn"], in_=cv)
                    P.wait("vector", "qdbg")
                    P.wait("sync", "qdbg")
                if ct < 7:
                    P.wait("sync", "tv", P.mark("tv"))
            ctxn_ready = P.mark("tv")
            P.wait("tensor", "tv", ctxn_ready)
            for co in range(8):
                pwo = (p_f, p_a)[co % 2]
                if len(wo_ev_marks) >= 2:
                    P.wait("tensor", "tv", wo_ev_marks[-2])
                for kt in range(8):
                    P.op("tensor", "matmul", pwo[:],
                         wo_sb[:, kt * 1024 + co * 128: kt * 1024 + co * 128 + 128],
                         ctxn_sb[:, kt * 512: kt * 512 + 512],
                         start=(kt == 0), stop=(kt == 7))
                P.inc_last("tensor", "tt")
                P.wait("vector", "tt", P.mark("tt"))
                P.op("vector", "tensor_copy", out=ctx_sb[:, co * 512:co * 512 + 512],
                     in_=pwo[:], inc=("tv", 1))
                wo_ev_marks.append(P.mark("tv"))
            ctxo_ready = P.mark("tv")
            q_free[0] = ctxo_ready
            P.wait("sync", a_x1_free[0][0], a_x1_free[0][1])
            P.dma("sync", "qx", out=x1c[:].rearrange("p (n c) -> p n c", c=1024),
                  in_=_tm(x1_d[bb], T0, NT2))
            P.wait("tensor", "tv", ctxo_ready)
            for tt2 in range(NT2):
                ctv = ctxo_t1[:, 0:1024]
                if tro_free[0] > 0:
                    P.wait("tensor", "tv", tro_free[0])
                for kt in range(8):
                    P.op("tensor", "transpose", p_trf[:],
                         ctx_sb[:, kt * 512 + tt2 * 128: kt * 512 + tt2 * 128 + 128],
                         ident_f[:], inc=("tt", 1))
                    P.wait("vector", "tt", P.mark("tt"))
                    P.op("vector", "tensor_copy", out=ctv[:, kt * 128:kt * 128 + 128],
                         in_=p_trf[:], inc=("tv", 1))
                    P.wait("tensor", "tv", P.mark("tv"))
                if tt2 == 0:
                    P.wait("vector", "qx")
                if x2_store_m[0] > 0:
                    P.wait("vector", "qx1", x2_store_m[0])
                x1v = x1c[:, tt2 * 1024:tt2 * 1024 + 1024]
                P.op("vector", "tensor_add", out=xsbuf[:], in0=x1v, in1=ctv,
                     inc=("tv", 1))
                P.wait("gpsimd", "tv", P.mark("tv"))
                P.dma("gpsimd", "qx1", out=_tm(x2_d[bb], T0 + tt2 * 128, 1),
                      in_=xsbuf[:].rearrange("p (n c) -> p n c", c=1024))
                x2s = P.mark("qx1")
                P.wait("vector", "qx1", x2s)
                P.op("vector", "scalar_tensor_tensor", out=xsbuf[:], in0=x1v,
                     scalar=2.0, in1=ctv, op0=ALU.mult, op1=ALU.add, inc=("tv", 1))
                tro_free[0] = P.mark("tv")
                P.wait("gpsimd", "tv", P.mark("tv"))
                P.dma("gpsimd", "qx1", out=_tm(ss_d[bb], T0 + tt2 * 128, 1),
                      in_=xsbuf[:].rearrange("p (n c) -> p n c", c=1024))
                x2_store_m[0] = P.mark("qx1")
            a_x1_free[0] = ("tv", P.mark("tv"))

    x2_done = P.mark("qx1")
    es3.close()

    if stop_after == "attn":
        P.wait("gpsimd", "qx1", x2_done)
        P.wait("gpsimd", "qdbg")
        return P, dr, es0

    TCF = 1024
    NTF = TCF // 128
    es4 = ExitStack()
    A4 = Alloc(nc, es4, "moe")
    wg_sb = A4.sb("wg_sb", [128, 8 * DFF], MMDT)
    wu_sb = A4.sb("wu_sb", [128, 8 * DFF], MMDT)
    wd_sb = A4.sb("wd_sb", [128, 16 * 1024], MMDT)
    h2_cm = A4.sb("h2_cm", [128, 8 * TCF], MMDT)
    act_sb = A4.sb("act_sb", [128, 16 * TCF], MMDT)
    sg_sb = A4.sb("sg_sb", [128, 2 * 512], MMDT)
    spec_sb = A4.sb("spec_sb", [128, 8 * TCF], F16)
    x2cf = A4.sb("x2cf", [128, 8 * 1024], F16)
    ce_all = A4.sb("ce_all", [128, 8 * 8], F16)
    ce_b = A4.sb("ce_b", [128, TCF], F16)
    rw_sb = A4.sb("rw_sb", [128, 8 * 8], MMDT)
    wldf = A4.sb("wldf", [128, 2 * DFF], F32)
    stats4 = A4.sb("stats4", [128, 16], F32)
    assert MMDT == F16
    h2tm = x2cf
    prb = A4.sb("prb", [128, 8], F32)
    prm = A4.sb("prm", [128, 8], F32)
    scf = A4.sb("scf", [128, 8], F32)
    moet1 = A4.sb("moet1", [128, 1024], F32)
    outc = A4.sb("outc", [128, 1024], F32)

    p_g0 = A4.ps("p_g0", [128, 512], F32)
    p_g1 = A4.ps("p_g1", [128, 512], F32)
    p_u0 = A4.ps("p_u0", [128, 512], F32)
    p_u1 = A4.ps("p_u1", [128, 512], F32)
    p_d0 = A4.ps("p_d0", [128, 512], F32)
    p_d1 = A4.ps("p_d1", [128, 512], F32)
    p_tr5 = A4.ps("p_tr5", [128, 128], MMDT)
    p_r = p_d0
    p_tr6 = A4.ps("p_tr6", [128, 128], F16)

    P.wait("sync", "qx1", x2_done)
    P.dma("sync", "qw", out=wldf[:, 0:64].rearrange("p (k e) -> p k e", e=NE),
          in_=rw_in.rearrange("(k p) e -> p k e", p=128), allow_slow_non_contiguous=True)
    P.wait("vector", "qw")
    for kt in range(8):
        P.op("vector", "tensor_scalar", out=rw_sb[:, kt * 8:kt * 8 + 8],
             in0=wldf[:, kt * 8:kt * 8 + 8], scalar1=n2w_sb[:, kt:kt + 1],
             scalar2=None, op0=ALU.mult)
    P.inc_last("vector", "tv")
    rw_ready = P.mark("tv")

    def stream_weight(wdr, wsb, nkt, ncols, fold_n2w, e_idx=None):
        for kt in range(nkt):
            par = len(wc_marks) % 2
            if len(wc_marks) >= 2:
                P.wait("sync", "tsl", wc_marks[-2])
            src2 = wdr if e_idx is None else wdr[e_idx]
            P.dma("sync", "qw", out=wldf[:, par * DFF:par * DFF + ncols],
                  in_=src2[kt * 128:kt * 128 + 128, :])
            P.wait("scalar", "qw")
            P.op("scalar", "activation", out=wsb[:, kt * ncols:kt * ncols + ncols],
                 in_=wldf[:, par * DFF:par * DFF + ncols], func=ACT.Copy, bias=0.0,
                 scale=(n2w_sb[:, kt:kt + 1] if fold_n2w else 1.0), inc=("tsl", 1))
            wc_marks.append(P.mark("tsl"))

    wc_marks = []
    for bb in range(BPC):
        for sc in range(2):
            T0 = sc * TCF
            P.wait("sync", f_x2_free[0][0], f_x2_free[0][1])
            P.dma("sync", "qx", out=x2cf[:].rearrange("p (n c) -> p n c", c=1024),
                  in_=_tm(x2_d[bb], T0, NTF))
            P.wait("scalar", "qx")
            for tt2 in range(NTF):
                P.op("scalar", "activation", out=act_sb[:, 0:1024],
                     in_=x2cf[:, tt2 * 1024:tt2 * 1024 + 1024], func=ACT.Square,
                     bias=0.0, scale=1.0, accum_out=stats4[:, tt2:tt2 + 1])
                P.op("scalar", "activation", out=stats4[:, 8 + tt2:9 + tt2],
                     in_=stats4[:, tt2:tt2 + 1], func=ACT.Sqrt, bias=eps6_sb[:],
                     scale=1.0 / 1024)
            P.inc_last("scalar", "tsl")
            P.wait("vector", "tsl", P.mark("tsl"))
            for tt2 in range(NTF):
                P.op("vector", "reciprocal", out=stats4[:, tt2:tt2 + 1],
                     in_=stats4[:, 8 + tt2:9 + tt2])
                P.op("vector", "tensor_scalar", out=h2tm[:, tt2 * 1024:tt2 * 1024 + 1024],
                     in0=x2cf[:, tt2 * 1024:tt2 * 1024 + 1024],
                     scalar1=stats4[:, tt2:tt2 + 1], scalar2=None, op0=ALU.mult)
            P.inc_last("vector", "tv")
            h2_ready = P.mark("tv")
            P.wait("tensor", "tv", h2_ready)
            trf_marks = []
            for idx in range(NTF * 8):
                tt2, kt = divmod(idx, 8)
                if len(trf_marks) >= 1:
                    P.wait("tensor", "tsl", trf_marks[-1])
                P.op("tensor", "transpose", p_tr5[:],
                     h2tm[:, tt2 * 1024 + kt * 128: tt2 * 1024 + kt * 128 + 128],
                     ident_b[:], inc=("tt", 1))
                P.wait("scalar", "tt", P.mark("tt"))
                P.op("scalar", "copy",
                     out=h2_cm[:, kt * TCF + tt2 * 128: kt * TCF + tt2 * 128 + 128],
                     in_=p_tr5[:], inc=("tsl", 1))
                trf_marks.append(P.mark("tsl"))
            h2cm_ready = P.mark("tsl")
            f_x2_free[0] = ("tsl", h2cm_ready)
            P.wait("tensor", "tsl", h2cm_ready)
            P.wait("tensor", "tv", rw_ready)
            for tt2 in range(NTF):
                if rt_free[0][1] > 0:
                    P.wait("tensor", rt_free[0][0], rt_free[0][1])
                    P.wait("tensor", rt_free[1][0], rt_free[1][1])
                for kt in range(8):
                    P.op("tensor", "matmul", p_r[:, 0:8],
                         h2_cm[:, kt * TCF + tt2 * 128: kt * TCF + tt2 * 128 + 128],
                         rw_sb[:, kt * 8:kt * 8 + 8],
                         start=(kt == 0), stop=(kt == 7))
                P.inc_last("tensor", "tt")
                P.wait("vector", "tt", P.mark("tt"))
                P.op("vector", "reduce_max", out=scf[:, 0:1], in_=p_r[:, 0:8], axis=AX,
                     negate=True)
                P.wait("scalar", "tv", P.mark("tv"))
                P.op("scalar", "activation", out=prb[:], in_=p_r[:, 0:8], func=ACT.Exp,
                     bias=scf[:, 0:1], scale=1.0, accum_out=scf[:, 1:2], inc=("tsl", 1))
                rt_free[0] = ("tv", P.mark("tv"))
                rt_free[1] = ("tsl", P.mark("tsl"))
                P.wait("vector", "tsl", P.mark("tsl"))
                P.op("vector", "reciprocal", out=scf[:, 2:3], in_=scf[:, 1:2])
                P.op("vector", "tensor_scalar", out=prb[:], in0=prb[:],
                     scalar1=scf[:, 2:3], scalar2=None, op0=ALU.mult)
                P.op("vector", "reduce_max", out=scf[:, 3:4], in_=prb[:], axis=AX)
                P.op("vector", "tensor_scalar", out=prm[:], in0=prb[:],
                     scalar1=scf[:, 3:4], scalar2=None, op0=ALU.is_ge)
                P.op("vector", "scalar_tensor_tensor", out=prm[:], in0=prm[:],
                     scalar=-1e30, in1=prb[:], op0=ALU.mult, op1=ALU.add)
                P.op("vector", "reduce_max", out=scf[:, 4:5], in_=prm[:], axis=AX)
                P.inc_last("vector", "tv")
                P.wait("scalar", "tv", P.mark("tv"))
                P.op("scalar", "activation", out=scf[:, 5:6], in_=scf[:, 3:4],
                     func=ACT.Exp, bias=0.0, scale=1.0)
                P.op("scalar", "activation", out=scf[:, 6:7], in_=scf[:, 4:5],
                     func=ACT.Exp, bias=0.0, scale=1.0, inc=("tsl", 1))
                P.wait("vector", "tsl", P.mark("tsl"))
                P.op("vector", "tensor_add", out=scf[:, 7:8], in0=scf[:, 5:6],
                     in1=scf[:, 6:7])
                P.op("vector", "reciprocal", out=scf[:, 7:8], in_=scf[:, 7:8])
                P.op("vector", "tensor_mul", out=scf[:, 5:6], in0=scf[:, 5:6],
                     in1=scf[:, 7:8])
                P.op("vector", "tensor_mul", out=scf[:, 6:7], in0=scf[:, 6:7],
                     in1=scf[:, 7:8])
                P.op("vector", "tensor_scalar", out=ce_all[:, tt2 * 8:tt2 * 8 + 8],
                     in0=prb[:], scalar1=scf[:, 3:4], scalar2=None, op0=ALU.is_ge)
                P.op("vector", "tensor_scalar", out=ce_all[:, tt2 * 8:tt2 * 8 + 8],
                     in0=ce_all[:, tt2 * 8:tt2 * 8 + 8], scalar1=scf[:, 5:6],
                     scalar2=None, op0=ALU.mult)
                P.op("vector", "tensor_scalar", out=prm[:], in0=prm[:],
                     scalar1=scf[:, 4:5], scalar2=None, op0=ALU.is_ge)
                P.op("vector", "tensor_scalar", out=prm[:], in0=prm[:],
                     scalar1=scf[:, 6:7], scalar2=None, op0=ALU.mult)
                P.op("vector", "tensor_add", out=ce_all[:, tt2 * 8:tt2 * 8 + 8],
                     in0=ce_all[:, tt2 * 8:tt2 * 8 + 8], in1=prm[:], inc=("tv", 1))
                if dbg and bb == 1 and sc == 1 and tt2 == 7:
                    P.wait("gpsimd", "tv", P.mark("tv"))
                    P.dma("gpsimd", "qdbg", out=bass.AP(dbg_d["d_rt"].tensor, 0,
                                                        [[64, 128], [1, 8]]),
                          in_=prb[:])
                    P.dma("gpsimd", "qdbg", out=bass.AP(dbg_d["d_rt"].tensor, 8,
                                                        [[64, 128], [1, 8]]),
                          in_=prm[:])
                    P.dma("gpsimd", "qdbg", out=bass.AP(dbg_d["d_rt"].tensor, 16,
                                                        [[64, 128], [1, 8]]),
                          in_=scf[:, 0:8])
                    P.dma("gpsimd", "qdbg", out=bass.AP(dbg_d["d_rt"].tensor, 24,
                                                        [[64, 128], [1, 8]]),
                          in_=ce_all[:, tt2 * 8:tt2 * 8 + 8])
                    P.wait("vector", "qdbg")
                    P.wait("scalar", "qdbg")
            P.wait("gpsimd", "tv", P.mark("tv"))
            P.wait("gpsimd", "qyl")
            for e in range(NE):
                P.dma("gpsimd", "qys",
                      out=bass.AP(ce_d.tensor, e * 2048 + T0, [[1, 128], [128, NTF]]),
                      in_=ce_all[:].rearrange("p (t e) -> p t e", e=NE)[:, :, e],
                      allow_slow_non_contiguous=True)
            P.wait("sync", "qys")
            P.wait("vector", "qo", out_store_m[0])
            P.op("vector", "memset", spec_sb[:], 0.0, inc=("tv", 1))
            for e in range(NE + 1):
                shared = (e == NE)
                P.wait("sync", "tt", wgu_free[0])
                if shared:
                    stream_weight(shg_in, wg_sb, 8, DFF, True)
                    stream_weight(shu_in, wu_sb, 8, DFF, True)
                else:
                    stream_weight(eg_in, wg_sb, 8, DFF, True, e)
                    stream_weight(eu_in, wu_sb, 8, DFF, True, e)
                gu_ready = P.mark("tsl")
                if not shared:
                    P.wait("sync", "tv", ceb_free[0])
                    P.dma("sync", "qyl", out=ce_b[:],
                          in_=bass.AP(ce_d.tensor, e * 2048 + T0, [[0, 128], [1, TCF]]))
                P.wait("tensor", "tsl", gu_ready)
                for ft in range(16):
                    for tn in range(TCF // 512):
                        sl = (ft * 2 + tn) % 2
                        pg = (p_g0, p_g1)[sl]
                        pu = (p_u0, p_u1)[sl]
                        if len(act_marks) >= 2:
                            P.wait("tensor", "tv", act_marks[-2])
                        for kt in range(8):
                            P.op("tensor", "matmul", pg[:],
                                 wg_sb[:, kt * DFF + ft * 128: kt * DFF + ft * 128 + 128],
                                 h2_cm[:, kt * TCF + tn * 512: kt * TCF + tn * 512 + 512],
                                 start=(kt == 0), stop=(kt == 7))
                        for kt in range(8):
                            P.op("tensor", "matmul", pu[:],
                                 wu_sb[:, kt * DFF + ft * 128: kt * DFF + ft * 128 + 128],
                                 h2_cm[:, kt * TCF + tn * 512: kt * TCF + tn * 512 + 512],
                                 start=(kt == 0), stop=(kt == 7))
                        P.inc_last("tensor", "tt")
                        P.wait("scalar", "tt", P.mark("tt"))
                        sgv = sg_sb[:, tn * 512:tn * 512 + 512]
                        P.op("scalar", "activation", out=sgv, in_=pg[:], func=ACT.Silu,
                             bias=0.0, scale=1.0, inc=("tsl", 1))
                        P.wait("vector", "tsl", P.mark("tsl"))
                        dst = act_sb[:, ft * TCF + tn * 512: ft * TCF + tn * 512 + 512]
                        P.op("vector", "tensor_mul", out=dst, in0=sgv, in1=pu[:])
                        if not shared:
                            P.wait("vector", "qyl")
                            P.op("vector", "tensor_mul", out=dst, in0=dst,
                                 in1=ce_b[:, tn * 512:tn * 512 + 512])
                        P.inc_last("vector", "tv")
                        act_marks.append(P.mark("tv"))
                act_ready = P.mark("tv")
                ceb_free[0] = act_ready
                wgu_free[0] = P.mark("tt")
                P.wait("sync", "tt", wd_free[0])
                stream_weight(shd_in if shared else ed_in, wd_sb, 16, 1024, False,
                              None if shared else e)
                wd_ready = P.mark("tsl")
                P.wait("tensor", "tsl", wd_ready)
                P.wait("tensor", "tv", act_ready)
                for co in range(8):
                    for tn in range(TCF // 512):
                        pd = (p_d0, p_d1)[(co * 2 + tn) % 2]
                        if len(dn_marks) >= 2:
                            P.wait("tensor", "tv", dn_marks[-2])
                        for ft in range(16):
                            P.op("tensor", "matmul", pd[:],
                                 wd_sb[:, ft * 1024 + co * 128: ft * 1024 + co * 128 + 128],
                                 act_sb[:, ft * TCF + tn * 512: ft * TCF + tn * 512 + 512],
                                 start=(ft == 0), stop=(ft == 15))
                        P.inc_last("tensor", "tt")
                        P.wait("vector", "tt", P.mark("tt"))
                        dsts = spec_sb[:, co * TCF + tn * 512: co * TCF + tn * 512 + 512]
                        P.op("vector", "tensor_add", out=dsts, in0=dsts, in1=pd[:],
                             inc=("tv", 1))
                        dn_marks.append(P.mark("tv"))
                wd_free[0] = P.mark("tt")
            spec_ready = P.mark("tv")
            P.wait("sync", f_x2_free[0][0], f_x2_free[0][1])
            P.dma("sync", "qx", out=x2cf[:].rearrange("p (n c) -> p n c", c=1024),
                  in_=_tm(ss_d[bb], T0, NTF))
            P.wait("tensor", "tv", spec_ready)
            for tt2 in range(NTF):
                if moet_free[0] > 0:
                    P.wait("tensor", "tv", moet_free[0])
                for kt in range(8):
                    P.op("tensor", "transpose", p_tr6[:],
                         spec_sb[:, kt * TCF + tt2 * 128: kt * TCF + tt2 * 128 + 128],
                         ident_h[:], inc=("tt", 1))
                    P.wait("vector", "tt", P.mark("tt"))
                    P.op("vector", "tensor_copy", out=moet1[:, kt * 128:kt * 128 + 128],
                         in_=p_tr6[:], inc=("tv", 1))
                    P.wait("tensor", "tv", P.mark("tv"))
                if tt2 == 0:
                    P.wait("vector", "qx")
                P.op("vector", "tensor_add", out=moet1[:], in0=moet1[:],
                     in1=x2cf[:, tt2 * 1024:tt2 * 1024 + 1024], inc=("tv", 1))
                moet_free[0] = P.mark("tv")
                f_x2_free[0] = ("tv", P.mark("tv"))
                P.wait("scalar", "tv", P.mark("tv"))
                P.op("scalar", "activation", out=act_sb[:, 0:1024], in_=moet1[:],
                     func=ACT.Copy, bias=0.0, scale=1.0, accum_out=stats4[:, 0:1],
                     inc=("tsl", 1))
                P.wait("vector", "tsl", P.mark("tsl"))
                P.op("vector", "tensor_scalar", out=stats4[:, 1:2], in0=stats4[:, 0:1],
                     scalar1=1.0 / 1024, scalar2=None, op0=ALU.mult)
                P.op("vector", "tensor_scalar", out=moet1[:], in0=moet1[:],
                     scalar1=stats4[:, 1:2], scalar2=None, op0=ALU.subtract,
                     inc=("tv", 1))
                P.wait("scalar", "tv", P.mark("tv"))
                P.op("scalar", "activation", out=act_sb[:, 0:1024], in_=moet1[:],
                     func=ACT.Square, bias=0.0, scale=1.0, accum_out=stats4[:, 2:3])
                P.op("scalar", "activation", out=stats4[:, 3:4], in_=stats4[:, 2:3],
                     func=ACT.Sqrt, bias=eps5_sb[:], scale=1.0 / 1024, inc=("tsl", 1))
                P.wait("vector", "tsl", P.mark("tsl"))
                P.op("vector", "reciprocal", out=stats4[:, 4:5], in_=stats4[:, 3:4])
                ov = outc[:, 0:1024]
                if len(out_t_marks) >= 1:
                    P.wait("vector", "qo", out_t_marks[-1])
                P.op("vector", "tensor_scalar", out=ov, in0=moet1[:],
                     scalar1=stats4[:, 4:5], scalar2=None, op0=ALU.mult,
                     inc=("tv", 1))
                P.wait("gpsimd", "tv", P.mark("tv"))
                P.dma("gpsimd", "qo", out=_tm(out_d[bb], T0 + tt2 * 128, 1),
                      in_=ov.rearrange("p (n c) -> p n c", c=1024))
                out_t_marks.append(P.mark("qo"))
            out_store_m[0] = P.mark("qo")

    es4.close()
    P.wait("gpsimd", "qo", P.mark("qo"))
    P.wait("gpsimd", "qdbg")
    return P, dr, es0


def host_inputs(inputs):
    x = np.ascontiguousarray(inputs["L0_attn_x"])
    maps = []
    w_in_t = np.ascontiguousarray(inputs["L0_attn_W_in"].T)
    w_x_t = np.ascontiguousarray(inputs["L0_attn_W_x"].T)
    w_dt_t = np.ascontiguousarray(inputs["L0_attn_W_dt"].T)
    w_out_t = np.ascontiguousarray(inputs["L0_attn_W_out"].T)
    convw = np.ascontiguousarray(inputs["L0_attn_conv_w"][:, 0, :])
    ident = np.eye(128, dtype=np.float32)
    common = dict(
        w_in_t=w_in_t, w_x_t=w_x_t, w_dt_t=w_dt_t, w_out_t=w_out_t,
        gamma=inputs["L0_attn_gamma"], step=inputs["L0_attn_step_size"],
        convw=convw, convb=inputs["L0_attn_conv_b"], bdt=inputs["L0_attn_b_dt"],
        ident=ident,
        n1w=inputs["L0_ffn_norm1_w"], n2w=inputs["L0_ffn_norm2_w"],
        wq=inputs["L0_ffn_wq"], wk=inputs["L0_ffn_wk"], wv=inputs["L0_ffn_wv"],
        wo=inputs["L0_ffn_wo"], rw=inputs["L0_ffn_router_w"],
        shg=inputs["L0_ffn_sh_gate"], shu=inputs["L0_ffn_sh_up"],
        shd=inputs["L0_ffn_sh_down"], eg=inputs["L0_ffn_e_gate"],
        eu=inputs["L0_ffn_e_up"], ed=inputs["L0_ffn_e_down"],
    )
    for c in range(NCORES):
        m = dict(common)
        m["x"] = np.ascontiguousarray(x[c * BPC:(c + 1) * BPC])
        maps.append(m)
    return maps


def build_nc(dbg=False, stop_after=None):
    nc = bass.Bass("TRN2", target_bir_lowering=False, detect_race_conditions=False)
    P, dr, es0 = build_core(nc, dbg=dbg, stop_after=stop_after)
    with nc.Block() as block:
        P.build(block)
    print("instr counts:", P.nops())
    return nc


def kernel(**inputs):
    nc = build_nc(dbg=False)
    maps = host_inputs(inputs)
    res = run_bass_kernel_spmd(nc, maps, core_ids=list(range(NCORES)))
    return np.concatenate([np.asarray(r["out"]) for r in res.results], axis=0)


# version 03
# speedup vs baseline: 1.0001x; 1.0001x over previous
import numpy as np
import ml_dtypes
import concourse.bass as bass
import concourse.mybir as mybir
from concourse.bass_utils import run_bass_kernel_spmd
from contextlib import ExitStack

F32 = mybir.dt.float32
F16 = mybir.dt.float16
BF16 = mybir.dt.bfloat16
ALU = mybir.AluOpType
ACT = mybir.ActivationFunctionType
AX = mybir.AxisListType.X

NCORES = 8
DBG_CHUNK = -1
PARANOID = False
B, L, D = 16, 2048, 1024
BPC = B // NCORES
DI = 2048
DTR = 64
NH, DH = 16, 64
NE, DFF = 8, 2048
TC1 = 256
TC2 = 512
MMDT = F16

ENGINES = ("sync", "vector", "scalar", "tensor", "gpsimd")
DTB = {F32: 4, F16: 2, BF16: 2, mybir.dt.int32: 4}


class Prog:
    def __init__(self, nc, es):
        self.nc = nc
        self.es = es
        self.items = {e: [] for e in ENGINES}
        self.cnt = {}
        self.sems = {}

    def sem(self, name):
        if name not in self.sems:
            self.sems[name] = self.es.enter_context(self.nc.semaphore(name))
            self.cnt[name] = 0
        return name

    HAZARD_OPS = ("reduce_sum", "reduce_max", "tensor_reduce",
                  "tensor_tensor_scan", "reciprocal", "bn_stats", "bn_aggr")
    ENG_SEM = {"vector": "tv", "gpsimd": "tg", "scalar": "tsl"}

    def _small_out(self, meth, a, kw):
        if kw.get("accum_out") is not None:
            return True
        obj = kw.get("out")
        if obj is None and a:
            obj = a[0]
        shp = getattr(obj, "shape", None)
        if not shp or len(shp) < 2:
            return False
        free = 1
        for d in list(shp)[1:]:
            free *= int(d)
        return free < 64

    def op(self, eng, meth, *a, inc=None, **kw):
        hazard = eng in ("vector", "gpsimd", "scalar") and (
            meth in self.HAZARD_OPS or self._small_out(meth, a, kw))
        if hazard and inc is None:
            inc = (self.ENG_SEM[eng], 1)
        if inc is not None:
            self.cnt[inc[0]] += inc[1]
            semname = inc[0]
            inc = (self.sems[inc[0]], inc[1])
        self.items[eng].append((meth, a, kw, inc))
        if hazard:
            self.items[eng].append(("wait_ge", (inc[0], self.cnt[semname]), {}, None))

    def inc_last(self, eng, semname, n=1):
        idx = len(self.items[eng]) - 1
        while idx >= 0 and self.items[eng][idx][0] == "wait_ge":
            idx -= 1
        meth, a, kw, inc = self.items[eng][idx]
        if inc is not None:
            assert inc[0] is self.sems[semname], f"{meth} inc on other sem"
            return
        self.cnt[semname] += n
        self.items[eng][idx] = (meth, a, kw, (self.sems[semname], n))

    def mark(self, semname):
        return self.cnt[semname]

    def ensure_inc(self, eng, semname):
        self.inc_last(eng, semname)

    def wait(self, eng, semname, val=None):
        v = self.cnt[semname] if val is None else val
        if v > 0:
            self.items[eng].append(("wait_ge", (self.sems[semname], v), {}, None))

    def dma(self, eng, qsem, out, in_, **kw):
        self.op(eng, "dma_start", inc=(qsem, 16), out=out, in_=in_, **kw)

    def build(self, block):
        for eng in ENGINES:
            if not self.items[eng]:
                continue
            items = self.items[eng]

            def body(h, items=items):
                for meth, a, kw, inc in items:
                    r = getattr(h, meth)(*a, **kw)
                    if inc is not None:
                        r.then_inc(inc[0], inc[1])
            getattr(block, eng)(body)

    def nops(self):
        return {e: len(v) for e, v in self.items.items()}

    def fence(self, engines=ENGINES, sems=("tv", "tt", "tsl", "qs", "qg")):
        for eng in engines:
            if not self.items[eng]:
                continue
            meth, a, kw, inc = self.items[eng][-1]
            if inc is None and meth != "wait_ge":
                sm = {"sync": "qs", "gpsimd": "tg", "vector": "tv",
                      "scalar": "tsl", "tensor": "tt"}[eng]
                if meth == "dma_start":
                    sm = "qs" if eng == "sync" else "qg"
                    self.inc_last(eng, sm, 16)
                else:
                    self.inc_last(eng, sm)
        vals = {s: self.cnt[s] for s in list(sems) + ["tg"]}
        for eng in engines:
            for s, v in vals.items():
                if v > 0:
                    self.items[eng].append(("wait_ge", (self.sems[s], v), {}, None))


class Alloc:
    def __init__(self, nc, es, label=""):
        self.nc = nc
        self.es = es
        self.used = 0
        self.label = label

    def sb(self, name, shape, dtype):
        t = self.es.enter_context(self.nc.sbuf_tensor(name, shape, dtype))
        self.used += int(np.prod(shape[1:])) * DTB[dtype]
        return t

    def ps(self, name, shape, dtype=F32):
        return self.es.enter_context(self.nc.psum_tensor(name, shape, dtype))


def _tm(dram2d, t0, nt, w=D):
    return dram2d[t0:t0 + nt * 128, :].rearrange("(n p) c -> p n c", p=128)


def build_core(nc, dbg=False, stop_after=None):
    dr = {}

    def din(name, shape, dtype=F32):
        dr[name] = nc.dram_tensor(name, list(shape), dtype, kind="ExternalInput").ap()
        return dr[name]

    x_in = din("x", (BPC, L, D))
    w_in_t = din("w_in_t", (D, 2 * DI))
    w_x_t = din("w_x_t", (DI, 96))
    w_dt_t = din("w_dt_t", (DTR, DI))
    w_out_t = din("w_out_t", (DI, D))
    gamma_in = din("gamma", (D,))
    step_in = din("step", (1,))
    convw_in = din("convw", (DI, 4))
    convb_in = din("convb", (DI,))
    bdt_in = din("bdt", (DI,))
    ident_in = din("ident", (128, 128))
    n1w_in = din("n1w", (D,))
    n2w_in = din("n2w", (D,))
    wq_in = din("wq", (D, D)); wk_in = din("wk", (D, D))
    wv_in = din("wv", (D, D)); wo_in = din("wo", (D, D))
    rw_in = din("rw", (D, NE))
    shg_in = din("shg", (D, DFF)); shu_in = din("shu", (D, DFF))
    shd_in = din("shd", (DFF, D))
    eg_in = din("eg", (NE, D, DFF)); eu_in = din("eu", (NE, D, DFF))
    ed_in = din("ed", (NE, DFF, D))

    out_d = nc.dram_tensor("out", [BPC, L, D], F32, kind="ExternalOutput").ap()
    sk = "ExternalOutput" if dbg else "Internal"
    ys_d = nc.dram_tensor("ys", [BPC, DI, L], MMDT, kind=sk).ap()
    dbg_d = {}
    if dbg:
        for nm, sh, dt_ in [("d_xn", [128, 2048], MMDT), ("d_xc", [128, 4096], F32),
                            ("d_z", [128, 4096], F32), ("d_cv", [128, 4096], F32),
                            ("d_u", [128, 4096], MMDT), ("d_spz", [128, 4096], MMDT),
                            ("d_dec", [128, 4096], F32), ("d_h", [128, 4096], F32),
                            ("d_spy", [128, 4096], F32), ("d_spr", [128, 4096], F32),
                            ("d_sq", [128, 4096], F32), ("d_sqin", [128, 4096], F32),
                            ("d_wq", [128, 512], MMDT), ("d_smc", [128, 16], F32),
                            ("d_rt", [128, 64], F32), ("d_h2", [128, 128], MMDT),
                            ("d_ctx", [128, 512], F32), ("d_den", [128, 512], F32),
                            ("d_pr", [128, 512], F32), ("d_ctxn", [128, 512], F32),
                            ("d_sc2", [128, 512], F32), ("d_qs", [128, 512], MMDT),
                            ("d_kc", [128, 512], MMDT),
                            ("d_xncm", [128, 2048], MMDT)]:
            dbg_d[nm] = nc.dram_tensor(nm, sh, dt_, kind="ExternalOutput").ap()
    x1_d = nc.dram_tensor("x1s", [BPC, L, D], F16, kind=sk).ap()
    x2_d = nc.dram_tensor("x2s", [BPC, L, D], F16, kind=sk).ap()
    ss_d = nc.dram_tensor("sss", [BPC, L, D], F16, kind=sk).ap()
    den_d = nc.dram_tensor("dens", [NH, TC2], F32, kind="Internal").ap()
    ce_d = nc.dram_tensor("ces", [NE, 2048], F16, kind="Internal").ap()

    es0 = ExitStack()
    nc._mk_es0 = es0
    P = Prog(nc, es0)
    A0 = Alloc(nc, es0)

    for s in ("qs", "qg", "tv", "tt", "tsl", "tg", "qx", "qw", "qyl",
              "qys", "qx1", "qo", "qdbg"):
        P.sem(s)

    ident_b = A0.sb("ident_b", [128, 128], MMDT)
    ident_f = A0.sb("ident_f", [128, 128], F32)
    gamma_sb = A0.sb("gamma_sb", [128, 8], F32)
    n1w_sb = A0.sb("n1w_sb", [128, 8], F32)
    n2w_sb = A0.sb("n2w_sb", [128, 8], F32)
    convw_sb = A0.sb("convw_sb", [128, 64], F32)
    convb_sb = A0.sb("convb_sb", [128, 16], F32)
    bdt_sb = A0.sb("bdt_sb", [128, 16], F32)
    step_sb = A0.sb("step_sb", [1, 1], F32)
    ones_sb = A0.sb("ones_sb", [128, 1], F32)
    ones_mat = A0.sb("ones_mat", [128, 128], MMDT)
    stepv = A0.sb("stepv", [128, 1], F32)
    stepv16 = A0.sb("stepv16", [128, 1], MMDT)
    acc16 = A0.sb("acc16", [128, 1], MMDT)
    ones_b = A0.sb("ones_b", [128, 1], MMDT)
    flag_sb = A0.sb("flag_sb", [128, 1], F32)
    eps5_sb = A0.sb("eps5_sb", [128, 1], F32)
    eps6_sb = A0.sb("eps6_sb", [128, 1], F32)
    smc = A0.sb("smc", [128, 16], F32)
    ident_h = A0.sb("ident_h", [128, 128], F16)

    P.dma("sync", "qs", out=ident_f[:], in_=ident_in)
    P.dma("sync", "qs", out=gamma_sb[:], in_=gamma_in.rearrange("(k p) -> p k", p=128), allow_slow_non_contiguous=True)
    P.dma("sync", "qs", out=n1w_sb[:], in_=n1w_in.rearrange("(k p) -> p k", p=128), allow_slow_non_contiguous=True)
    P.dma("sync", "qs", out=n2w_sb[:], in_=n2w_in.rearrange("(k p) -> p k", p=128), allow_slow_non_contiguous=True)
    P.dma("sync", "qs", out=convw_sb[:].rearrange("p (c k) -> p c k", k=4),
          in_=convw_in.rearrange("(c p) k -> p c k", p=128), allow_slow_non_contiguous=True)
    P.dma("sync", "qs", out=convb_sb[:], in_=convb_in.rearrange("(c p) -> p c", p=128), allow_slow_non_contiguous=True)
    P.dma("sync", "qs", out=bdt_sb[:], in_=bdt_in.rearrange("(c p) -> p c", p=128), allow_slow_non_contiguous=True)
    P.dma("sync", "qs", out=step_sb[:, 0:1], in_=step_in.rearrange("(a o) -> a o", a=1))
    ld0 = P.mark("qs")

    P.wait("vector", "qs", ld0)
    P.op("vector", "memset", ones_sb[:], 1.0)
    P.op("vector", "memset", ones_mat[:], 1.0)
    P.op("vector", "memset", smc[:], 0.0)
    P.op("vector", "memset", ones_b[:], 1.0)
    P.op("vector", "memset", eps5_sb[:], 1e-5)
    P.op("vector", "memset", eps6_sb[:], 1e-6)
    P.op("vector", "tensor_copy", out=ident_b[:], in_=ident_f[:])
    P.op("vector", "tensor_copy", out=ident_h[:], in_=ident_f[:])
    P.op("vector", "tensor_scalar", out=flag_sb[:], in0=ones_sb[:],
         scalar1=2.0 ** -30, scalar2=1.0, op0=ALU.add, op1=ALU.subtract)
    P.op("vector", "tensor_scalar", out=flag_sb[:], in0=flag_sb[:],
         scalar1=0.0, scalar2=None, op0=ALU.is_gt, inc=("tv", 1))
    P.op("vector", "memset", stepv[:], 0.0)
    P.op("vector", "tensor_copy", out=stepv[0:1, 0:1], in_=step_sb[:, 0:1])
    P.op("vector", "tensor_copy", out=stepv16[:], in_=stepv[:], inc=("tv", 1))

    def quantize(w_dram, K, M, wq_sb, fold_const, fold_gamma, wld, wmask, ps_sm, scol):
        nkt = (K + 127) // 128
        kp = min(K, 128)
        acc = smc[:, scol:scol + 1]
        red = smc[:, scol + 1:scol + 2]
        for kt in range(nkt):
            P.wait("sync", "tv", P.mark("tv"))
            P.dma("sync", "qw", out=wld[:kp, :M], in_=w_dram[kt * 128:kt * 128 + kp, :])
            P.wait("vector", "qw")
            P.op("vector", "reduce_sum", out=red[:kp, :], in_=wld[:kp, :M], axis=AX,
                 apply_absolute_value=True)
            if kt == 0:
                P.op("vector", "memset", acc[:], 0.0)
            P.op("vector", "tensor_add", out=acc[:kp, :], in0=acc[:kp, :],
                 in1=red[:kp, :], inc=("tv", 1))
        P.op("vector", "tensor_copy", out=acc16[:], in_=acc[:], inc=("tv", 1))
        P.wait("tensor", "tv", P.mark("tv"))
        P.op("tensor", "matmul", ps_sm[:, 0:1], ones_mat[:], acc16[:],
             start=True, stop=True, inc=("tt", 1))
        P.wait("vector", "tt", P.mark("tt"))
        thr = smc[:, scol + 2:scol + 3]
        fold = smc[:, scol + 3:scol + 4]
        nthr = smc[:, scol + 4:scol + 5]
        P.op("vector", "tensor_scalar", out=thr, in0=ps_sm[:, 0:1],
             scalar1=1.0 / (K * M), scalar2=1e-8, op0=ALU.mult, op1=ALU.add)
        P.op("vector", "tensor_scalar", out=fold, in0=thr,
             scalar1=fold_const, scalar2=None, op0=ALU.mult)
        if fold_gamma:
            P.op("vector", "tensor_scalar", out=fold, in0=fold,
                 scalar1=smc[:, 15:16], scalar2=None, op0=ALU.mult)
        P.op("vector", "tensor_scalar", out=thr, in0=thr,
             scalar1=0.5, scalar2=None, op0=ALU.mult)
        P.op("vector", "tensor_scalar", out=nthr, in0=thr, scalar1=-1.0, scalar2=None,
             op0=ALU.mult, inc=("tv", 1))
        for kt in range(nkt):
            P.wait("sync", "tv", P.mark("tv"))
            P.dma("sync", "qw", out=wld[:kp, :M], in_=w_dram[kt * 128:kt * 128 + kp, :])
            P.wait("vector", "qw")
            P.op("vector", "tensor_scalar", out=wmask[:kp, :M], in0=wld[:kp, :M],
                 scalar1=thr[:kp, :], scalar2=None, op0=ALU.is_gt)
            P.op("vector", "scalar_tensor_tensor", out=wmask[:kp, :M],
                 in0=wld[:kp, :M], scalar=nthr[:kp, :], in1=wmask[:kp, :M],
                 op0=ALU.is_lt, op1=ALU.subtract)
            if fold_gamma:
                gsc = smc[:, scol + 5:scol + 6]
                P.op("vector", "tensor_scalar", out=gsc, in0=gamma_sb[:, kt:kt + 1],
                     scalar1=fold, scalar2=-1.0, op0=ALU.mult, op1=ALU.mult)
                P.op("vector", "tensor_scalar", out=wq_sb[:kp, kt * M:kt * M + M],
                     in0=wmask[:kp, :M], scalar1=gsc[:kp, :], scalar2=None,
                     op0=ALU.mult, inc=("tv", 1))
            else:
                P.op("vector", "tensor_scalar", out=wq_sb[:kp, kt * M:kt * M + M],
                     in0=wmask[:kp, :M], scalar1=fold[:kp, :], scalar2=-1.0,
                     op0=ALU.mult, op1=ALU.mult, inc=("tv", 1))

    es1 = ExitStack()
    A1 = Alloc(nc, es1, "m")
    wqin = A1.sb("wqin", [128, 8 * 4096], MMDT)
    wqx = A1.sb("wqx", [128, 16 * 96], MMDT)
    wqdt = A1.sb("wqdt", [128, DI], MMDT)
    x0b = A1.sb("x0b", [128, 2 * 2048], F32)
    y32 = A1.sb("y32", [128, 16 * TC1], F32)
    r16 = A1.sb("r16", [128, 16 * TC1], F16)
    t16 = A1.sb("t16", [128, 16 * TC1], F16)
    stats = A1.sb("stats", [128, 16], F32)
    carry = A1.sb("carry", [128, 16], F32)
    dump = A1.sb("dump", [128, 1024], F32)
    ps_sm = A1.ps("ps_sm", [128, 16], F32)

    es1a = ExitStack()
    A1a = Alloc(nc, es1a, "m1only")
    xc0 = A1a.sb("xc0", [128, 2048], F32)
    xn_tm = A1a.sb("xn_tm", [128, 2048], MMDT)
    xn_cm = A1a.sb("xn_cm", [128, 8 * TC1], MMDT)
    xc_sb = A1a.sb("xc_sb", [128, 16 * TC1], F32)
    z_sb = A1a.sb("z_sb", [128, 16 * TC1], F32)
    spz = A1a.sb("spz", [128, 16 * TC1], MMDT)
    u16 = A1a.sb("u16", [128, 16 * TC1], MMDT)
    b16 = A1a.sb("b16", [128, 16 * TC1], F16)
    dbl_sb = A1a.sb("dbl_sb", [128, TC1], MMDT)
    hal = A1a.sb("hal", [128, 48], F32)
    ps_xz = [A1a.ps(f"ps_xz{i}", [128, TC1], F32) for i in range(2)]
    ps_tr = [A1a.ps(f"ps_tr{i}", [128, 128], MMDT) for i in range(2)]
    ps_dt = [A1a.ps(f"ps_dt{i}", [128, TC1], F32) for i in range(2)]
    dt_sb = xc_sb
    h_sb = y32

    P.wait("tensor", "tv", P.mark("tv"))
    P.op("tensor", "matmul", ps_sm[:, 0:1], ones_mat[:], stepv16[:],
         start=True, stop=True, inc=("tt", 1))
    P.wait("vector", "tt", P.mark("tt"))
    P.op("vector", "tensor_copy", out=smc[:, 15:16], in_=ps_sm[:, 0:1],
         inc=("tv", 1))

    wcast_marks = []
    a_x1_free = [("tv", 0)]
    sc_free_marks = []
    ctx_free_marks = []
    wo_ev_marks = []
    x2_store_m = [0]
    x2_marks = []
    tro_marks = []
    ctv_ev = []
    q_free = [0]
    den_prev = [0]
    dgrp_flush = [0]
    tro_free = [0]
    f_x2_free = [("tv", 0)]
    wgu_free = [0]
    wd_free = [0]
    ceb_free = [0]
    act_marks = []
    dn_marks = []
    out_store_m = [0]
    out_t_marks = []
    moet_free = [0]
    rt_free = [("tv", 0), ("tsl", 0)]
    quantize(w_in_t, D, 2 * DI, wqin, 1.0, True, y32[:, 0:4096], r16[:, 0:4096],
             ps_sm, 0)
    if dbg:
        P.ensure_inc("vector", "tv")
        P.wait("gpsimd", "tv", P.mark("tv"))
        P.dma("gpsimd", "qdbg", out=dbg_d["d_wq"], in_=wqin[:, 0:512])
        P.dma("gpsimd", "qdbg", out=dbg_d["d_smc"], in_=smc[:])
        P.wait("vector", "qdbg")
    quantize(w_x_t, DI, 96, wqx, 0.5, False, y32[:, 0:96], r16[:, 0:96], ps_sm, 6)
    quantize(w_dt_t, DTR, DI, wqdt, 1.0, False, y32[:, 0:DI], r16[:, 0:DI], ps_sm, 0)

    def sp_emit(v_in, out_f16, W, dump_init=False):
        y = y32[:, 0:W]; r = r16[:, 0:W]; t = t16[:, 0:W]
        P.ensure_inc("vector", "tv")
        P.wait("scalar", "tv", P.mark("tv"))
        P.op("scalar", "activation", out=y, in_=v_in, func=ACT.Square,
             bias=0.0, scale=1.0, inc=("tsl", 1))
        P.wait("vector", "tsl", P.mark("tsl"))
        if dump_init:
            P.wait("gpsimd", "tsl", P.mark("tsl"))
            P.dma("gpsimd", "qdbg", out=dbg_d["d_sq"], in_=y)
            P.dma("gpsimd", "qdbg", out=dbg_d["d_sqin"], in_=v_in)
            P.wait("vector", "qdbg")
            P.wait("scalar", "qdbg")
        P.op("vector", "tensor_scalar", out=y, in0=y, scalar1=4.0, scalar2=1e6,
             op0=ALU.add, op1=ALU.min)
        P.op("vector", "tensor_scalar", out=r, in0=y, scalar1=16.0, scalar2=-0.375,
             op0=ALU.is_gt, op1=ALU.mult)
        P.op("vector", "tensor_scalar", out=t, in0=y, scalar1=64.0, scalar2=-0.0625,
             op0=ALU.is_gt, op1=ALU.mult)
        P.op("vector", "tensor_add", out=r, in0=r, in1=t)
        P.op("vector", "tensor_scalar", out=t, in0=y, scalar1=256.0, scalar2=-0.03125,
             op0=ALU.is_gt, op1=ALU.mult)
        P.op("vector", "tensor_add", out=r, in0=r, in1=t)
        P.op("vector", "tensor_scalar", out=r, in0=r, scalar1=0.5, scalar2=None,
             op0=ALU.add)
        if dump_init:
            P.inc_last("vector", "tv")
            P.wait("gpsimd", "tv", P.mark("tv"))
            P.dma("gpsimd", "qdbg", out=dbg_d["d_spy"], in_=y)
            P.dma("gpsimd", "qdbg", out=dbg_d["d_spr"], in_=r)
            P.wait("vector", "qdbg")
        for _ in range(3):
            P.op("vector", "tensor_mul", out=t, in0=r, in1=r)
            P.op("vector", "tensor_mul", out=t, in0=t, in1=y)
            P.op("vector", "tensor_scalar", out=t, in0=t, scalar1=-0.5, scalar2=1.5,
                 op0=ALU.mult, op1=ALU.add)
            P.op("vector", "tensor_mul", out=r, in0=r, in1=t)
            P.op("vector", "tensor_scalar", out=r, in0=r, scalar1=1e-6, scalar2=1000.0,
                 op0=ALU.max, op1=ALU.min)
        P.op("vector", "tensor_mul", out=t, in0=y, in1=r)
        P.op("vector", "tensor_scalar", out=t, in0=t, scalar1=0.0, scalar2=1000.0,
             op0=ALU.max, op1=ALU.min)
        P.op("vector", "tensor_add", out=out_f16, in0=v_in, in1=t)

    nch1 = L // TC1
    NT1 = TC1 // 128
    x0_free = {}
    win_done = {}
    xn_ready_prev = [0]
    y_store_marks = {}
    ev_marks = []
    tr_ev_marks = []
    db_marks = []

    for bb in range(BPC):
        for ci in range(nch1):
            gi = bb * nch1 + ci
            par = gi % 2
            T0 = ci * TC1
            x0v = x0b[:, par * 2048:par * 2048 + 2048]
            if gi >= 2:
                P.wait("sync", "tv", x0_free[gi - 2])
            P.dma("sync", "qx", out=x0v.rearrange("p (n c) -> p n c", c=1024),
                  in_=_tm(x_in[bb], T0, NT1))
            P.wait("scalar", "qx")
            if gi >= 1:
                P.wait("scalar", "tv", xn_ready_prev[0])
            for tt2 in range(NT1):
                P.op("scalar", "activation", out=xc0[:, tt2 * 1024:tt2 * 1024 + 1024],
                     in_=x0v[:, tt2 * 1024:tt2 * 1024 + 1024], func=ACT.Copy,
                     bias=0.0, scale=1.0, accum_out=stats[:, tt2:tt2 + 1])
            P.inc_last("scalar", "tsl")
            P.wait("vector", "tsl", P.mark("tsl"))
            for tt2 in range(NT1):
                mean = stats[:, 4 + tt2:5 + tt2]
                P.op("vector", "tensor_scalar", out=mean, in0=stats[:, tt2:tt2 + 1],
                     scalar1=1.0 / 1024, scalar2=None, op0=ALU.mult)
                P.op("vector", "tensor_scalar", out=xc0[:, tt2 * 1024:tt2 * 1024 + 1024],
                     in0=xc0[:, tt2 * 1024:tt2 * 1024 + 1024],
                     scalar1=mean, scalar2=None, op0=ALU.subtract)
            P.inc_last("vector", "tv")
            x0_free[gi] = P.mark("tv")
            P.wait("scalar", "tv", P.mark("tv"))
            for tt2 in range(NT1):
                P.op("scalar", "activation", out=dump[:],
                     in_=xc0[:, tt2 * 1024:tt2 * 1024 + 1024], func=ACT.Square,
                     bias=0.0, scale=1.0, accum_out=stats[:, tt2:tt2 + 1])
            P.inc_last("scalar", "tsl")
            P.wait("vector", "tsl", P.mark("tsl"))
            for tt2 in range(NT1):
                var = stats[:, 4 + tt2:5 + tt2]
                inv = stats[:, 6 + tt2:7 + tt2]
                msk = stats[:, 7 + tt2:8 + tt2]
                P.op("vector", "tensor_scalar", out=var, in0=stats[:, tt2:tt2 + 1],
                     scalar1=1.0 / 1024, scalar2=1e-9, op0=ALU.mult, op1=ALU.add)
                P.op("vector", "tensor_scalar", out=inv, in0=var, scalar1=4.0,
                     scalar2=-0.5, op0=ALU.is_ge, op1=ALU.mult)
                for thv, dv in [(16.0, -0.25), (64.0, -0.125), (256.0, -0.0625),
                                (1024.0, -0.03125), (4096.0, -0.015625),
                                (16384.0, -0.0078125), (65536.0, -0.00390625)]:
                    P.op("vector", "tensor_scalar", out=msk, in0=var, scalar1=thv,
                         scalar2=dv, op0=ALU.is_ge, op1=ALU.mult)
                    P.op("vector", "tensor_add", out=inv, in0=inv, in1=msk)
                P.op("vector", "tensor_scalar", out=msk, in0=var, scalar1=0.25,
                     scalar2=1.0, op0=ALU.is_lt, op1=ALU.mult)
                P.op("vector", "tensor_add", out=inv, in0=inv, in1=msk)
                P.op("vector", "tensor_scalar", out=msk, in0=var, scalar1=0.0625,
                     scalar2=2.0, op0=ALU.is_lt, op1=ALU.mult)
                P.op("vector", "tensor_add", out=inv, in0=inv, in1=msk)
                P.op("vector", "tensor_scalar", out=inv, in0=inv, scalar1=1.0,
                     scalar2=None, op0=ALU.add)
                P.op("vector", "tensor_scalar", out=xn_tm[:, tt2 * 1024:tt2 * 1024 + 1024],
                     in0=xc0[:, tt2 * 1024:tt2 * 1024 + 1024], scalar1=inv,
                     scalar2=None, op0=ALU.mult)
            P.inc_last("vector", "tv")
            xn_ready = P.mark("tv")
            xn_ready_prev[0] = xn_ready
            if dbg and gi == DBG_CHUNK:
                P.wait("gpsimd", "tv", P.mark("tv"))
                P.dma("gpsimd", "qdbg", out=dbg_d["d_xn"], in_=xn_tm[:])
                P.wait("tensor", "qdbg")
            P.wait("tensor", "tv", xn_ready)
            if gi >= 1:
                P.wait("scalar", "tt", win_done[gi - 1])
            for idx in range(NT1 * 8):
                tt2, kt = divmod(idx, 8)
                slot = idx % 2
                if len(tr_ev_marks) >= 2:
                    P.wait("tensor", "tsl", tr_ev_marks[-2])
                P.op("tensor", "transpose", ps_tr[slot][:],
                     xn_tm[:, tt2 * 1024 + kt * 128: tt2 * 1024 + kt * 128 + 128],
                     ident_b[:], inc=("tt", 1))
                P.wait("scalar", "tt", P.mark("tt"))
                P.op("scalar", "copy",
                     out=xn_cm[:, kt * TC1 + tt2 * 128: kt * TC1 + tt2 * 128 + 128],
                     in_=ps_tr[slot][:], inc=("tsl", 1))
                tr_ev_marks.append(P.mark("tsl"))
            xncm_ready = P.mark("tsl")
            if dbg and gi == DBG_CHUNK:
                P.wait("gpsimd", "tsl", xncm_ready)
                P.dma("gpsimd", "qdbg", out=dbg_d["d_xncm"], in_=xn_cm[:])
                P.wait("tensor", "qdbg")
            P.wait("tensor", "tsl", xncm_ready)
            for jt in range(32):
                slot = jt % 2
                if len(ev_marks) >= 2:
                    P.wait("tensor", "tv", ev_marks[-2])
                for kt in range(8):
                    P.op("tensor", "matmul", ps_xz[slot][:],
                         wqin[:, kt * 4096 + jt * 128: kt * 4096 + jt * 128 + 128],
                         xn_cm[:, kt * TC1:kt * TC1 + TC1],
                         start=(kt == 0), stop=(kt == 7))
                P.inc_last("tensor", "tt")
                P.wait("vector", "tt", P.mark("tt"))
                dstt = xc_sb if jt < 16 else z_sb
                ct = jt % 16
                P.op("vector", "tensor_copy", out=dstt[:, ct * TC1:ct * TC1 + TC1],
                     in_=ps_xz[slot][:], inc=("tv", 1))
                ev_marks.append(P.mark("tv"))
            win_done[gi] = P.mark("tt")
            if dbg and gi == DBG_CHUNK:
                P.wait("gpsimd", "tv", P.mark("tv"))
                P.dma("gpsimd", "qdbg", out=dbg_d["d_xc"], in_=xc_sb[:])
                P.dma("gpsimd", "qdbg", out=dbg_d["d_z"], in_=z_sb[:])
                P.wait("vector", "qdbg")
            if gi >= 1:
                P.wait("vector", "qys", y_store_marks[gi - 1])
            sp_emit(z_sb[:], spz[:], 16 * TC1)
            if ci == 0:
                P.op("vector", "memset", hal[:], 0.0)
            for ct in range(16):
                xcv = xc_sb[:, ct * TC1:ct * TC1 + TC1]
                ov = z_sb[:, ct * TC1:ct * TC1 + TC1]
                P.op("vector", "tensor_scalar", out=ov, in0=xcv,
                     scalar1=convw_sb[:, ct * 4 + 3:ct * 4 + 4],
                     scalar2=convb_sb[:, ct:ct + 1], op0=ALU.mult, op1=ALU.add)
                for k in range(3):
                    sh = 3 - k
                    P.op("vector", "scalar_tensor_tensor",
                         out=z_sb[:, ct * TC1 + sh:ct * TC1 + TC1],
                         in0=xc_sb[:, ct * TC1:ct * TC1 + TC1 - sh],
                         scalar=convw_sb[:, ct * 4 + k:ct * 4 + k + 1],
                         in1=z_sb[:, ct * TC1 + sh:ct * TC1 + TC1],
                         op0=ALU.mult, op1=ALU.add)
                    P.op("vector", "scalar_tensor_tensor",
                         out=z_sb[:, ct * TC1:ct * TC1 + sh],
                         in0=hal[:, ct * 3 + k:ct * 3 + 3],
                         scalar=convw_sb[:, ct * 4 + k:ct * 4 + k + 1],
                         in1=z_sb[:, ct * TC1:ct * TC1 + sh],
                         op0=ALU.mult, op1=ALU.add)
            P.op("vector", "tensor_copy",
                 out=hal[:].rearrange("p (c k) -> p c k", k=3),
                 in_=xc_sb[:].rearrange("p (c t) -> p c t", t=TC1)[:, :, TC1 - 3:TC1])
            if dbg and gi == DBG_CHUNK:
                P.inc_last("vector", "tv")
                P.wait("gpsimd", "tv", P.mark("tv"))
                P.dma("gpsimd", "qdbg", out=dbg_d["d_spz"], in_=spz[:])
                P.dma("gpsimd", "qdbg", out=dbg_d["d_cv"], in_=z_sb[:])
                P.wait("vector", "qdbg")
            sp_emit(z_sb[:], u16[:], 16 * TC1, dump_init=(dbg and gi == DBG_CHUNK))
            P.inc_last("vector", "tv")
            u_ready = P.mark("tv")
            if dbg and gi == DBG_CHUNK:
                P.wait("gpsimd", "tv", P.mark("tv"))
                P.dma("gpsimd", "qdbg", out=dbg_d["d_u"], in_=u16[:])
                P.wait("vector", "qdbg")
            P.wait("tensor", "tv", u_ready)
            if len(db_marks) >= 1:
                P.wait("tensor", "tv", db_marks[-1])
            for kt in range(16):
                P.op("tensor", "matmul", ps_dt[0][0:96, :],
                     wqx[:, kt * 96:kt * 96 + 96], u16[:, kt * TC1:kt * TC1 + TC1],
                     start=(kt == 0), stop=(kt == 15))
            P.inc_last("tensor", "tt")
            P.wait("vector", "tt", P.mark("tt"))
            P.op("vector", "tensor_copy", out=dbl_sb[0:64, :],
                 in_=ps_dt[0][0:64, :], inc=("tv", 1))
            dbl_ready = P.mark("tv")
            P.wait("tensor", "tv", dbl_ready)
            for mt in range(16):
                slot = mt % 2
                if mt >= 2:
                    P.wait("tensor", "tv", db_marks[-2])
                P.op("tensor", "matmul", ps_dt[slot][:],
                     wqdt[0:64, mt * 128:mt * 128 + 128], dbl_sb[0:64, :],
                     start=True, stop=True, inc=("tt", 1))
                P.wait("vector", "tt", P.mark("tt"))
                P.op("vector", "tensor_scalar", out=dt_sb[:, mt * TC1:mt * TC1 + TC1],
                     in0=ps_dt[slot][:],
                     scalar1=bdt_sb[:, mt:mt + 1], scalar2=None, op0=ALU.add,
                     inc=("tv", 1))
                db_marks.append(P.mark("tv"))
            W = 16 * TC1
            aab = y32[:]
            P.wait("scalar", "tv", P.mark("tv"))
            P.op("scalar", "activation", out=aab, in_=dt_sb[:], func=ACT.Abs,
                 bias=0.0, scale=1.0, inc=("tsl", 1))
            P.wait("vector", "tsl", P.mark("tsl"))
            P.op("vector", "tensor_scalar", out=aab, in0=aab, scalar1=1.0,
                 scalar2=None, op0=ALU.add)
            P.op("vector", "reciprocal", out=aab, in_=aab)
            P.op("vector", "tensor_mul", out=dt_sb[:], in0=dt_sb[:], in1=aab)
            P.op("vector", "tensor_scalar", out=dt_sb[:], in0=dt_sb[:], scalar1=0.5,
                 scalar2=0.5, op0=ALU.mult, op1=ALU.add)
            P.op("vector", "tensor_scalar", out=aab, in0=dt_sb[:], scalar1=32768.0,
                 scalar2=2.0 ** 23, op0=ALU.mult, op1=ALU.add)
            P.op("vector", "tensor_scalar", out=aab, in0=aab, scalar1=2.0 ** 23,
                 scalar2=2.0 ** -15, op0=ALU.subtract, op1=ALU.mult)
            zb = z_sb[:]
            P.op("vector", "tensor_scalar", out=zb, in0=dt_sb[:], scalar1=32768.0,
                 scalar2=2.0 ** 52, op0=ALU.mult, op1=ALU.add)
            P.op("vector", "tensor_scalar", out=zb, in0=zb, scalar1=2.0 ** 52,
                 scalar2=2.0 ** -15, op0=ALU.subtract, op1=ALU.mult)
            P.op("vector", "tensor_sub", out=zb, in0=zb, in1=aab)
            P.op("vector", "scalar_tensor_tensor", out=dt_sb[:], in0=zb,
                 scalar=flag_sb[:], in1=aab, op0=ALU.mult, op1=ALU.add)
            P.op("vector", "tensor_scalar", out=aab, in0=dt_sb[:], scalar1=-1.0,
                 scalar2=1.0, op0=ALU.mult, op1=ALU.add)
            P.op("vector", "tensor_mul", out=b16[:], in0=aab, in1=u16[:])
            for ct in range(16):
                ini = 0.0 if ci == 0 else carry[:, ct:ct + 1]
                P.op("vector", "tensor_tensor_scan",
                     out=h_sb[:, ct * TC1:ct * TC1 + TC1],
                     data0=dt_sb[:, ct * TC1:ct * TC1 + TC1],
                     data1=b16[:, ct * TC1:ct * TC1 + TC1],
                     initial=ini, op0=ALU.mult, op1=ALU.add)
            if dbg and gi == DBG_CHUNK:
                P.inc_last("vector", "tv")
                P.wait("gpsimd", "tv", P.mark("tv"))
                P.dma("gpsimd", "qdbg", out=dbg_d["d_dec"], in_=dt_sb[:])
                P.dma("gpsimd", "qdbg", out=dbg_d["d_h"], in_=h_sb[:])
                P.wait("vector", "qdbg")
            P.op("vector", "tensor_copy", out=carry[:],
                 in_=h_sb[:].rearrange("p (c t) -> p c t", t=TC1)[:, :, TC1 - 1:TC1]
                 .rearrange("p c o -> p (c o)"))
            P.op("vector", "tensor_mul", out=spz[:], in0=h_sb[:], in1=spz[:])
            P.inc_last("vector", "tv")
            y_ready = P.mark("tv")
            P.wait("gpsimd", "tv", y_ready)
            P.dma("gpsimd", "qys",
                  out=ys_d[bb].rearrange("(k p) t -> p k t", p=128)[:, :, T0:T0 + TC1],
                  in_=spz[:].rearrange("p (k t) -> p k t", t=TC1))
            y_store_marks[gi] = P.mark("qys")
            if PARANOID:
                P.fence()
    ys_done = P.mark("qys")

    if stop_after == "m1":
        P.wait("gpsimd", "qys", ys_done)
        P.wait("gpsimd", "qdbg")
        return P, dr, es0

    P.wait("vector", "tt", P.mark("tt"))
    P.wait("sync", "tv", P.mark("tv"))
    es1a.close()
    quantize(w_out_t, DI, D, wqin[:, 0:16 * D], 0.25, False, y32[:, 0:1024],
             r16[:, 0:1024], ps_sm, 0)
    wqout = wqin

    NT2 = TC2 // 128
    nch2 = L // TC2
    mo_cm = None
    es2 = ExitStack()
    A2 = Alloc(nc, es2, "m2")
    ybig = A2.sb("ybig", [128, 16 * TC2], MMDT)
    mo_cm2 = A2.sb("mo_cm2", [128, 8 * TC2], F32)
    mo_tm = A2.sb("mo_tm", [128, 4 * 1024], F32)
    x1c = A2.sb("x1c", [128, 4 * 1024], F16)
    ps_mo = [A2.ps(f"ps_mo{i}", [128, TC2], F32) for i in range(2)]
    ps_tr2 = [A2.ps(f"ps_tr2{i}", [128, 128], F32) for i in range(2)]

    mo_marks = []
    tr2_marks = []
    x0_free2 = {}
    x1_vec_done = {}
    x1_store_marks = {}
    ymark_free = {}
    for bb in range(BPC):
        for ci in range(nch2):
            gi = bb * nch2 + ci
            T0 = ci * TC2
            par = gi % 2
            x0v = x0b[:, par * 2048:par * 2048 + 2048]
            if gi >= 1:
                P.wait("sync", "tv", x0_free2[gi - 1])
            else:
                P.wait("sync", "tv", P.mark("tv"))
                P.wait("sync", "qys", ys_done)
            P.dma("sync", "qx", out=x0b[:].rearrange("p (n c) -> p n c", c=1024),
                  in_=_tm(x_in[bb], T0, NT2))
            x0mark = P.mark("qx")
            if gi >= 1:
                P.wait("sync", "tv", ymark_free[gi - 1])
            P.dma("sync", "qyl",
                  out=ybig[:].rearrange("p (k t) -> p k t", t=TC2),
                  in_=ys_d[bb].rearrange("(k p) t -> p k t", p=128)[:, :, T0:T0 + TC2])
            P.wait("tensor", "qyl")
            if gi >= 1:
                P.wait("tensor", "tv", mo_marks[-1])
            for jt in range(8):
                slot = jt % 2
                if jt >= 2:
                    P.wait("tensor", "tv", mo_marks[-2])
                for kt in range(16):
                    P.op("tensor", "matmul", ps_mo[slot][:],
                         wqout[:, kt * 1024 + jt * 128: kt * 1024 + jt * 128 + 128],
                         ybig[:, kt * TC2:kt * TC2 + TC2],
                         start=(kt == 0), stop=(kt == 15))
                P.inc_last("tensor", "tt")
                P.wait("vector", "tt", P.mark("tt"))
                P.op("vector", "tensor_copy", out=mo_cm2[:, jt * TC2:jt * TC2 + TC2],
                     in_=ps_mo[slot][:], inc=("tv", 1))
                mo_marks.append(P.mark("tv"))
            ymark_free[gi] = P.mark("tv")
            P.wait("tensor", "tv", mo_marks[-1])
            if gi >= 1:
                P.wait("scalar", "tv", x1_vec_done[gi - 1])
            for idx in range(NT2 * 8):
                tt2, kt = divmod(idx, 8)
                slot = idx % 2
                if len(tr2_marks) >= 2:
                    P.wait("tensor", "tsl", tr2_marks[-2])
                P.op("tensor", "transpose", ps_tr2[slot][:],
                     mo_cm2[:, kt * TC2 + tt2 * 128: kt * TC2 + tt2 * 128 + 128],
                     ident_f[:], inc=("tt", 1))
                P.wait("scalar", "tt", P.mark("tt"))
                P.op("scalar", "copy",
                     out=mo_tm[:, tt2 * 1024 + kt * 128: tt2 * 1024 + kt * 128 + 128],
                     in_=ps_tr2[slot][:], inc=("tsl", 1))
                tr2_marks.append(P.mark("tsl"))
            P.wait("vector", "tsl", P.mark("tsl"))
            P.wait("vector", "qx", x0mark)
            P.op("vector", "scalar_tensor_tensor", out=mo_tm[:], in0=x0b[:, 0:4096],
                 scalar=2.0, in1=mo_tm[:], op0=ALU.mult, op1=ALU.add,
                 inc=("tv", 1))
            x0_free2[gi] = P.mark("tv")
            P.wait("scalar", "tv", P.mark("tv"))
            for tt2 in range(NT2):
                P.op("scalar", "activation", out=dump[:],
                     in_=mo_tm[:, tt2 * 1024:tt2 * 1024 + 1024], func=ACT.Copy,
                     bias=0.0, scale=1.0, accum_out=stats[:, tt2:tt2 + 1])
            P.inc_last("scalar", "tsl")
            P.wait("vector", "tsl", P.mark("tsl"))
            for tt2 in range(NT2):
                mean = stats[:, 4 + tt2:5 + tt2]
                P.op("vector", "tensor_scalar", out=mean, in0=stats[:, tt2:tt2 + 1],
                     scalar1=1.0 / 1024, scalar2=None, op0=ALU.mult)
                P.op("vector", "tensor_scalar",
                     out=mo_tm[:, tt2 * 1024:tt2 * 1024 + 1024],
                     in0=mo_tm[:, tt2 * 1024:tt2 * 1024 + 1024],
                     scalar1=mean, scalar2=None, op0=ALU.subtract)
            P.inc_last("vector", "tv")
            P.wait("scalar", "tv", P.mark("tv"))
            for tt2 in range(NT2):
                P.op("scalar", "activation", out=dump[:],
                     in_=mo_tm[:, tt2 * 1024:tt2 * 1024 + 1024], func=ACT.Square,
                     bias=0.0, scale=1.0, accum_out=stats[:, tt2:tt2 + 1])
            for tt2 in range(NT2):
                P.op("scalar", "activation", out=stats[:, 4 + tt2:5 + tt2],
                     in_=stats[:, tt2:tt2 + 1], func=ACT.Sqrt, bias=eps5_sb[:],
                     scale=1.0 / 1024)
            P.inc_last("scalar", "tsl")
            P.wait("vector", "tsl", P.mark("tsl"))
            if gi >= 1:
                P.wait("vector", "qx1", x1_store_marks[gi - 1])
            for tt2 in range(NT2):
                P.op("vector", "reciprocal", out=stats[:, 8 + tt2:9 + tt2],
                     in_=stats[:, 4 + tt2:5 + tt2])
                P.op("vector", "tensor_scalar", out=x1c[:, tt2 * 1024:tt2 * 1024 + 1024],
                     in0=mo_tm[:, tt2 * 1024:tt2 * 1024 + 1024],
                     scalar1=stats[:, 8 + tt2:9 + tt2], scalar2=None, op0=ALU.mult)
            P.inc_last("vector", "tv")
            x1_vec_done[gi] = P.mark("tv")
            P.wait("gpsimd", "tv", P.mark("tv"))
            P.dma("gpsimd", "qx1", out=_tm(x1_d[bb], T0, NT2),
                  in_=x1c[:].rearrange("p (n c) -> p n c", c=1024))
            x1_store_marks[gi] = P.mark("qx1")
            if PARANOID:
                P.fence()
    x1_done_q = P.mark("qx1")

    if stop_after == "m2":
        P.wait("gpsimd", "qx1", x1_done_q)
        P.wait("gpsimd", "qdbg")
        return P, dr, es0

    es2.close()
    es1.close()

    es3 = ExitStack()
    A3 = Alloc(nc, es3, "attn")
    wq_sb = A3.sb("wq_sb", [128, 8 * 1024], MMDT)
    wk_sb = A3.sb("wk_sb", [128, 8 * 1024], MMDT)
    wv_sb = A3.sb("wv_sb", [128, 8 * 1024], MMDT)
    wo_sb = A3.sb("wo_sb", [128, 8 * 1024], MMDT)
    h_cm = A3.sb("h_cm", [128, 8 * L], MMDT)
    k_cm = A3.sb("k_cm", [128, 8 * L], MMDT)
    v_buf = A3.sb("v_buf", [128, 16 * (NH * 65)], BF16)
    q_sb = A3.sb("q_sb", [128, 512], MMDT)
    x1c = A3.sb("x1c_a", [128, 4 * 1024], F16)
    pr_sb = A3.sb("pr_sb", [128, 2 * 512], BF16)
    ctx_sb = A3.sb("ctx_sb", [128, 8 * 512], F32)
    ctxn_sb = A3.sb("ctxn_sb", [128, 8 * 512], MMDT)
    ctxo_t1 = A3.sb("ctxo_t1", [128, 2 * 1024], F16)
    xsbuf = A3.sb("xsbuf", [128, 1024], F16)
    dcol = A3.sb("dcol", [128, 512], F32)
    den_b = A3.sb("den_b", [128, 512], F32)
    stats3 = A3.sb("stats3", [128, 16], F32)
    assert MMDT == F16, "in-place rmsnorm assumes x1 spill dtype == MMDT"
    htm = x1c

    p_a = A3.ps("p_a", [128, 512], F32)
    p_b = A3.ps("p_b", [128, 512], F32)
    p_c = A3.ps("p_c", [128, 512], F32)
    p_dd = A3.ps("p_dd", [128, 512], F32)
    p_e = A3.ps("p_e", [128, 512], F32)
    p_f = A3.ps("p_f", [128, 512], F32)
    p_tr3 = A3.ps("p_tr3", [128, 128], MMDT)
    p_tr4 = A3.ps("p_tr4", [128, 128], MMDT)

    wlda = ctx_sb
    P.wait("sync", "qx1", x1_done_q)
    for wi, (wdr, wsb, foldw) in enumerate([(wq_in, wq_sb, True), (wk_in, wk_sb, True),
                                            (wv_in, wv_sb, True), (wo_in, wo_sb, False)]):
        for kt in range(8):
            par = (wi * 8 + kt) % 2
            if len(wcast_marks) >= 2:
                P.wait("sync", "tsl", wcast_marks[-2])
            P.dma("sync", "qw", out=wlda[:, par * 1024:par * 1024 + 1024],
                  in_=wdr[kt * 128:kt * 128 + 128, :])
            P.wait("scalar", "qw")
            P.op("scalar", "activation", out=wsb[:, kt * 1024:kt * 1024 + 1024],
                 in_=wlda[:, par * 1024:par * 1024 + 1024], func=ACT.Copy,
                 bias=0.0, scale=(n1w_sb[:, kt:kt + 1] if foldw else 1.0),
                 inc=("tsl", 1))
            wcast_marks.append(P.mark("tsl"))

    NT2 = TC2 // 128
    for bb in range(BPC):
        trh_marks = []
        kv_marks = []
        P.op("vector", "memset", dcol[:], 1.0)
        P.op("vector", "memset",
             v_buf[:].rearrange("p (tt h d) -> p tt h d", h=NH, d=65)[:, :, :, 64:65],
             1.0, inc=("tv", 1))
        for sc in range(4):
            T0 = sc * TC2
            P.wait("sync", a_x1_free[0][0], a_x1_free[0][1])
            P.dma("sync", "qx", out=x1c[:].rearrange("p (n c) -> p n c", c=1024),
                  in_=_tm(x1_d[bb], T0, NT2))
            P.wait("scalar", "qx")
            for tt2 in range(NT2):
                P.op("scalar", "activation", out=ctx_sb[:, 0:1024],
                     in_=x1c[:, tt2 * 1024:tt2 * 1024 + 1024], func=ACT.Square,
                     bias=0.0, scale=1.0, accum_out=stats3[:, tt2:tt2 + 1])
            for tt2 in range(NT2):
                P.op("scalar", "activation", out=stats3[:, 4 + tt2:5 + tt2],
                     in_=stats3[:, tt2:tt2 + 1], func=ACT.Sqrt, bias=eps6_sb[:],
                     scale=1.0 / 1024)
            P.inc_last("scalar", "tsl")
            P.wait("vector", "tsl", P.mark("tsl"))
            for tt2 in range(NT2):
                P.op("vector", "reciprocal", out=stats3[:, 8 + tt2:9 + tt2],
                     in_=stats3[:, 4 + tt2:5 + tt2])
                P.op("vector", "tensor_scalar", out=htm[:, tt2 * 1024:tt2 * 1024 + 1024],
                     in0=x1c[:, tt2 * 1024:tt2 * 1024 + 1024],
                     scalar1=stats3[:, 8 + tt2:9 + tt2], scalar2=None, op0=ALU.mult)
            P.inc_last("vector", "tv")
            htm_ready = P.mark("tv")
            P.wait("tensor", "tv", htm_ready)
            for idx in range(NT2 * 8):
                tt2, kt = divmod(idx, 8)
                ptr = (p_tr3, p_tr4)[idx % 2]
                if len(trh_marks) >= 2:
                    P.wait("tensor", "tsl", trh_marks[-2])
                P.op("tensor", "transpose", ptr[:],
                     htm[:, tt2 * 1024 + kt * 128: tt2 * 1024 + kt * 128 + 128],
                     ident_b[:], inc=("tt", 1))
                P.wait("scalar", "tt", P.mark("tt"))
                P.op("scalar", "copy",
                     out=h_cm[:, kt * L + T0 + tt2 * 128: kt * L + T0 + tt2 * 128 + 128],
                     in_=ptr[:], inc=("tsl", 1))
                trh_marks.append(P.mark("tsl"))
            hch_ready = P.mark("tsl")
            a_x1_free[0] = ("tsl", hch_ready)
            P.wait("tensor", "tsl", hch_ready)
            for ct in range(8):
                pk = (p_a, p_b)[len(kv_marks) % 2]
                if len(kv_marks) >= 2:
                    P.wait("tensor", "tv", kv_marks[-2])
                for kt in range(8):
                    P.op("tensor", "matmul", pk[:],
                         wk_sb[:, kt * 1024 + ct * 128: kt * 1024 + ct * 128 + 128],
                         h_cm[:, kt * L + T0: kt * L + T0 + TC2],
                         start=(kt == 0), stop=(kt == 7))
                P.inc_last("tensor", "tt")
                P.wait("vector", "tt", P.mark("tt"))
                P.op("vector", "tensor_copy",
                     out=k_cm[:, ct * L + T0: ct * L + T0 + TC2],
                     in_=pk[:], inc=("tv", 1))
                kv_marks.append(P.mark("tv"))
            for tt in range(NT2):
                for half in range(2):
                    pv = (p_c, p_dd)[len(kv_marks) % 2]
                    if len(kv_marks) >= 2:
                        P.wait("tensor", "tv", kv_marks[-2])
                    for kt in range(8):
                        P.op("tensor", "matmul", pv[:],
                             h_cm[:, kt * L + T0 + tt * 128: kt * L + T0 + tt * 128 + 128],
                             wv_sb[:, kt * 1024 + half * 512: kt * 1024 + half * 512 + 512],
                             start=(kt == 0), stop=(kt == 7))
                    P.inc_last("tensor", "tt")
                    P.wait("vector", "tt", P.mark("tt"))
                    gt = sc * NT2 + tt
                    dstv = v_buf[:, gt * 1040 + half * 520: gt * 1040 + half * 520 + 520] \
                        .rearrange("p (h d) -> p h d", d=65)[:, :, 0:64]
                    P.op("vector", "tensor_copy", out=dstv,
                         in_=pv[:].rearrange("p (h d) -> p h d", d=64), inc=("tv", 1))
                    kv_marks.append(P.mark("tv"))
        for sc in range(4):
            T0 = sc * TC2
            den_row_marks = []
            if len(ctv_ev) >= 1:
                P.wait("tensor", "tv", ctv_ev[-1])
            for ct in range(8):
                P.wait("tensor", "tv", q_free[0])
                for kt in range(8):
                    P.op("tensor", "matmul", p_a[:],
                         wq_sb[:, kt * 1024 + ct * 128: kt * 1024 + ct * 128 + 128],
                         h_cm[:, kt * L + T0: kt * L + T0 + TC2],
                         start=(kt == 0), stop=(kt == 7))
                P.inc_last("tensor", "tt")
                P.wait("vector", "tt", P.mark("tt"))
                P.op("vector", "tensor_copy", out=q_sb[:], in_=p_a[:], inc=("tv", 1))
                q_ready = P.mark("tv")
                P.wait("tensor", "tv", q_ready)
                for hh in range(2):
                    h = ct * 2 + hh
                    pctx = (p_c, p_dd)[h % 2]
                    if len(ctx_free_marks) >= 2:
                        P.wait("tensor", "tv", ctx_free_marks[-2])
                    if len(den_row_marks) >= 2:
                        P.wait("tensor", "tsl", den_row_marks[-2])
                    elif den_prev[0] > 0 and len(den_row_marks) == 0:
                        P.wait("tensor", "tsl", den_prev[0])
                    exp_marks = []

                    def issue_score(tt):
                        psc = (p_b, p_e)[tt % 2]
                        P.op("tensor", "matmul", psc[:],
                             k_cm[hh * 64:hh * 64 + 64,
                                  ct * L + tt * 128: ct * L + tt * 128 + 128],
                             q_sb[hh * 64:hh * 64 + 64, 0:512],
                             start=True, stop=True, inc=("tt", 1))
                        P.wait("scalar", "tt", P.mark("tt"))
                        ppr = pr_sb[:, (tt % 2) * 512:(tt % 2) * 512 + 512]
                        P.op("scalar", "activation", out=ppr, in_=psc[:],
                             func=ACT.Exp, bias=0.0, scale=0.125, inc=("tsl", 1))
                        exp_marks.append(P.mark("tsl"))

                    issue_score(0)
                    for tt in range(16):
                        if tt + 1 < 16:
                            issue_score(tt + 1)
                        ppr = pr_sb[:, (tt % 2) * 512:(tt % 2) * 512 + 512]
                        if dbg and bb == 0 and sc == 0 and h == 0 and tt == 0:
                            psc = (p_b, p_e)[tt % 2]
                            P.wait("vector", "tsl", exp_marks[tt])
                            P.op("vector", "tensor_copy", out=ctx_sb[:, 3584:4096],
                                 in_=psc[:], inc=("tv", 1))
                            P.wait("gpsimd", "tv", P.mark("tv"))
                            P.wait("gpsimd", "tsl", exp_marks[tt])
                            P.dma("gpsimd", "qdbg", out=dbg_d["d_pr"], in_=ppr)
                            P.dma("gpsimd", "qdbg", out=dbg_d["d_sc2"],
                                  in_=ctx_sb[:, 3584:4096])
                            P.dma("gpsimd", "qdbg", out=dbg_d["d_qs"], in_=q_sb[:])
                            P.dma("gpsimd", "qdbg", out=dbg_d["d_kc"],
                                  in_=k_cm[:, 0:512])
                            P.wait("scalar", "qdbg")
                            P.wait("tensor", "qdbg")
                            P.wait("vector", "qdbg")
                        P.wait("tensor", "tsl", exp_marks[tt])
                        P.op("tensor", "matmul", pctx[0:65, :],
                             v_buf[:, tt * 1040 + h * 65: tt * 1040 + h * 65 + 65],
                             ppr, start=(tt == 0), stop=(tt == 15))
                    P.inc_last("tensor", "tt")
                    ctx_done = P.mark("tt")
                    P.wait("scalar", "tt", ctx_done)
                    if h % 4 == 0 and dgrp_flush[0] > 0:
                        P.wait("scalar", "qys", dgrp_flush[0])
                    P.op("scalar", "copy",
                         out=dcol[(h % 4) * 32:(h % 4) * 32 + 1, :],
                         in_=pctx[64:65, :], inc=("tsl", 1))
                    den_row_marks.append(P.mark("tsl"))
                    if h % 4 == 3:
                        P.wait("vector", "tsl", P.mark("tsl"))
                        P.op("vector", "reciprocal", out=dcol[:], in_=dcol[:])
                        P.inc_last("vector", "tv")
                        P.wait("gpsimd", "tv", P.mark("tv"))
                        if h == 3:
                            P.wait("gpsimd", "qyl")
                        for pp in range(4):
                            P.dma("gpsimd", "qys",
                                  out=bass.AP(den_d.tensor, (h - 3 + pp) * TC2,
                                              [[1, 1], [1, 512]]),
                                  in_=dcol[pp * 32:pp * 32 + 1, :])
                        dgrp_flush[0] = P.mark("qys")
                    P.wait("vector", "tt", ctx_done)
                    P.op("vector", "tensor_copy",
                         out=ctx_sb[:, ct * 512:ct * 512 + 512][hh * 64:hh * 64 + 64, :],
                         in_=pctx[0:64, :], inc=("tv", 1))
                    ctx_free_marks.append(P.mark("tv"))
            den_prev[0] = den_row_marks[-1]
            P.wait("sync", "qys")
            for ct in range(8):
                for hh in range(2):
                    h = ct * 2 + hh
                    P.dma("sync", "qyl",
                          out=den_b[hh * 64:hh * 64 + 64, :],
                          in_=bass.AP(den_d.tensor, h * TC2, [[0, 64], [1, 512]]))
                P.wait("vector", "qyl")
                P.op("vector", "tensor_mul", out=ctxn_sb[:, ct * 512:ct * 512 + 512],
                     in0=ctx_sb[:, ct * 512:ct * 512 + 512], in1=den_b[:],
                     inc=("tv", 1))
                if dbg and bb == 0 and sc == 0 and ct == 0:
                    P.wait("gpsimd", "tv", P.mark("tv"))
                    P.dma("gpsimd", "qdbg", out=dbg_d["d_ctx"], in_=ctx_sb[:, 0:512])
                    P.dma("gpsimd", "qdbg", out=dbg_d["d_den"], in_=den_b[:])
                    cv = ctxn_sb[:, 0:512]
                    P.dma("gpsimd", "qdbg", out=dbg_d["d_ctxn"], in_=cv)
                    P.wait("vector", "qdbg")
                    P.wait("sync", "qdbg")
                if ct < 7:
                    P.wait("sync", "tv", P.mark("tv"))
            ctxn_ready = P.mark("tv")
            P.wait("tensor", "tv", ctxn_ready)
            for co in range(8):
                pwo = (p_f, p_a)[co % 2]
                if len(wo_ev_marks) >= 2:
                    P.wait("tensor", "tv", wo_ev_marks[-2])
                for kt in range(8):
                    P.op("tensor", "matmul", pwo[:],
                         wo_sb[:, kt * 1024 + co * 128: kt * 1024 + co * 128 + 128],
                         ctxn_sb[:, kt * 512: kt * 512 + 512],
                         start=(kt == 0), stop=(kt == 7))
                P.inc_last("tensor", "tt")
                P.wait("vector", "tt", P.mark("tt"))
                P.op("vector", "tensor_copy", out=ctx_sb[:, co * 512:co * 512 + 512],
                     in_=pwo[:], inc=("tv", 1))
                wo_ev_marks.append(P.mark("tv"))
            ctxo_ready = P.mark("tv")
            q_free[0] = ctxo_ready
            P.wait("sync", a_x1_free[0][0], a_x1_free[0][1])
            P.dma("sync", "qx", out=x1c[:].rearrange("p (n c) -> p n c", c=1024),
                  in_=_tm(x1_d[bb], T0, NT2))
            P.wait("tensor", "tv", ctxo_ready)
            for tt2 in range(NT2):
                par2 = tt2 % 2
                ctv = ctxo_t1[:, par2 * 1024:par2 * 1024 + 1024]
                sctv = ctxo_t1[:, (1 - par2) * 1024:(1 - par2) * 1024 + 1024]
                if len(tro_marks) >= 2:
                    P.wait("tensor", "tv", tro_marks[-2])
                for kt in range(8):
                    pslice = (p_f, p_dd)[kt % 2][:, 0:128]
                    if len(ctv_ev) >= 2:
                        P.wait("tensor", "tv", ctv_ev[-2])
                    P.op("tensor", "transpose", pslice,
                         ctx_sb[:, kt * 512 + tt2 * 128: kt * 512 + tt2 * 128 + 128],
                         ident_f[:], inc=("tt", 1))
                    P.wait("vector", "tt", P.mark("tt"))
                    if kt == 0:
                        if len(x2_marks) >= 1:
                            P.wait("vector", "qx1", x2_marks[-1])
                    P.op("vector", "tensor_copy", out=ctv[:, kt * 128:kt * 128 + 128],
                         in_=pslice, inc=("tv", 1))
                    ctv_ev.append(P.mark("tv"))
                if tt2 == 0:
                    P.wait("vector", "qx")
                x1v = x1c[:, tt2 * 1024:tt2 * 1024 + 1024]
                P.op("vector", "tensor_add", out=xsbuf[:], in0=x1v, in1=ctv)
                P.op("vector", "scalar_tensor_tensor", out=sctv,
                     in0=x1v, scalar=2.0, in1=ctv, op0=ALU.mult, op1=ALU.add,
                     inc=("tv", 1))
                tro_marks.append(P.mark("tv"))
                P.wait("gpsimd", "tv", P.mark("tv"))
                P.dma("gpsimd", "qx1", out=_tm(x2_d[bb], T0 + tt2 * 128, 1),
                      in_=xsbuf[:].rearrange("p (n c) -> p n c", c=1024))
                P.dma("gpsimd", "qx1", out=_tm(ss_d[bb], T0 + tt2 * 128, 1),
                      in_=sctv.rearrange("p (n c) -> p n c", c=1024))
                x2_marks.append(P.mark("qx1"))
            x2_store_m[0] = P.mark("qx1")
            a_x1_free[0] = ("tv", P.mark("tv"))

    x2_done = P.mark("qx1")
    es3.close()

    if stop_after == "attn":
        P.wait("gpsimd", "qx1", x2_done)
        P.wait("gpsimd", "qdbg")
        return P, dr, es0

    TCF = 1024
    NTF = TCF // 128
    es4 = ExitStack()
    A4 = Alloc(nc, es4, "moe")
    wg_sb = A4.sb("wg_sb", [128, 8 * DFF], MMDT)
    wu_sb = A4.sb("wu_sb", [128, 8 * DFF], MMDT)
    wd_sb = A4.sb("wd_sb", [128, 16 * 1024], MMDT)
    h2_cm = A4.sb("h2_cm", [128, 8 * TCF], MMDT)
    act_sb = A4.sb("act_sb", [128, 16 * TCF], MMDT)
    sg_sb = A4.sb("sg_sb", [128, 2 * 512], MMDT)
    spec_sb = A4.sb("spec_sb", [128, 8 * TCF], F16)
    x2cf = A4.sb("x2cf", [128, 8 * 1024], F16)
    ce_all = A4.sb("ce_all", [128, 8 * 8], F16)
    ce_b = A4.sb("ce_b", [128, TCF], F16)
    rw_sb = A4.sb("rw_sb", [128, 8 * 8], MMDT)
    wldf = A4.sb("wldf", [128, 2 * DFF], F32)
    stats4 = A4.sb("stats4", [128, 16], F32)
    assert MMDT == F16
    h2tm = x2cf
    prb = A4.sb("prb", [128, 8], F32)
    prm = A4.sb("prm", [128, 8], F32)
    scf = A4.sb("scf", [128, 8], F32)
    moet1 = A4.sb("moet1", [128, 1024], F32)
    outc = A4.sb("outc", [128, 1024], F32)

    p_g0 = A4.ps("p_g0", [128, 512], F32)
    p_g1 = A4.ps("p_g1", [128, 512], F32)
    p_u0 = A4.ps("p_u0", [128, 512], F32)
    p_u1 = A4.ps("p_u1", [128, 512], F32)
    p_d0 = A4.ps("p_d0", [128, 512], F32)
    p_d1 = A4.ps("p_d1", [128, 512], F32)
    p_tr5 = A4.ps("p_tr5", [128, 128], MMDT)
    p_r = p_d0
    p_tr6 = A4.ps("p_tr6", [128, 128], F16)

    P.wait("sync", "qx1", x2_done)
    P.dma("sync", "qw", out=wldf[:, 0:64].rearrange("p (k e) -> p k e", e=NE),
          in_=rw_in.rearrange("(k p) e -> p k e", p=128), allow_slow_non_contiguous=True)
    P.wait("vector", "qw")
    for kt in range(8):
        P.op("vector", "tensor_scalar", out=rw_sb[:, kt * 8:kt * 8 + 8],
             in0=wldf[:, kt * 8:kt * 8 + 8], scalar1=n2w_sb[:, kt:kt + 1],
             scalar2=None, op0=ALU.mult)
    P.inc_last("vector", "tv")
    rw_ready = P.mark("tv")

    def stream_weight(wdr, wsb, nkt, ncols, fold_n2w, e_idx=None):
        for kt in range(nkt):
            par = len(wc_marks) % 2
            if len(wc_marks) >= 2:
                P.wait("sync", wc_marks[-2][0], wc_marks[-2][1])
            src2 = wdr if e_idx is None else wdr[e_idx]
            P.dma("sync", "qw", out=wldf[:, par * DFF:par * DFF + ncols],
                  in_=src2[kt * 128:kt * 128 + 128, :])
            P.wait("vector", "qw")
            if fold_n2w:
                P.op("vector", "tensor_scalar",
                     out=wsb[:, kt * ncols:kt * ncols + ncols],
                     in0=wldf[:, par * DFF:par * DFF + ncols],
                     scalar1=n2w_sb[:, kt:kt + 1], scalar2=None, op0=ALU.mult,
                     inc=("tv", 1))
            else:
                P.op("vector", "tensor_copy",
                     out=wsb[:, kt * ncols:kt * ncols + ncols],
                     in_=wldf[:, par * DFF:par * DFF + ncols], inc=("tv", 1))
            wc_marks.append(("tv", P.mark("tv")))

    wc_marks = []
    for bb in range(BPC):
        for sc in range(2):
            T0 = sc * TCF
            P.wait("sync", f_x2_free[0][0], f_x2_free[0][1])
            P.dma("sync", "qx", out=x2cf[:].rearrange("p (n c) -> p n c", c=1024),
                  in_=_tm(x2_d[bb], T0, NTF))
            P.wait("scalar", "qx")
            for tt2 in range(NTF):
                P.op("scalar", "activation", out=act_sb[:, 0:1024],
                     in_=x2cf[:, tt2 * 1024:tt2 * 1024 + 1024], func=ACT.Square,
                     bias=0.0, scale=1.0, accum_out=stats4[:, tt2:tt2 + 1])
                P.op("scalar", "activation", out=stats4[:, 8 + tt2:9 + tt2],
                     in_=stats4[:, tt2:tt2 + 1], func=ACT.Sqrt, bias=eps6_sb[:],
                     scale=1.0 / 1024)
            P.inc_last("scalar", "tsl")
            P.wait("vector", "tsl", P.mark("tsl"))
            for tt2 in range(NTF):
                P.op("vector", "reciprocal", out=stats4[:, tt2:tt2 + 1],
                     in_=stats4[:, 8 + tt2:9 + tt2])
                P.op("vector", "tensor_scalar", out=h2tm[:, tt2 * 1024:tt2 * 1024 + 1024],
                     in0=x2cf[:, tt2 * 1024:tt2 * 1024 + 1024],
                     scalar1=stats4[:, tt2:tt2 + 1], scalar2=None, op0=ALU.mult)
            P.inc_last("vector", "tv")
            h2_ready = P.mark("tv")
            P.wait("tensor", "tv", h2_ready)
            trf_marks = []
            for idx in range(NTF * 8):
                tt2, kt = divmod(idx, 8)
                if len(trf_marks) >= 1:
                    P.wait("tensor", "tsl", trf_marks[-1])
                P.op("tensor", "transpose", p_tr5[:],
                     h2tm[:, tt2 * 1024 + kt * 128: tt2 * 1024 + kt * 128 + 128],
                     ident_b[:], inc=("tt", 1))
                P.wait("scalar", "tt", P.mark("tt"))
                P.op("scalar", "copy",
                     out=h2_cm[:, kt * TCF + tt2 * 128: kt * TCF + tt2 * 128 + 128],
                     in_=p_tr5[:], inc=("tsl", 1))
                trf_marks.append(P.mark("tsl"))
            h2cm_ready = P.mark("tsl")
            f_x2_free[0] = ("tsl", h2cm_ready)
            P.wait("tensor", "tsl", h2cm_ready)
            P.wait("tensor", "tv", rw_ready)
            for tt2 in range(NTF):
                if rt_free[0][1] > 0:
                    P.wait("tensor", rt_free[0][0], rt_free[0][1])
                    P.wait("tensor", rt_free[1][0], rt_free[1][1])
                for kt in range(8):
                    P.op("tensor", "matmul", p_r[:, 0:8],
                         h2_cm[:, kt * TCF + tt2 * 128: kt * TCF + tt2 * 128 + 128],
                         rw_sb[:, kt * 8:kt * 8 + 8],
                         start=(kt == 0), stop=(kt == 7))
                P.inc_last("tensor", "tt")
                P.wait("vector", "tt", P.mark("tt"))
                P.op("vector", "reduce_max", out=scf[:, 0:1], in_=p_r[:, 0:8], axis=AX,
                     negate=True)
                P.wait("scalar", "tv", P.mark("tv"))
                P.op("scalar", "activation", out=prb[:], in_=p_r[:, 0:8], func=ACT.Exp,
                     bias=scf[:, 0:1], scale=1.0, accum_out=scf[:, 1:2], inc=("tsl", 1))
                rt_free[0] = ("tv", P.mark("tv"))
                rt_free[1] = ("tsl", P.mark("tsl"))
                P.wait("vector", "tsl", P.mark("tsl"))
                P.op("vector", "reciprocal", out=scf[:, 2:3], in_=scf[:, 1:2])
                P.op("vector", "tensor_scalar", out=prb[:], in0=prb[:],
                     scalar1=scf[:, 2:3], scalar2=None, op0=ALU.mult)
                P.op("vector", "reduce_max", out=scf[:, 3:4], in_=prb[:], axis=AX)
                P.op("vector", "tensor_scalar", out=prm[:], in0=prb[:],
                     scalar1=scf[:, 3:4], scalar2=None, op0=ALU.is_ge)
                P.op("vector", "scalar_tensor_tensor", out=prm[:], in0=prm[:],
                     scalar=-1e30, in1=prb[:], op0=ALU.mult, op1=ALU.add)
                P.op("vector", "reduce_max", out=scf[:, 4:5], in_=prm[:], axis=AX)
                P.inc_last("vector", "tv")
                P.wait("scalar", "tv", P.mark("tv"))
                P.op("scalar", "activation", out=scf[:, 5:6], in_=scf[:, 3:4],
                     func=ACT.Exp, bias=0.0, scale=1.0)
                P.op("scalar", "activation", out=scf[:, 6:7], in_=scf[:, 4:5],
                     func=ACT.Exp, bias=0.0, scale=1.0, inc=("tsl", 1))
                P.wait("vector", "tsl", P.mark("tsl"))
                P.op("vector", "tensor_add", out=scf[:, 7:8], in0=scf[:, 5:6],
                     in1=scf[:, 6:7])
                P.op("vector", "reciprocal", out=scf[:, 7:8], in_=scf[:, 7:8])
                P.op("vector", "tensor_mul", out=scf[:, 5:6], in0=scf[:, 5:6],
                     in1=scf[:, 7:8])
                P.op("vector", "tensor_mul", out=scf[:, 6:7], in0=scf[:, 6:7],
                     in1=scf[:, 7:8])
                P.op("vector", "tensor_scalar", out=ce_all[:, tt2 * 8:tt2 * 8 + 8],
                     in0=prb[:], scalar1=scf[:, 3:4], scalar2=None, op0=ALU.is_ge)
                P.op("vector", "tensor_scalar", out=ce_all[:, tt2 * 8:tt2 * 8 + 8],
                     in0=ce_all[:, tt2 * 8:tt2 * 8 + 8], scalar1=scf[:, 5:6],
                     scalar2=None, op0=ALU.mult)
                P.op("vector", "tensor_scalar", out=prm[:], in0=prm[:],
                     scalar1=scf[:, 4:5], scalar2=None, op0=ALU.is_ge)
                P.op("vector", "tensor_scalar", out=prm[:], in0=prm[:],
                     scalar1=scf[:, 6:7], scalar2=None, op0=ALU.mult)
                P.op("vector", "tensor_add", out=ce_all[:, tt2 * 8:tt2 * 8 + 8],
                     in0=ce_all[:, tt2 * 8:tt2 * 8 + 8], in1=prm[:], inc=("tv", 1))
                if dbg and bb == 1 and sc == 1 and tt2 == 7:
                    P.wait("gpsimd", "tv", P.mark("tv"))
                    P.dma("gpsimd", "qdbg", out=bass.AP(dbg_d["d_rt"].tensor, 0,
                                                        [[64, 128], [1, 8]]),
                          in_=prb[:])
                    P.dma("gpsimd", "qdbg", out=bass.AP(dbg_d["d_rt"].tensor, 8,
                                                        [[64, 128], [1, 8]]),
                          in_=prm[:])
                    P.dma("gpsimd", "qdbg", out=bass.AP(dbg_d["d_rt"].tensor, 16,
                                                        [[64, 128], [1, 8]]),
                          in_=scf[:, 0:8])
                    P.dma("gpsimd", "qdbg", out=bass.AP(dbg_d["d_rt"].tensor, 24,
                                                        [[64, 128], [1, 8]]),
                          in_=ce_all[:, tt2 * 8:tt2 * 8 + 8])
                    P.wait("vector", "qdbg")
                    P.wait("scalar", "qdbg")
            P.wait("gpsimd", "tv", P.mark("tv"))
            P.wait("gpsimd", "qyl")
            for e in range(NE):
                P.dma("gpsimd", "qys",
                      out=bass.AP(ce_d.tensor, e * 2048 + T0, [[1, 128], [128, NTF]]),
                      in_=ce_all[:].rearrange("p (t e) -> p t e", e=NE)[:, :, e],
                      allow_slow_non_contiguous=True)
            P.wait("sync", "qys")
            P.wait("vector", "qo", out_store_m[0])
            P.op("vector", "memset", spec_sb[:], 0.0, inc=("tv", 1))
            for e in range(NE + 1):
                shared = (e == NE)
                P.wait("sync", "tt", wgu_free[0])
                if shared:
                    stream_weight(shg_in, wg_sb, 8, DFF, True)
                    stream_weight(shu_in, wu_sb, 8, DFF, True)
                else:
                    stream_weight(eg_in, wg_sb, 8, DFF, True, e)
                    stream_weight(eu_in, wu_sb, 8, DFF, True, e)
                gu_ready = P.mark("tv")
                if not shared:
                    P.wait("sync", "tv", ceb_free[0])
                    P.dma("sync", "qyl", out=ce_b[:],
                          in_=bass.AP(ce_d.tensor, e * 2048 + T0, [[0, 128], [1, TCF]]))
                P.wait("tensor", "tv", gu_ready)
                for ft in range(16):
                    for tn in range(TCF // 512):
                        sl = (ft * 2 + tn) % 2
                        pg = (p_g0, p_g1)[sl]
                        pu = (p_u0, p_u1)[sl]
                        if len(act_marks) >= 2:
                            P.wait("tensor", "tv", act_marks[-2])
                        for kt in range(8):
                            P.op("tensor", "matmul", pg[:],
                                 wg_sb[:, kt * DFF + ft * 128: kt * DFF + ft * 128 + 128],
                                 h2_cm[:, kt * TCF + tn * 512: kt * TCF + tn * 512 + 512],
                                 start=(kt == 0), stop=(kt == 7))
                        for kt in range(8):
                            P.op("tensor", "matmul", pu[:],
                                 wu_sb[:, kt * DFF + ft * 128: kt * DFF + ft * 128 + 128],
                                 h2_cm[:, kt * TCF + tn * 512: kt * TCF + tn * 512 + 512],
                                 start=(kt == 0), stop=(kt == 7))
                        P.inc_last("tensor", "tt")
                        P.wait("scalar", "tt", P.mark("tt"))
                        sgv = sg_sb[:, tn * 512:tn * 512 + 512]
                        P.op("scalar", "activation", out=sgv, in_=pg[:], func=ACT.Silu,
                             bias=0.0, scale=1.0, inc=("tsl", 1))
                        P.wait("vector", "tsl", P.mark("tsl"))
                        dst = act_sb[:, ft * TCF + tn * 512: ft * TCF + tn * 512 + 512]
                        P.op("vector", "tensor_mul", out=dst, in0=sgv, in1=pu[:])
                        if not shared:
                            P.wait("vector", "qyl")
                            P.op("vector", "tensor_mul", out=dst, in0=dst,
                                 in1=ce_b[:, tn * 512:tn * 512 + 512])
                        P.inc_last("vector", "tv")
                        act_marks.append(P.mark("tv"))
                act_ready = P.mark("tv")
                ceb_free[0] = act_ready
                wgu_free[0] = P.mark("tt")
                P.wait("sync", "tt", wd_free[0])
                stream_weight(shd_in if shared else ed_in, wd_sb, 16, 1024, False,
                              None if shared else e)
                wd_ready = P.mark("tv")
                P.wait("tensor", "tv", wd_ready)
                P.wait("tensor", "tv", act_ready)
                for co in range(8):
                    for tn in range(TCF // 512):
                        pd = (p_d0, p_d1)[(co * 2 + tn) % 2]
                        if len(dn_marks) >= 2:
                            P.wait("tensor", "tv", dn_marks[-2])
                        for ft in range(16):
                            P.op("tensor", "matmul", pd[:],
                                 wd_sb[:, ft * 1024 + co * 128: ft * 1024 + co * 128 + 128],
                                 act_sb[:, ft * TCF + tn * 512: ft * TCF + tn * 512 + 512],
                                 start=(ft == 0), stop=(ft == 15))
                        P.inc_last("tensor", "tt")
                        P.wait("vector", "tt", P.mark("tt"))
                        dsts = spec_sb[:, co * TCF + tn * 512: co * TCF + tn * 512 + 512]
                        P.op("vector", "tensor_add", out=dsts, in0=dsts, in1=pd[:],
                             inc=("tv", 1))
                        dn_marks.append(P.mark("tv"))
                wd_free[0] = P.mark("tt")
            spec_ready = P.mark("tv")
            P.wait("sync", f_x2_free[0][0], f_x2_free[0][1])
            P.dma("sync", "qx", out=x2cf[:].rearrange("p (n c) -> p n c", c=1024),
                  in_=_tm(ss_d[bb], T0, NTF))
            P.wait("tensor", "tv", spec_ready)
            for tt2 in range(NTF):
                if moet_free[0] > 0:
                    P.wait("tensor", "tv", moet_free[0])
                for kt in range(8):
                    P.op("tensor", "transpose", p_tr6[:],
                         spec_sb[:, kt * TCF + tt2 * 128: kt * TCF + tt2 * 128 + 128],
                         ident_h[:], inc=("tt", 1))
                    P.wait("vector", "tt", P.mark("tt"))
                    P.op("vector", "tensor_copy", out=moet1[:, kt * 128:kt * 128 + 128],
                         in_=p_tr6[:], inc=("tv", 1))
                    P.wait("tensor", "tv", P.mark("tv"))
                if tt2 == 0:
                    P.wait("vector", "qx")
                P.op("vector", "tensor_add", out=moet1[:], in0=moet1[:],
                     in1=x2cf[:, tt2 * 1024:tt2 * 1024 + 1024], inc=("tv", 1))
                moet_free[0] = P.mark("tv")
                f_x2_free[0] = ("tv", P.mark("tv"))
                P.wait("scalar", "tv", P.mark("tv"))
                P.op("scalar", "activation", out=act_sb[:, 0:1024], in_=moet1[:],
                     func=ACT.Copy, bias=0.0, scale=1.0, accum_out=stats4[:, 0:1],
                     inc=("tsl", 1))
                P.wait("vector", "tsl", P.mark("tsl"))
                P.op("vector", "tensor_scalar", out=stats4[:, 1:2], in0=stats4[:, 0:1],
                     scalar1=1.0 / 1024, scalar2=None, op0=ALU.mult)
                P.op("vector", "tensor_scalar", out=moet1[:], in0=moet1[:],
                     scalar1=stats4[:, 1:2], scalar2=None, op0=ALU.subtract,
                     inc=("tv", 1))
                P.wait("scalar", "tv", P.mark("tv"))
                P.op("scalar", "activation", out=act_sb[:, 0:1024], in_=moet1[:],
                     func=ACT.Square, bias=0.0, scale=1.0, accum_out=stats4[:, 2:3])
                P.op("scalar", "activation", out=stats4[:, 3:4], in_=stats4[:, 2:3],
                     func=ACT.Sqrt, bias=eps5_sb[:], scale=1.0 / 1024, inc=("tsl", 1))
                P.wait("vector", "tsl", P.mark("tsl"))
                P.op("vector", "reciprocal", out=stats4[:, 4:5], in_=stats4[:, 3:4])
                ov = outc[:, 0:1024]
                if len(out_t_marks) >= 1:
                    P.wait("vector", "qo", out_t_marks[-1])
                P.op("vector", "tensor_scalar", out=ov, in0=moet1[:],
                     scalar1=stats4[:, 4:5], scalar2=None, op0=ALU.mult,
                     inc=("tv", 1))
                P.wait("gpsimd", "tv", P.mark("tv"))
                P.dma("gpsimd", "qo", out=_tm(out_d[bb], T0 + tt2 * 128, 1),
                      in_=ov.rearrange("p (n c) -> p n c", c=1024))
                out_t_marks.append(P.mark("qo"))
            out_store_m[0] = P.mark("qo")

    es4.close()
    P.wait("gpsimd", "qo", P.mark("qo"))
    P.wait("gpsimd", "qdbg")
    return P, dr, es0


def host_inputs(inputs):
    x = np.ascontiguousarray(inputs["L0_attn_x"])
    maps = []
    w_in_t = np.ascontiguousarray(inputs["L0_attn_W_in"].T)
    w_x_t = np.ascontiguousarray(inputs["L0_attn_W_x"].T)
    w_dt_t = np.ascontiguousarray(inputs["L0_attn_W_dt"].T)
    w_out_t = np.ascontiguousarray(inputs["L0_attn_W_out"].T)
    convw = np.ascontiguousarray(inputs["L0_attn_conv_w"][:, 0, :])
    ident = np.eye(128, dtype=np.float32)
    common = dict(
        w_in_t=w_in_t, w_x_t=w_x_t, w_dt_t=w_dt_t, w_out_t=w_out_t,
        gamma=inputs["L0_attn_gamma"], step=inputs["L0_attn_step_size"],
        convw=convw, convb=inputs["L0_attn_conv_b"], bdt=inputs["L0_attn_b_dt"],
        ident=ident,
        n1w=inputs["L0_ffn_norm1_w"], n2w=inputs["L0_ffn_norm2_w"],
        wq=inputs["L0_ffn_wq"], wk=inputs["L0_ffn_wk"], wv=inputs["L0_ffn_wv"],
        wo=inputs["L0_ffn_wo"], rw=inputs["L0_ffn_router_w"],
        shg=inputs["L0_ffn_sh_gate"], shu=inputs["L0_ffn_sh_up"],
        shd=inputs["L0_ffn_sh_down"], eg=inputs["L0_ffn_e_gate"],
        eu=inputs["L0_ffn_e_up"], ed=inputs["L0_ffn_e_down"],
    )
    for c in range(NCORES):
        m = dict(common)
        m["x"] = np.ascontiguousarray(x[c * BPC:(c + 1) * BPC])
        maps.append(m)
    return maps


def build_nc(dbg=False, stop_after=None):
    nc = bass.Bass("TRN2", target_bir_lowering=False, detect_race_conditions=False)
    P, dr, es0 = build_core(nc, dbg=dbg, stop_after=stop_after)
    with nc.Block() as block:
        P.build(block)
    print("instr counts:", P.nops())
    return nc


def kernel(**inputs):
    nc = build_nc(dbg=False)
    maps = host_inputs(inputs)
    res = run_bass_kernel_spmd(nc, maps, core_ids=list(range(NCORES)))
    return np.concatenate([np.asarray(r["out"]) for r in res.results], axis=0)


# version 04
# speedup vs baseline: 1.1243x; 1.1242x over previous
import numpy as np
import ml_dtypes
import concourse.bass as bass
import concourse.mybir as mybir
from concourse.bass_utils import run_bass_kernel_spmd
from contextlib import ExitStack

F32 = mybir.dt.float32
F16 = mybir.dt.float16
BF16 = mybir.dt.bfloat16
ALU = mybir.AluOpType
ACT = mybir.ActivationFunctionType
AX = mybir.AxisListType.X

NCORES = 8
DBG_CHUNK = -1
PARANOID = False
B, L, D = 16, 2048, 1024
BPC = B // NCORES
DI = 2048
DTR = 64
NH, DH = 16, 64
NE, DFF = 8, 2048
TC1 = 256
TC2 = 512
MMDT = F16

ENGINES = ("sync", "vector", "scalar", "tensor", "gpsimd")
DTB = {F32: 4, F16: 2, BF16: 2, mybir.dt.int32: 4}


class Prog:
    def __init__(self, nc, es):
        self.nc = nc
        self.es = es
        self.items = {e: [] for e in ENGINES}
        self.cnt = {}
        self.sems = {}

    def sem(self, name):
        if name not in self.sems:
            self.sems[name] = self.es.enter_context(self.nc.semaphore(name))
            self.cnt[name] = 0
        return name

    HAZARD_OPS = ("reduce_sum", "reduce_max", "tensor_reduce",
                  "tensor_tensor_scan", "reciprocal", "bn_stats", "bn_aggr")
    ENG_SEM = {"vector": "tv", "gpsimd": "tg", "scalar": "tsl"}

    def _small_out(self, meth, a, kw):
        if kw.get("accum_out") is not None:
            return True
        obj = kw.get("out")
        if obj is None and a:
            obj = a[0]
        shp = getattr(obj, "shape", None)
        if not shp or len(shp) < 2:
            return False
        free = 1
        for d in list(shp)[1:]:
            free *= int(d)
        return free < 64

    def op(self, eng, meth, *a, inc=None, **kw):
        hazard = eng in ("vector", "gpsimd", "scalar") and (
            meth in self.HAZARD_OPS or self._small_out(meth, a, kw))
        if hazard and inc is None:
            inc = (self.ENG_SEM[eng], 1)
        if inc is not None:
            self.cnt[inc[0]] += inc[1]
            semname = inc[0]
            inc = (self.sems[inc[0]], inc[1])
        self.items[eng].append((meth, a, kw, inc))
        if hazard:
            self.items[eng].append(("wait_ge", (inc[0], self.cnt[semname]), {}, None))

    def inc_last(self, eng, semname, n=1):
        idx = len(self.items[eng]) - 1
        while idx >= 0 and self.items[eng][idx][0] == "wait_ge":
            idx -= 1
        meth, a, kw, inc = self.items[eng][idx]
        if inc is not None:
            assert inc[0] is self.sems[semname], f"{meth} inc on other sem"
            return
        self.cnt[semname] += n
        self.items[eng][idx] = (meth, a, kw, (self.sems[semname], n))

    def mark(self, semname):
        return self.cnt[semname]

    def ensure_inc(self, eng, semname):
        self.inc_last(eng, semname)

    def wait(self, eng, semname, val=None):
        v = self.cnt[semname] if val is None else val
        if v > 0:
            self.items[eng].append(("wait_ge", (self.sems[semname], v), {}, None))

    def dma(self, eng, qsem, out, in_, **kw):
        self.op(eng, "dma_start", inc=(qsem, 16), out=out, in_=in_, **kw)

    def build(self, block):
        for eng in ENGINES:
            if not self.items[eng]:
                continue
            items = self.items[eng]

            def body(h, items=items):
                for meth, a, kw, inc in items:
                    r = getattr(h, meth)(*a, **kw)
                    if inc is not None:
                        r.then_inc(inc[0], inc[1])
            getattr(block, eng)(body)

    def nops(self):
        return {e: len(v) for e, v in self.items.items()}

    def fence(self, engines=ENGINES, sems=("tv", "tt", "tsl", "qs", "qg")):
        for eng in engines:
            if not self.items[eng]:
                continue
            meth, a, kw, inc = self.items[eng][-1]
            if inc is None and meth != "wait_ge":
                sm = {"sync": "qs", "gpsimd": "tg", "vector": "tv",
                      "scalar": "tsl", "tensor": "tt"}[eng]
                if meth == "dma_start":
                    sm = "qs" if eng == "sync" else "qg"
                    self.inc_last(eng, sm, 16)
                else:
                    self.inc_last(eng, sm)
        vals = {s: self.cnt[s] for s in list(sems) + ["tg"]}
        for eng in engines:
            for s, v in vals.items():
                if v > 0:
                    self.items[eng].append(("wait_ge", (self.sems[s], v), {}, None))


class Alloc:
    def __init__(self, nc, es, label=""):
        self.nc = nc
        self.es = es
        self.used = 0
        self.label = label

    def sb(self, name, shape, dtype):
        t = self.es.enter_context(self.nc.sbuf_tensor(name, shape, dtype))
        self.used += int(np.prod(shape[1:])) * DTB[dtype]
        return t

    def ps(self, name, shape, dtype=F32):
        return self.es.enter_context(self.nc.psum_tensor(name, shape, dtype))


def _tm(dram2d, t0, nt, w=D):
    return dram2d[t0:t0 + nt * 128, :].rearrange("(n p) c -> p n c", p=128)


def build_core(nc, dbg=False, stop_after=None):
    dr = {}

    def din(name, shape, dtype=F32):
        dr[name] = nc.dram_tensor(name, list(shape), dtype, kind="ExternalInput").ap()
        return dr[name]

    x_in = din("x", (BPC, L, D))
    w_in_t = din("w_in_t", (D, 2 * DI))
    w_x_t = din("w_x_t", (DI, 96))
    w_dt_t = din("w_dt_t", (DTR, DI))
    w_out_t = din("w_out_t", (DI, D))
    gamma_in = din("gamma", (D,))
    step_in = din("step", (1,))
    convw_in = din("convw", (DI, 4))
    convb_in = din("convb", (DI,))
    bdt_in = din("bdt", (DI,))
    ident_in = din("ident", (128, 128))
    n1w_in = din("n1w", (D,))
    n2w_in = din("n2w", (D,))
    wq_in = din("wq", (D, D)); wk_in = din("wk", (D, D))
    wv_in = din("wv", (D, D)); wo_in = din("wo", (D, D))
    rw_in = din("rw", (D, NE))
    shg_in = din("shg", (D, DFF)); shu_in = din("shu", (D, DFF))
    shd_in = din("shd", (DFF, D))
    eg_in = din("eg", (NE, D, DFF)); eu_in = din("eu", (NE, D, DFF))
    ed_in = din("ed", (NE, DFF, D))

    out_d = nc.dram_tensor("out", [BPC, L, D], F32, kind="ExternalOutput").ap()
    sk = "ExternalOutput" if dbg else "Internal"
    ys_d = nc.dram_tensor("ys", [BPC, DI, L], MMDT, kind=sk).ap()
    dbg_d = {}
    if dbg:
        for nm, sh, dt_ in [("d_xn", [128, 2048], MMDT), ("d_xc", [128, 4096], F32),
                            ("d_z", [128, 4096], F32), ("d_cv", [128, 4096], F32),
                            ("d_u", [128, 4096], MMDT), ("d_spz", [128, 4096], MMDT),
                            ("d_dec", [128, 4096], F32), ("d_h", [128, 4096], F32),
                            ("d_spy", [128, 4096], F32), ("d_spr", [128, 4096], F32),
                            ("d_sq", [128, 4096], F32), ("d_sqin", [128, 4096], F32),
                            ("d_wq", [128, 512], MMDT), ("d_smc", [128, 16], F32),
                            ("d_rt", [128, 64], F32), ("d_h2", [128, 128], MMDT),
                            ("d_ctx", [128, 512], F32), ("d_den", [128, 512], F32),
                            ("d_pr", [128, 512], F32), ("d_ctxn", [128, 512], F32),
                            ("d_sc2", [128, 512], F32), ("d_qs", [128, 512], MMDT),
                            ("d_kc", [128, 512], MMDT),
                            ("d_xncm", [128, 2048], MMDT)]:
            dbg_d[nm] = nc.dram_tensor(nm, sh, dt_, kind="ExternalOutput").ap()
    x1_d = nc.dram_tensor("x1s", [BPC, L, D], F16, kind=sk).ap()
    x2_d = nc.dram_tensor("x2s", [BPC, L, D], F16, kind=sk).ap()
    ss_d = nc.dram_tensor("sss", [BPC, L, D], F16, kind=sk).ap()
    den_d = nc.dram_tensor("dens", [NH, TC2], F32, kind="Internal").ap()
    ce_d = nc.dram_tensor("ces", [NE, 2048], F16, kind="Internal").ap()
    wfg_d = nc.dram_tensor("wfg", [NE + 1, 128, 8 * DFF], F16, kind="Internal").ap()
    wfu_d = nc.dram_tensor("wfu", [NE + 1, 128, 8 * DFF], F16, kind="Internal").ap()
    wfd_d = nc.dram_tensor("wfd", [NE + 1, 128, 16 * 1024], F16, kind="Internal").ap()

    es0 = ExitStack()
    nc._mk_es0 = es0
    P = Prog(nc, es0)
    A0 = Alloc(nc, es0)

    for s in ("qs", "qg", "tv", "tt", "tsl", "tg", "qx", "qw", "qyl",
              "qys", "qx1", "qo", "qdbg"):
        P.sem(s)

    ident_b = A0.sb("ident_b", [128, 128], MMDT)
    ident_f = A0.sb("ident_f", [128, 128], F32)
    gamma_sb = A0.sb("gamma_sb", [128, 8], F32)
    n1w_sb = A0.sb("n1w_sb", [128, 8], F32)
    n2w_sb = A0.sb("n2w_sb", [128, 8], F32)
    convw_sb = A0.sb("convw_sb", [128, 64], F32)
    convb_sb = A0.sb("convb_sb", [128, 16], F32)
    bdt_sb = A0.sb("bdt_sb", [128, 16], F32)
    step_sb = A0.sb("step_sb", [1, 1], F32)
    ones_sb = A0.sb("ones_sb", [128, 1], F32)
    ones_mat = A0.sb("ones_mat", [128, 128], MMDT)
    stepv = A0.sb("stepv", [128, 1], F32)
    stepv16 = A0.sb("stepv16", [128, 1], MMDT)
    acc16 = A0.sb("acc16", [128, 1], MMDT)
    ones_b = A0.sb("ones_b", [128, 1], MMDT)
    flag_sb = A0.sb("flag_sb", [128, 1], F32)
    eps5_sb = A0.sb("eps5_sb", [128, 1], F32)
    eps6_sb = A0.sb("eps6_sb", [128, 1], F32)
    smc = A0.sb("smc", [128, 16], F32)
    ident_h = A0.sb("ident_h", [128, 128], F16)

    P.dma("sync", "qs", out=ident_f[:], in_=ident_in)
    P.dma("sync", "qs", out=gamma_sb[:], in_=gamma_in.rearrange("(k p) -> p k", p=128), allow_slow_non_contiguous=True)
    P.dma("sync", "qs", out=n1w_sb[:], in_=n1w_in.rearrange("(k p) -> p k", p=128), allow_slow_non_contiguous=True)
    P.dma("sync", "qs", out=n2w_sb[:], in_=n2w_in.rearrange("(k p) -> p k", p=128), allow_slow_non_contiguous=True)
    P.dma("sync", "qs", out=convw_sb[:].rearrange("p (c k) -> p c k", k=4),
          in_=convw_in.rearrange("(c p) k -> p c k", p=128), allow_slow_non_contiguous=True)
    P.dma("sync", "qs", out=convb_sb[:], in_=convb_in.rearrange("(c p) -> p c", p=128), allow_slow_non_contiguous=True)
    P.dma("sync", "qs", out=bdt_sb[:], in_=bdt_in.rearrange("(c p) -> p c", p=128), allow_slow_non_contiguous=True)
    P.dma("sync", "qs", out=step_sb[:, 0:1], in_=step_in.rearrange("(a o) -> a o", a=1))
    ld0 = P.mark("qs")

    P.wait("vector", "qs", ld0)
    P.op("vector", "memset", ones_sb[:], 1.0)
    P.op("vector", "memset", ones_mat[:], 1.0)
    P.op("vector", "memset", smc[:], 0.0)
    P.op("vector", "memset", ones_b[:], 1.0)
    P.op("vector", "memset", eps5_sb[:], 1e-5)
    P.op("vector", "memset", eps6_sb[:], 1e-6)
    P.op("vector", "tensor_copy", out=ident_b[:], in_=ident_f[:])
    P.op("vector", "tensor_copy", out=ident_h[:], in_=ident_f[:])
    P.op("vector", "tensor_scalar", out=flag_sb[:], in0=ones_sb[:],
         scalar1=2.0 ** -30, scalar2=1.0, op0=ALU.add, op1=ALU.subtract)
    P.op("vector", "tensor_scalar", out=flag_sb[:], in0=flag_sb[:],
         scalar1=0.0, scalar2=None, op0=ALU.is_gt, inc=("tv", 1))
    P.op("vector", "memset", stepv[:], 0.0)
    P.op("vector", "tensor_copy", out=stepv[0:1, 0:1], in_=step_sb[:, 0:1])
    P.op("vector", "tensor_copy", out=stepv16[:], in_=stepv[:], inc=("tv", 1))

    def quantize(w_dram, K, M, wq_sb, fold_const, fold_gamma, wld, wmask, ps_sm, scol):
        nkt = (K + 127) // 128
        kp = min(K, 128)
        acc = smc[:, scol:scol + 1]
        red = smc[:, scol + 1:scol + 2]
        for kt in range(nkt):
            P.wait("sync", "tv", P.mark("tv"))
            P.dma("sync", "qw", out=wld[:kp, :M], in_=w_dram[kt * 128:kt * 128 + kp, :])
            P.wait("vector", "qw")
            P.op("vector", "reduce_sum", out=red[:kp, :], in_=wld[:kp, :M], axis=AX,
                 apply_absolute_value=True)
            if kt == 0:
                P.op("vector", "memset", acc[:], 0.0)
            P.op("vector", "tensor_add", out=acc[:kp, :], in0=acc[:kp, :],
                 in1=red[:kp, :], inc=("tv", 1))
        P.op("vector", "tensor_copy", out=acc16[:], in_=acc[:], inc=("tv", 1))
        P.wait("tensor", "tv", P.mark("tv"))
        P.op("tensor", "matmul", ps_sm[:, 0:1], ones_mat[:], acc16[:],
             start=True, stop=True, inc=("tt", 1))
        P.wait("vector", "tt", P.mark("tt"))
        thr = smc[:, scol + 2:scol + 3]
        fold = smc[:, scol + 3:scol + 4]
        nthr = smc[:, scol + 4:scol + 5]
        P.op("vector", "tensor_scalar", out=thr, in0=ps_sm[:, 0:1],
             scalar1=1.0 / (K * M), scalar2=1e-8, op0=ALU.mult, op1=ALU.add)
        P.op("vector", "tensor_scalar", out=fold, in0=thr,
             scalar1=fold_const, scalar2=None, op0=ALU.mult)
        if fold_gamma:
            P.op("vector", "tensor_scalar", out=fold, in0=fold,
                 scalar1=smc[:, 15:16], scalar2=None, op0=ALU.mult)
        P.op("vector", "tensor_scalar", out=thr, in0=thr,
             scalar1=0.5, scalar2=None, op0=ALU.mult)
        P.op("vector", "tensor_scalar", out=nthr, in0=thr, scalar1=-1.0, scalar2=None,
             op0=ALU.mult, inc=("tv", 1))
        for kt in range(nkt):
            P.wait("sync", "tv", P.mark("tv"))
            P.dma("sync", "qw", out=wld[:kp, :M], in_=w_dram[kt * 128:kt * 128 + kp, :])
            P.wait("vector", "qw")
            P.op("vector", "tensor_scalar", out=wmask[:kp, :M], in0=wld[:kp, :M],
                 scalar1=thr[:kp, :], scalar2=None, op0=ALU.is_gt)
            P.op("vector", "scalar_tensor_tensor", out=wmask[:kp, :M],
                 in0=wld[:kp, :M], scalar=nthr[:kp, :], in1=wmask[:kp, :M],
                 op0=ALU.is_lt, op1=ALU.subtract)
            if fold_gamma:
                gsc = smc[:, scol + 5:scol + 6]
                P.op("vector", "tensor_scalar", out=gsc, in0=gamma_sb[:, kt:kt + 1],
                     scalar1=fold, scalar2=-1.0, op0=ALU.mult, op1=ALU.mult)
                P.op("vector", "tensor_scalar", out=wq_sb[:kp, kt * M:kt * M + M],
                     in0=wmask[:kp, :M], scalar1=gsc[:kp, :], scalar2=None,
                     op0=ALU.mult, inc=("tv", 1))
            else:
                P.op("vector", "tensor_scalar", out=wq_sb[:kp, kt * M:kt * M + M],
                     in0=wmask[:kp, :M], scalar1=fold[:kp, :], scalar2=-1.0,
                     op0=ALU.mult, op1=ALU.mult, inc=("tv", 1))

    es1 = ExitStack()
    A1 = Alloc(nc, es1, "m")
    wqin = A1.sb("wqin", [128, 8 * 4096], MMDT)
    wqx = A1.sb("wqx", [128, 16 * 96], MMDT)
    wqdt = A1.sb("wqdt", [128, DI], MMDT)
    x0b = A1.sb("x0b", [128, 2 * 2048], F32)
    y32 = A1.sb("y32", [128, 16 * TC1], F32)
    r16 = A1.sb("r16", [128, 16 * TC1], F16)
    t16 = A1.sb("t16", [128, 16 * TC1], F16)
    stats = A1.sb("stats", [128, 16], F32)
    carry = A1.sb("carry", [128, 16], F32)
    dump = A1.sb("dump", [128, 1024], F32)
    ps_sm = A1.ps("ps_sm", [128, 16], F32)

    es1a = ExitStack()
    A1a = Alloc(nc, es1a, "m1only")
    xc0 = A1a.sb("xc0", [128, 2048], F32)
    xn_tm = A1a.sb("xn_tm", [128, 2048], MMDT)
    xn_cm = A1a.sb("xn_cm", [128, 8 * TC1], MMDT)
    xc_sb = A1a.sb("xc_sb", [128, 16 * TC1], F32)
    z_sb = A1a.sb("z_sb", [128, 16 * TC1], F32)
    spz = A1a.sb("spz", [128, 16 * TC1], MMDT)
    u16 = A1a.sb("u16", [128, 16 * TC1], MMDT)
    b16 = A1a.sb("b16", [128, 16 * TC1], F16)
    dbl_sb = A1a.sb("dbl_sb", [128, TC1], MMDT)
    hal = A1a.sb("hal", [128, 48], F32)
    ps_xz = [A1a.ps(f"ps_xz{i}", [128, TC1], F32) for i in range(2)]
    ps_tr = [A1a.ps(f"ps_tr{i}", [128, 128], MMDT) for i in range(2)]
    ps_dt = [A1a.ps(f"ps_dt{i}", [128, TC1], F32) for i in range(2)]
    dt_sb = xc_sb
    h_sb = y32

    P.wait("tensor", "tv", P.mark("tv"))
    P.op("tensor", "matmul", ps_sm[:, 0:1], ones_mat[:], stepv16[:],
         start=True, stop=True, inc=("tt", 1))
    P.wait("vector", "tt", P.mark("tt"))
    P.op("vector", "tensor_copy", out=smc[:, 15:16], in_=ps_sm[:, 0:1],
         inc=("tv", 1))

    wcast_marks = []
    a_x1_free = [("tv", 0)]
    sc_free_marks = []
    ctx_free_marks = []
    wo_ev_marks = []
    x2_store_m = [0]
    x2_marks = []
    tro_marks = []
    ctv_ev = []
    q_free = [0]
    den_prev = [0]
    dgrp_flush = [0]
    tro_free = [0]
    f_x2_free = [("tv", 0)]
    wgu_free = [0]
    wd_free = [0]
    ceb_free = [0]
    act_marks = []
    dn_marks = []
    out_store_m = [0]
    out_t_marks = []
    moet_free = [0]
    rt_free = [("tv", 0), ("tsl", 0)]
    quantize(w_in_t, D, 2 * DI, wqin, 1.0, True, y32[:, 0:4096], r16[:, 0:4096],
             ps_sm, 0)
    if dbg:
        P.ensure_inc("vector", "tv")
        P.wait("gpsimd", "tv", P.mark("tv"))
        P.dma("gpsimd", "qdbg", out=dbg_d["d_wq"], in_=wqin[:, 0:512])
        P.dma("gpsimd", "qdbg", out=dbg_d["d_smc"], in_=smc[:])
        P.wait("vector", "qdbg")
    quantize(w_x_t, DI, 96, wqx, 0.5, False, y32[:, 0:96], r16[:, 0:96], ps_sm, 6)
    quantize(w_dt_t, DTR, DI, wqdt, 1.0, False, y32[:, 0:DI], r16[:, 0:DI], ps_sm, 0)

    def sp_emit(v_in, out_f16, W, dump_init=False):
        y = y32[:, 0:W]; r = r16[:, 0:W]; t = t16[:, 0:W]
        P.ensure_inc("vector", "tv")
        P.wait("scalar", "tv", P.mark("tv"))
        P.op("scalar", "activation", out=y, in_=v_in, func=ACT.Square,
             bias=0.0, scale=1.0, inc=("tsl", 1))
        P.wait("vector", "tsl", P.mark("tsl"))
        if dump_init:
            P.wait("gpsimd", "tsl", P.mark("tsl"))
            P.dma("gpsimd", "qdbg", out=dbg_d["d_sq"], in_=y)
            P.dma("gpsimd", "qdbg", out=dbg_d["d_sqin"], in_=v_in)
            P.wait("vector", "qdbg")
            P.wait("scalar", "qdbg")
        P.op("vector", "tensor_scalar", out=y, in0=y, scalar1=4.0, scalar2=1e6,
             op0=ALU.add, op1=ALU.min)
        P.op("vector", "tensor_scalar", out=r, in0=y, scalar1=16.0, scalar2=-0.375,
             op0=ALU.is_gt, op1=ALU.mult)
        P.op("vector", "tensor_scalar", out=t, in0=y, scalar1=64.0, scalar2=-0.0625,
             op0=ALU.is_gt, op1=ALU.mult)
        P.op("vector", "tensor_add", out=r, in0=r, in1=t)
        P.op("vector", "tensor_scalar", out=t, in0=y, scalar1=256.0, scalar2=-0.03125,
             op0=ALU.is_gt, op1=ALU.mult)
        P.op("vector", "tensor_add", out=r, in0=r, in1=t)
        P.op("vector", "tensor_scalar", out=r, in0=r, scalar1=0.5, scalar2=None,
             op0=ALU.add)
        if dump_init:
            P.inc_last("vector", "tv")
            P.wait("gpsimd", "tv", P.mark("tv"))
            P.dma("gpsimd", "qdbg", out=dbg_d["d_spy"], in_=y)
            P.dma("gpsimd", "qdbg", out=dbg_d["d_spr"], in_=r)
            P.wait("vector", "qdbg")
        for _ in range(3):
            P.op("vector", "tensor_mul", out=t, in0=r, in1=r)
            P.op("vector", "tensor_mul", out=t, in0=t, in1=y)
            P.op("vector", "tensor_scalar", out=t, in0=t, scalar1=-0.5, scalar2=1.5,
                 op0=ALU.mult, op1=ALU.add)
            P.op("vector", "tensor_mul", out=r, in0=r, in1=t)
            P.op("vector", "tensor_scalar", out=r, in0=r, scalar1=1e-6, scalar2=1000.0,
                 op0=ALU.max, op1=ALU.min)
        P.op("vector", "tensor_mul", out=t, in0=y, in1=r)
        P.op("vector", "tensor_scalar", out=t, in0=t, scalar1=0.0, scalar2=1000.0,
             op0=ALU.max, op1=ALU.min)
        P.op("vector", "tensor_add", out=out_f16, in0=v_in, in1=t)

    nch1 = L // TC1
    NT1 = TC1 // 128
    x0_free = {}
    win_done = {}
    xn_ready_prev = [0]
    y_store_marks = {}
    ev_marks = []
    tr_ev_marks = []
    db_marks = []

    for bb in range(BPC):
        for ci in range(nch1):
            gi = bb * nch1 + ci
            par = gi % 2
            T0 = ci * TC1
            x0v = x0b[:, par * 2048:par * 2048 + 2048]
            if gi >= 2:
                P.wait("sync", "tv", x0_free[gi - 2])
            P.dma("sync", "qx", out=x0v.rearrange("p (n c) -> p n c", c=1024),
                  in_=_tm(x_in[bb], T0, NT1))
            P.wait("scalar", "qx")
            if gi >= 1:
                P.wait("scalar", "tv", xn_ready_prev[0])
            for tt2 in range(NT1):
                P.op("scalar", "activation", out=xc0[:, tt2 * 1024:tt2 * 1024 + 1024],
                     in_=x0v[:, tt2 * 1024:tt2 * 1024 + 1024], func=ACT.Copy,
                     bias=0.0, scale=1.0, accum_out=stats[:, tt2:tt2 + 1])
            P.inc_last("scalar", "tsl")
            P.wait("vector", "tsl", P.mark("tsl"))
            for tt2 in range(NT1):
                mean = stats[:, 4 + tt2:5 + tt2]
                P.op("vector", "tensor_scalar", out=mean, in0=stats[:, tt2:tt2 + 1],
                     scalar1=1.0 / 1024, scalar2=None, op0=ALU.mult)
                P.op("vector", "tensor_scalar", out=xc0[:, tt2 * 1024:tt2 * 1024 + 1024],
                     in0=xc0[:, tt2 * 1024:tt2 * 1024 + 1024],
                     scalar1=mean, scalar2=None, op0=ALU.subtract)
            P.inc_last("vector", "tv")
            x0_free[gi] = P.mark("tv")
            P.wait("scalar", "tv", P.mark("tv"))
            for tt2 in range(NT1):
                P.op("scalar", "activation", out=dump[:],
                     in_=xc0[:, tt2 * 1024:tt2 * 1024 + 1024], func=ACT.Square,
                     bias=0.0, scale=1.0, accum_out=stats[:, tt2:tt2 + 1])
            P.inc_last("scalar", "tsl")
            P.wait("vector", "tsl", P.mark("tsl"))
            for tt2 in range(NT1):
                var = stats[:, 4 + tt2:5 + tt2]
                inv = stats[:, 6 + tt2:7 + tt2]
                msk = stats[:, 7 + tt2:8 + tt2]
                P.op("vector", "tensor_scalar", out=var, in0=stats[:, tt2:tt2 + 1],
                     scalar1=1.0 / 1024, scalar2=1e-9, op0=ALU.mult, op1=ALU.add)
                P.op("vector", "tensor_scalar", out=inv, in0=var, scalar1=4.0,
                     scalar2=-0.5, op0=ALU.is_ge, op1=ALU.mult)
                for thv, dv in [(16.0, -0.25), (64.0, -0.125), (256.0, -0.0625),
                                (1024.0, -0.03125), (4096.0, -0.015625),
                                (16384.0, -0.0078125), (65536.0, -0.00390625)]:
                    P.op("vector", "tensor_scalar", out=msk, in0=var, scalar1=thv,
                         scalar2=dv, op0=ALU.is_ge, op1=ALU.mult)
                    P.op("vector", "tensor_add", out=inv, in0=inv, in1=msk)
                P.op("vector", "tensor_scalar", out=msk, in0=var, scalar1=0.25,
                     scalar2=1.0, op0=ALU.is_lt, op1=ALU.mult)
                P.op("vector", "tensor_add", out=inv, in0=inv, in1=msk)
                P.op("vector", "tensor_scalar", out=msk, in0=var, scalar1=0.0625,
                     scalar2=2.0, op0=ALU.is_lt, op1=ALU.mult)
                P.op("vector", "tensor_add", out=inv, in0=inv, in1=msk)
                P.op("vector", "tensor_scalar", out=inv, in0=inv, scalar1=1.0,
                     scalar2=None, op0=ALU.add)
                P.op("vector", "tensor_scalar", out=xn_tm[:, tt2 * 1024:tt2 * 1024 + 1024],
                     in0=xc0[:, tt2 * 1024:tt2 * 1024 + 1024], scalar1=inv,
                     scalar2=None, op0=ALU.mult)
            P.inc_last("vector", "tv")
            xn_ready = P.mark("tv")
            xn_ready_prev[0] = xn_ready
            if dbg and gi == DBG_CHUNK:
                P.wait("gpsimd", "tv", P.mark("tv"))
                P.dma("gpsimd", "qdbg", out=dbg_d["d_xn"], in_=xn_tm[:])
                P.wait("tensor", "qdbg")
            P.wait("tensor", "tv", xn_ready)
            if gi >= 1:
                P.wait("scalar", "tt", win_done[gi - 1])
            for idx in range(NT1 * 8):
                tt2, kt = divmod(idx, 8)
                slot = idx % 2
                if len(tr_ev_marks) >= 2:
                    P.wait("tensor", "tsl", tr_ev_marks[-2])
                P.op("tensor", "transpose", ps_tr[slot][:],
                     xn_tm[:, tt2 * 1024 + kt * 128: tt2 * 1024 + kt * 128 + 128],
                     ident_b[:], inc=("tt", 1))
                P.wait("scalar", "tt", P.mark("tt"))
                P.op("scalar", "copy",
                     out=xn_cm[:, kt * TC1 + tt2 * 128: kt * TC1 + tt2 * 128 + 128],
                     in_=ps_tr[slot][:], inc=("tsl", 1))
                tr_ev_marks.append(P.mark("tsl"))
            xncm_ready = P.mark("tsl")
            if dbg and gi == DBG_CHUNK:
                P.wait("gpsimd", "tsl", xncm_ready)
                P.dma("gpsimd", "qdbg", out=dbg_d["d_xncm"], in_=xn_cm[:])
                P.wait("tensor", "qdbg")
            P.wait("tensor", "tsl", xncm_ready)
            for jt in range(32):
                slot = jt % 2
                if len(ev_marks) >= 2:
                    P.wait("tensor", "tv", ev_marks[-2])
                for kt in range(8):
                    P.op("tensor", "matmul", ps_xz[slot][:],
                         wqin[:, kt * 4096 + jt * 128: kt * 4096 + jt * 128 + 128],
                         xn_cm[:, kt * TC1:kt * TC1 + TC1],
                         start=(kt == 0), stop=(kt == 7))
                P.inc_last("tensor", "tt")
                P.wait("vector", "tt", P.mark("tt"))
                dstt = xc_sb if jt < 16 else z_sb
                ct = jt % 16
                P.op("vector", "tensor_copy", out=dstt[:, ct * TC1:ct * TC1 + TC1],
                     in_=ps_xz[slot][:], inc=("tv", 1))
                ev_marks.append(P.mark("tv"))
            win_done[gi] = P.mark("tt")
            if dbg and gi == DBG_CHUNK:
                P.wait("gpsimd", "tv", P.mark("tv"))
                P.dma("gpsimd", "qdbg", out=dbg_d["d_xc"], in_=xc_sb[:])
                P.dma("gpsimd", "qdbg", out=dbg_d["d_z"], in_=z_sb[:])
                P.wait("vector", "qdbg")
            if gi >= 1:
                P.wait("vector", "qys", y_store_marks[gi - 1])
            sp_emit(z_sb[:], spz[:], 16 * TC1)
            if ci == 0:
                P.op("vector", "memset", hal[:], 0.0)
            for ct in range(16):
                xcv = xc_sb[:, ct * TC1:ct * TC1 + TC1]
                ov = z_sb[:, ct * TC1:ct * TC1 + TC1]
                P.op("vector", "tensor_scalar", out=ov, in0=xcv,
                     scalar1=convw_sb[:, ct * 4 + 3:ct * 4 + 4],
                     scalar2=convb_sb[:, ct:ct + 1], op0=ALU.mult, op1=ALU.add)
                for k in range(3):
                    sh = 3 - k
                    P.op("vector", "scalar_tensor_tensor",
                         out=z_sb[:, ct * TC1 + sh:ct * TC1 + TC1],
                         in0=xc_sb[:, ct * TC1:ct * TC1 + TC1 - sh],
                         scalar=convw_sb[:, ct * 4 + k:ct * 4 + k + 1],
                         in1=z_sb[:, ct * TC1 + sh:ct * TC1 + TC1],
                         op0=ALU.mult, op1=ALU.add)
                    P.op("vector", "scalar_tensor_tensor",
                         out=z_sb[:, ct * TC1:ct * TC1 + sh],
                         in0=hal[:, ct * 3 + k:ct * 3 + 3],
                         scalar=convw_sb[:, ct * 4 + k:ct * 4 + k + 1],
                         in1=z_sb[:, ct * TC1:ct * TC1 + sh],
                         op0=ALU.mult, op1=ALU.add)
            P.op("vector", "tensor_copy",
                 out=hal[:].rearrange("p (c k) -> p c k", k=3),
                 in_=xc_sb[:].rearrange("p (c t) -> p c t", t=TC1)[:, :, TC1 - 3:TC1])
            if dbg and gi == DBG_CHUNK:
                P.inc_last("vector", "tv")
                P.wait("gpsimd", "tv", P.mark("tv"))
                P.dma("gpsimd", "qdbg", out=dbg_d["d_spz"], in_=spz[:])
                P.dma("gpsimd", "qdbg", out=dbg_d["d_cv"], in_=z_sb[:])
                P.wait("vector", "qdbg")
            sp_emit(z_sb[:], u16[:], 16 * TC1, dump_init=(dbg and gi == DBG_CHUNK))
            P.inc_last("vector", "tv")
            u_ready = P.mark("tv")
            if dbg and gi == DBG_CHUNK:
                P.wait("gpsimd", "tv", P.mark("tv"))
                P.dma("gpsimd", "qdbg", out=dbg_d["d_u"], in_=u16[:])
                P.wait("vector", "qdbg")
            P.wait("tensor", "tv", u_ready)
            if len(db_marks) >= 1:
                P.wait("tensor", "tv", db_marks[-1])
            for kt in range(16):
                P.op("tensor", "matmul", ps_dt[0][0:96, :],
                     wqx[:, kt * 96:kt * 96 + 96], u16[:, kt * TC1:kt * TC1 + TC1],
                     start=(kt == 0), stop=(kt == 15))
            P.inc_last("tensor", "tt")
            P.wait("vector", "tt", P.mark("tt"))
            P.op("vector", "tensor_copy", out=dbl_sb[0:64, :],
                 in_=ps_dt[0][0:64, :], inc=("tv", 1))
            dbl_ready = P.mark("tv")
            P.wait("tensor", "tv", dbl_ready)
            for mt in range(16):
                slot = mt % 2
                if mt >= 2:
                    P.wait("tensor", "tv", db_marks[-2])
                P.op("tensor", "matmul", ps_dt[slot][:],
                     wqdt[0:64, mt * 128:mt * 128 + 128], dbl_sb[0:64, :],
                     start=True, stop=True, inc=("tt", 1))
                P.wait("vector", "tt", P.mark("tt"))
                P.op("vector", "tensor_scalar", out=dt_sb[:, mt * TC1:mt * TC1 + TC1],
                     in0=ps_dt[slot][:],
                     scalar1=bdt_sb[:, mt:mt + 1], scalar2=None, op0=ALU.add,
                     inc=("tv", 1))
                db_marks.append(P.mark("tv"))
            W = 16 * TC1
            aab = y32[:]
            P.wait("scalar", "tv", P.mark("tv"))
            P.op("scalar", "activation", out=aab, in_=dt_sb[:], func=ACT.Abs,
                 bias=0.0, scale=1.0, inc=("tsl", 1))
            P.wait("vector", "tsl", P.mark("tsl"))
            P.op("vector", "tensor_scalar", out=aab, in0=aab, scalar1=1.0,
                 scalar2=None, op0=ALU.add)
            P.op("vector", "reciprocal", out=aab, in_=aab)
            P.op("vector", "tensor_mul", out=dt_sb[:], in0=dt_sb[:], in1=aab)
            P.op("vector", "tensor_scalar", out=dt_sb[:], in0=dt_sb[:], scalar1=0.5,
                 scalar2=0.5, op0=ALU.mult, op1=ALU.add)
            P.op("vector", "tensor_scalar", out=aab, in0=dt_sb[:], scalar1=32768.0,
                 scalar2=2.0 ** 23, op0=ALU.mult, op1=ALU.add)
            P.op("vector", "tensor_scalar", out=aab, in0=aab, scalar1=2.0 ** 23,
                 scalar2=2.0 ** -15, op0=ALU.subtract, op1=ALU.mult)
            zb = z_sb[:]
            P.op("vector", "tensor_scalar", out=zb, in0=dt_sb[:], scalar1=32768.0,
                 scalar2=2.0 ** 52, op0=ALU.mult, op1=ALU.add)
            P.op("vector", "tensor_scalar", out=zb, in0=zb, scalar1=2.0 ** 52,
                 scalar2=2.0 ** -15, op0=ALU.subtract, op1=ALU.mult)
            P.op("vector", "tensor_sub", out=zb, in0=zb, in1=aab)
            P.op("vector", "scalar_tensor_tensor", out=dt_sb[:], in0=zb,
                 scalar=flag_sb[:], in1=aab, op0=ALU.mult, op1=ALU.add)
            P.op("vector", "tensor_scalar", out=aab, in0=dt_sb[:], scalar1=-1.0,
                 scalar2=1.0, op0=ALU.mult, op1=ALU.add)
            P.op("vector", "tensor_mul", out=b16[:], in0=aab, in1=u16[:])
            for ct in range(16):
                ini = 0.0 if ci == 0 else carry[:, ct:ct + 1]
                P.op("vector", "tensor_tensor_scan",
                     out=h_sb[:, ct * TC1:ct * TC1 + TC1],
                     data0=dt_sb[:, ct * TC1:ct * TC1 + TC1],
                     data1=b16[:, ct * TC1:ct * TC1 + TC1],
                     initial=ini, op0=ALU.mult, op1=ALU.add)
            if dbg and gi == DBG_CHUNK:
                P.inc_last("vector", "tv")
                P.wait("gpsimd", "tv", P.mark("tv"))
                P.dma("gpsimd", "qdbg", out=dbg_d["d_dec"], in_=dt_sb[:])
                P.dma("gpsimd", "qdbg", out=dbg_d["d_h"], in_=h_sb[:])
                P.wait("vector", "qdbg")
            P.op("vector", "tensor_copy", out=carry[:],
                 in_=h_sb[:].rearrange("p (c t) -> p c t", t=TC1)[:, :, TC1 - 1:TC1]
                 .rearrange("p c o -> p (c o)"))
            P.op("vector", "tensor_mul", out=spz[:], in0=h_sb[:], in1=spz[:])
            P.inc_last("vector", "tv")
            y_ready = P.mark("tv")
            P.wait("gpsimd", "tv", y_ready)
            P.dma("gpsimd", "qys",
                  out=ys_d[bb].rearrange("(k p) t -> p k t", p=128)[:, :, T0:T0 + TC1],
                  in_=spz[:].rearrange("p (k t) -> p k t", t=TC1))
            y_store_marks[gi] = P.mark("qys")
            if PARANOID:
                P.fence()
    ys_done = P.mark("qys")

    if stop_after == "m1":
        P.wait("gpsimd", "qys", ys_done)
        P.wait("gpsimd", "qdbg")
        return P, dr, es0

    P.wait("vector", "tt", P.mark("tt"))
    P.wait("sync", "tv", P.mark("tv"))
    es1a.close()
    quantize(w_out_t, DI, D, wqin[:, 0:16 * D], 0.25, False, y32[:, 0:1024],
             r16[:, 0:1024], ps_sm, 0)
    wqout = wqin

    NT2 = TC2 // 128
    nch2 = L // TC2
    mo_cm = None
    es2 = ExitStack()
    A2 = Alloc(nc, es2, "m2")
    ybig = A2.sb("ybig", [128, 16 * TC2], MMDT)
    mo_cm2 = A2.sb("mo_cm2", [128, 8 * TC2], F32)
    mo_tm = A2.sb("mo_tm", [128, 4 * 1024], F32)
    x1c = A2.sb("x1c", [128, 4 * 1024], F16)
    ps_mo = [A2.ps(f"ps_mo{i}", [128, TC2], F32) for i in range(2)]
    ps_tr2 = [A2.ps(f"ps_tr2{i}", [128, 128], F32) for i in range(2)]

    mo_marks = []
    tr2_marks = []
    x0_free2 = {}
    x1_vec_done = {}
    x1_store_marks = {}
    ymark_free = {}
    for bb in range(BPC):
        for ci in range(nch2):
            gi = bb * nch2 + ci
            T0 = ci * TC2
            par = gi % 2
            x0v = x0b[:, par * 2048:par * 2048 + 2048]
            if gi >= 1:
                P.wait("sync", "tv", x0_free2[gi - 1])
            else:
                P.wait("sync", "tv", P.mark("tv"))
                P.wait("sync", "qys", ys_done)
            P.dma("sync", "qx", out=x0b[:].rearrange("p (n c) -> p n c", c=1024),
                  in_=_tm(x_in[bb], T0, NT2))
            x0mark = P.mark("qx")
            if gi >= 1:
                P.wait("sync", "tv", ymark_free[gi - 1])
            P.dma("sync", "qyl",
                  out=ybig[:].rearrange("p (k t) -> p k t", t=TC2),
                  in_=ys_d[bb].rearrange("(k p) t -> p k t", p=128)[:, :, T0:T0 + TC2])
            P.wait("tensor", "qyl")
            if gi >= 1:
                P.wait("tensor", "tv", mo_marks[-1])
            for jt in range(8):
                slot = jt % 2
                if jt >= 2:
                    P.wait("tensor", "tv", mo_marks[-2])
                for kt in range(16):
                    P.op("tensor", "matmul", ps_mo[slot][:],
                         wqout[:, kt * 1024 + jt * 128: kt * 1024 + jt * 128 + 128],
                         ybig[:, kt * TC2:kt * TC2 + TC2],
                         start=(kt == 0), stop=(kt == 15))
                P.inc_last("tensor", "tt")
                P.wait("vector", "tt", P.mark("tt"))
                P.op("vector", "tensor_copy", out=mo_cm2[:, jt * TC2:jt * TC2 + TC2],
                     in_=ps_mo[slot][:], inc=("tv", 1))
                mo_marks.append(P.mark("tv"))
            ymark_free[gi] = P.mark("tv")
            P.wait("tensor", "tv", mo_marks[-1])
            if gi >= 1:
                P.wait("scalar", "tv", x1_vec_done[gi - 1])
            for idx in range(NT2 * 8):
                tt2, kt = divmod(idx, 8)
                slot = idx % 2
                if len(tr2_marks) >= 2:
                    P.wait("tensor", "tsl", tr2_marks[-2])
                P.op("tensor", "transpose", ps_tr2[slot][:],
                     mo_cm2[:, kt * TC2 + tt2 * 128: kt * TC2 + tt2 * 128 + 128],
                     ident_f[:], inc=("tt", 1))
                P.wait("scalar", "tt", P.mark("tt"))
                P.op("scalar", "copy",
                     out=mo_tm[:, tt2 * 1024 + kt * 128: tt2 * 1024 + kt * 128 + 128],
                     in_=ps_tr2[slot][:], inc=("tsl", 1))
                tr2_marks.append(P.mark("tsl"))
            P.wait("vector", "tsl", P.mark("tsl"))
            P.wait("vector", "qx", x0mark)
            P.op("vector", "scalar_tensor_tensor", out=mo_tm[:], in0=x0b[:, 0:4096],
                 scalar=2.0, in1=mo_tm[:], op0=ALU.mult, op1=ALU.add,
                 inc=("tv", 1))
            x0_free2[gi] = P.mark("tv")
            P.wait("scalar", "tv", P.mark("tv"))
            for tt2 in range(NT2):
                P.op("scalar", "activation", out=dump[:],
                     in_=mo_tm[:, tt2 * 1024:tt2 * 1024 + 1024], func=ACT.Copy,
                     bias=0.0, scale=1.0, accum_out=stats[:, tt2:tt2 + 1])
            P.inc_last("scalar", "tsl")
            P.wait("vector", "tsl", P.mark("tsl"))
            for tt2 in range(NT2):
                mean = stats[:, 4 + tt2:5 + tt2]
                P.op("vector", "tensor_scalar", out=mean, in0=stats[:, tt2:tt2 + 1],
                     scalar1=1.0 / 1024, scalar2=None, op0=ALU.mult)
                P.op("vector", "tensor_scalar",
                     out=mo_tm[:, tt2 * 1024:tt2 * 1024 + 1024],
                     in0=mo_tm[:, tt2 * 1024:tt2 * 1024 + 1024],
                     scalar1=mean, scalar2=None, op0=ALU.subtract)
            P.inc_last("vector", "tv")
            P.wait("scalar", "tv", P.mark("tv"))
            for tt2 in range(NT2):
                P.op("scalar", "activation", out=dump[:],
                     in_=mo_tm[:, tt2 * 1024:tt2 * 1024 + 1024], func=ACT.Square,
                     bias=0.0, scale=1.0, accum_out=stats[:, tt2:tt2 + 1])
            for tt2 in range(NT2):
                P.op("scalar", "activation", out=stats[:, 4 + tt2:5 + tt2],
                     in_=stats[:, tt2:tt2 + 1], func=ACT.Sqrt, bias=eps5_sb[:],
                     scale=1.0 / 1024)
            P.inc_last("scalar", "tsl")
            P.wait("vector", "tsl", P.mark("tsl"))
            if gi >= 1:
                P.wait("vector", "qx1", x1_store_marks[gi - 1])
            for tt2 in range(NT2):
                P.op("vector", "reciprocal", out=stats[:, 8 + tt2:9 + tt2],
                     in_=stats[:, 4 + tt2:5 + tt2])
                P.op("vector", "tensor_scalar", out=x1c[:, tt2 * 1024:tt2 * 1024 + 1024],
                     in0=mo_tm[:, tt2 * 1024:tt2 * 1024 + 1024],
                     scalar1=stats[:, 8 + tt2:9 + tt2], scalar2=None, op0=ALU.mult)
            P.inc_last("vector", "tv")
            x1_vec_done[gi] = P.mark("tv")
            P.wait("gpsimd", "tv", P.mark("tv"))
            P.dma("gpsimd", "qx1", out=_tm(x1_d[bb], T0, NT2),
                  in_=x1c[:].rearrange("p (n c) -> p n c", c=1024))
            x1_store_marks[gi] = P.mark("qx1")
            if PARANOID:
                P.fence()
    x1_done_q = P.mark("qx1")

    if stop_after == "m2":
        P.wait("gpsimd", "qx1", x1_done_q)
        P.wait("gpsimd", "qdbg")
        return P, dr, es0

    es2.close()
    es1.close()

    es3 = ExitStack()
    A3 = Alloc(nc, es3, "attn")
    wq_sb = A3.sb("wq_sb", [128, 8 * 1024], MMDT)
    wk_sb = A3.sb("wk_sb", [128, 8 * 1024], MMDT)
    wv_sb = A3.sb("wv_sb", [128, 8 * 1024], MMDT)
    wo_sb = A3.sb("wo_sb", [128, 8 * 1024], MMDT)
    h_cm = A3.sb("h_cm", [128, 8 * L], MMDT)
    k_cm = A3.sb("k_cm", [128, 8 * L], MMDT)
    v_buf = A3.sb("v_buf", [128, 16 * (NH * 65)], BF16)
    q_sb = A3.sb("q_sb", [128, 512], MMDT)
    x1c = A3.sb("x1c_a", [128, 4 * 1024], F16)
    pr_sb = A3.sb("pr_sb", [128, 2 * 512], BF16)
    ctx_sb = A3.sb("ctx_sb", [128, 8 * 512], F32)
    ctxn_sb = A3.sb("ctxn_sb", [128, 8 * 512], MMDT)
    ctxo_t1 = A3.sb("ctxo_t1", [128, 2 * 1024], F16)
    xsbuf = A3.sb("xsbuf", [128, 1024], F16)
    dcol = A3.sb("dcol", [128, 512], F32)
    den_b = A3.sb("den_b", [128, 512], F32)
    stats3 = A3.sb("stats3", [128, 16], F32)
    assert MMDT == F16, "in-place rmsnorm assumes x1 spill dtype == MMDT"
    htm = x1c

    p_a = A3.ps("p_a", [128, 512], F32)
    p_b = A3.ps("p_b", [128, 512], F32)
    p_c = A3.ps("p_c", [128, 512], F32)
    p_dd = A3.ps("p_dd", [128, 512], F32)
    p_e = A3.ps("p_e", [128, 512], F32)
    p_f = A3.ps("p_f", [128, 512], F32)
    p_tr3 = A3.ps("p_tr3", [128, 128], MMDT)
    p_tr4 = A3.ps("p_tr4", [128, 128], MMDT)

    wlda = ctx_sb
    P.wait("sync", "qx1", x1_done_q)
    for wi, (wdr, wsb, foldw) in enumerate([(wq_in, wq_sb, True), (wk_in, wk_sb, True),
                                            (wv_in, wv_sb, True), (wo_in, wo_sb, False)]):
        for kt in range(8):
            par = (wi * 8 + kt) % 2
            if len(wcast_marks) >= 2:
                P.wait("sync", "tsl", wcast_marks[-2])
            P.dma("sync", "qw", out=wlda[:, par * 1024:par * 1024 + 1024],
                  in_=wdr[kt * 128:kt * 128 + 128, :])
            P.wait("scalar", "qw")
            P.op("scalar", "activation", out=wsb[:, kt * 1024:kt * 1024 + 1024],
                 in_=wlda[:, par * 1024:par * 1024 + 1024], func=ACT.Copy,
                 bias=0.0, scale=(n1w_sb[:, kt:kt + 1] if foldw else 1.0),
                 inc=("tsl", 1))
            wcast_marks.append(P.mark("tsl"))

    NT2 = TC2 // 128
    for bb in range(BPC):
        trh_marks = []
        kv_marks = []
        P.op("vector", "memset", dcol[:], 1.0)
        P.op("vector", "memset",
             v_buf[:].rearrange("p (tt h d) -> p tt h d", h=NH, d=65)[:, :, :, 64:65],
             1.0, inc=("tv", 1))
        for sc in range(4):
            T0 = sc * TC2
            P.wait("sync", a_x1_free[0][0], a_x1_free[0][1])
            P.dma("sync", "qx", out=x1c[:].rearrange("p (n c) -> p n c", c=1024),
                  in_=_tm(x1_d[bb], T0, NT2))
            P.wait("scalar", "qx")
            for tt2 in range(NT2):
                P.op("scalar", "activation", out=ctx_sb[:, 0:1024],
                     in_=x1c[:, tt2 * 1024:tt2 * 1024 + 1024], func=ACT.Square,
                     bias=0.0, scale=1.0, accum_out=stats3[:, tt2:tt2 + 1])
            for tt2 in range(NT2):
                P.op("scalar", "activation", out=stats3[:, 4 + tt2:5 + tt2],
                     in_=stats3[:, tt2:tt2 + 1], func=ACT.Sqrt, bias=eps6_sb[:],
                     scale=1.0 / 1024)
            P.inc_last("scalar", "tsl")
            P.wait("vector", "tsl", P.mark("tsl"))
            for tt2 in range(NT2):
                P.op("vector", "reciprocal", out=stats3[:, 8 + tt2:9 + tt2],
                     in_=stats3[:, 4 + tt2:5 + tt2])
                P.op("vector", "tensor_scalar", out=htm[:, tt2 * 1024:tt2 * 1024 + 1024],
                     in0=x1c[:, tt2 * 1024:tt2 * 1024 + 1024],
                     scalar1=stats3[:, 8 + tt2:9 + tt2], scalar2=None, op0=ALU.mult)
            P.inc_last("vector", "tv")
            htm_ready = P.mark("tv")
            P.wait("tensor", "tv", htm_ready)
            for idx in range(NT2 * 8):
                tt2, kt = divmod(idx, 8)
                ptr = (p_tr3, p_tr4)[idx % 2]
                if len(trh_marks) >= 2:
                    P.wait("tensor", "tsl", trh_marks[-2])
                P.op("tensor", "transpose", ptr[:],
                     htm[:, tt2 * 1024 + kt * 128: tt2 * 1024 + kt * 128 + 128],
                     ident_b[:], inc=("tt", 1))
                P.wait("scalar", "tt", P.mark("tt"))
                P.op("scalar", "copy",
                     out=h_cm[:, kt * L + T0 + tt2 * 128: kt * L + T0 + tt2 * 128 + 128],
                     in_=ptr[:], inc=("tsl", 1))
                trh_marks.append(P.mark("tsl"))
            hch_ready = P.mark("tsl")
            a_x1_free[0] = ("tsl", hch_ready)
            P.wait("tensor", "tsl", hch_ready)
            for ct in range(8):
                pk = (p_a, p_b)[len(kv_marks) % 2]
                if len(kv_marks) >= 2:
                    P.wait("tensor", "tv", kv_marks[-2])
                for kt in range(8):
                    P.op("tensor", "matmul", pk[:],
                         wk_sb[:, kt * 1024 + ct * 128: kt * 1024 + ct * 128 + 128],
                         h_cm[:, kt * L + T0: kt * L + T0 + TC2],
                         start=(kt == 0), stop=(kt == 7))
                P.inc_last("tensor", "tt")
                P.wait("vector", "tt", P.mark("tt"))
                P.op("vector", "tensor_copy",
                     out=k_cm[:, ct * L + T0: ct * L + T0 + TC2],
                     in_=pk[:], inc=("tv", 1))
                kv_marks.append(P.mark("tv"))
            for tt in range(NT2):
                for half in range(2):
                    pv = (p_c, p_dd)[len(kv_marks) % 2]
                    if len(kv_marks) >= 2:
                        P.wait("tensor", "tv", kv_marks[-2])
                    for kt in range(8):
                        P.op("tensor", "matmul", pv[:],
                             h_cm[:, kt * L + T0 + tt * 128: kt * L + T0 + tt * 128 + 128],
                             wv_sb[:, kt * 1024 + half * 512: kt * 1024 + half * 512 + 512],
                             start=(kt == 0), stop=(kt == 7))
                    P.inc_last("tensor", "tt")
                    P.wait("vector", "tt", P.mark("tt"))
                    gt = sc * NT2 + tt
                    dstv = v_buf[:, gt * 1040 + half * 520: gt * 1040 + half * 520 + 520] \
                        .rearrange("p (h d) -> p h d", d=65)[:, :, 0:64]
                    P.op("vector", "tensor_copy", out=dstv,
                         in_=pv[:].rearrange("p (h d) -> p h d", d=64), inc=("tv", 1))
                    kv_marks.append(P.mark("tv"))
        for sc in range(4):
            T0 = sc * TC2
            den_row_marks = []
            if len(ctv_ev) >= 1:
                P.wait("tensor", "tv", ctv_ev[-1])
            for ct in range(8):
                P.wait("tensor", "tv", q_free[0])
                for kt in range(8):
                    P.op("tensor", "matmul", p_a[:],
                         wq_sb[:, kt * 1024 + ct * 128: kt * 1024 + ct * 128 + 128],
                         h_cm[:, kt * L + T0: kt * L + T0 + TC2],
                         start=(kt == 0), stop=(kt == 7))
                P.inc_last("tensor", "tt")
                P.wait("vector", "tt", P.mark("tt"))
                P.op("vector", "tensor_copy", out=q_sb[:], in_=p_a[:], inc=("tv", 1))
                q_ready = P.mark("tv")
                P.wait("tensor", "tv", q_ready)
                for hh in range(2):
                    h = ct * 2 + hh
                    pctx = (p_c, p_dd)[h % 2]
                    if len(ctx_free_marks) >= 2:
                        P.wait("tensor", "tv", ctx_free_marks[-2])
                    if len(den_row_marks) >= 2:
                        P.wait("tensor", "tsl", den_row_marks[-2])
                    elif den_prev[0] > 0 and len(den_row_marks) == 0:
                        P.wait("tensor", "tsl", den_prev[0])
                    exp_marks = []

                    def issue_score(tt):
                        psc = (p_b, p_e)[tt % 2]
                        P.op("tensor", "matmul", psc[:],
                             k_cm[hh * 64:hh * 64 + 64,
                                  ct * L + tt * 128: ct * L + tt * 128 + 128],
                             q_sb[hh * 64:hh * 64 + 64, 0:512],
                             start=True, stop=True, inc=("tt", 1))
                        P.wait("scalar", "tt", P.mark("tt"))
                        ppr = pr_sb[:, (tt % 2) * 512:(tt % 2) * 512 + 512]
                        P.op("scalar", "activation", out=ppr, in_=psc[:],
                             func=ACT.Exp, bias=0.0, scale=0.125, inc=("tsl", 1))
                        exp_marks.append(P.mark("tsl"))

                    issue_score(0)
                    for tt in range(16):
                        if tt + 1 < 16:
                            issue_score(tt + 1)
                        ppr = pr_sb[:, (tt % 2) * 512:(tt % 2) * 512 + 512]
                        if dbg and bb == 0 and sc == 0 and h == 0 and tt == 0:
                            psc = (p_b, p_e)[tt % 2]
                            P.wait("vector", "tsl", exp_marks[tt])
                            P.op("vector", "tensor_copy", out=ctx_sb[:, 3584:4096],
                                 in_=psc[:], inc=("tv", 1))
                            P.wait("gpsimd", "tv", P.mark("tv"))
                            P.wait("gpsimd", "tsl", exp_marks[tt])
                            P.dma("gpsimd", "qdbg", out=dbg_d["d_pr"], in_=ppr)
                            P.dma("gpsimd", "qdbg", out=dbg_d["d_sc2"],
                                  in_=ctx_sb[:, 3584:4096])
                            P.dma("gpsimd", "qdbg", out=dbg_d["d_qs"], in_=q_sb[:])
                            P.dma("gpsimd", "qdbg", out=dbg_d["d_kc"],
                                  in_=k_cm[:, 0:512])
                            P.wait("scalar", "qdbg")
                            P.wait("tensor", "qdbg")
                            P.wait("vector", "qdbg")
                        P.wait("tensor", "tsl", exp_marks[tt])
                        P.op("tensor", "matmul", pctx[0:65, :],
                             v_buf[:, tt * 1040 + h * 65: tt * 1040 + h * 65 + 65],
                             ppr, start=(tt == 0), stop=(tt == 15))
                    P.inc_last("tensor", "tt")
                    ctx_done = P.mark("tt")
                    P.wait("scalar", "tt", ctx_done)
                    if h % 4 == 0 and dgrp_flush[0] > 0:
                        P.wait("scalar", "qys", dgrp_flush[0])
                    P.op("scalar", "copy",
                         out=dcol[(h % 4) * 32:(h % 4) * 32 + 1, :],
                         in_=pctx[64:65, :], inc=("tsl", 1))
                    den_row_marks.append(P.mark("tsl"))
                    if h % 4 == 3:
                        P.wait("vector", "tsl", P.mark("tsl"))
                        P.op("vector", "reciprocal", out=dcol[:], in_=dcol[:])
                        P.inc_last("vector", "tv")
                        P.wait("gpsimd", "tv", P.mark("tv"))
                        if h == 3:
                            P.wait("gpsimd", "qyl")
                        for pp in range(4):
                            P.dma("gpsimd", "qys",
                                  out=bass.AP(den_d.tensor, (h - 3 + pp) * TC2,
                                              [[1, 1], [1, 512]]),
                                  in_=dcol[pp * 32:pp * 32 + 1, :])
                        dgrp_flush[0] = P.mark("qys")
                    P.wait("vector", "tt", ctx_done)
                    P.op("vector", "tensor_copy",
                         out=ctx_sb[:, ct * 512:ct * 512 + 512][hh * 64:hh * 64 + 64, :],
                         in_=pctx[0:64, :], inc=("tv", 1))
                    ctx_free_marks.append(P.mark("tv"))
            den_prev[0] = den_row_marks[-1]
            P.wait("sync", "qys")
            for ct in range(8):
                for hh in range(2):
                    h = ct * 2 + hh
                    P.dma("sync", "qyl",
                          out=den_b[hh * 64:hh * 64 + 64, :],
                          in_=bass.AP(den_d.tensor, h * TC2, [[0, 64], [1, 512]]))
                P.wait("vector", "qyl")
                P.op("vector", "tensor_mul", out=ctxn_sb[:, ct * 512:ct * 512 + 512],
                     in0=ctx_sb[:, ct * 512:ct * 512 + 512], in1=den_b[:],
                     inc=("tv", 1))
                if dbg and bb == 0 and sc == 0 and ct == 0:
                    P.wait("gpsimd", "tv", P.mark("tv"))
                    P.dma("gpsimd", "qdbg", out=dbg_d["d_ctx"], in_=ctx_sb[:, 0:512])
                    P.dma("gpsimd", "qdbg", out=dbg_d["d_den"], in_=den_b[:])
                    cv = ctxn_sb[:, 0:512]
                    P.dma("gpsimd", "qdbg", out=dbg_d["d_ctxn"], in_=cv)
                    P.wait("vector", "qdbg")
                    P.wait("sync", "qdbg")
                if ct < 7:
                    P.wait("sync", "tv", P.mark("tv"))
            ctxn_ready = P.mark("tv")
            P.wait("tensor", "tv", ctxn_ready)
            for co in range(8):
                pwo = (p_f, p_a)[co % 2]
                if len(wo_ev_marks) >= 2:
                    P.wait("tensor", "tv", wo_ev_marks[-2])
                for kt in range(8):
                    P.op("tensor", "matmul", pwo[:],
                         wo_sb[:, kt * 1024 + co * 128: kt * 1024 + co * 128 + 128],
                         ctxn_sb[:, kt * 512: kt * 512 + 512],
                         start=(kt == 0), stop=(kt == 7))
                P.inc_last("tensor", "tt")
                P.wait("vector", "tt", P.mark("tt"))
                P.op("vector", "tensor_copy", out=ctx_sb[:, co * 512:co * 512 + 512],
                     in_=pwo[:], inc=("tv", 1))
                wo_ev_marks.append(P.mark("tv"))
            ctxo_ready = P.mark("tv")
            q_free[0] = ctxo_ready
            P.wait("sync", a_x1_free[0][0], a_x1_free[0][1])
            P.dma("sync", "qx", out=x1c[:].rearrange("p (n c) -> p n c", c=1024),
                  in_=_tm(x1_d[bb], T0, NT2))
            P.wait("tensor", "tv", ctxo_ready)
            for tt2 in range(NT2):
                par2 = tt2 % 2
                ctv = ctxo_t1[:, par2 * 1024:par2 * 1024 + 1024]
                sctv = ctxo_t1[:, (1 - par2) * 1024:(1 - par2) * 1024 + 1024]
                if len(tro_marks) >= 2:
                    P.wait("tensor", "tv", tro_marks[-2])
                for kt in range(8):
                    pslice = (p_f, p_dd)[kt % 2][:, 0:128]
                    if len(ctv_ev) >= 2:
                        P.wait("tensor", "tv", ctv_ev[-2])
                    P.op("tensor", "transpose", pslice,
                         ctx_sb[:, kt * 512 + tt2 * 128: kt * 512 + tt2 * 128 + 128],
                         ident_f[:], inc=("tt", 1))
                    P.wait("vector", "tt", P.mark("tt"))
                    if kt == 0:
                        if len(x2_marks) >= 1:
                            P.wait("vector", "qx1", x2_marks[-1])
                    P.op("vector", "tensor_copy", out=ctv[:, kt * 128:kt * 128 + 128],
                         in_=pslice, inc=("tv", 1))
                    ctv_ev.append(P.mark("tv"))
                if tt2 == 0:
                    P.wait("vector", "qx")
                x1v = x1c[:, tt2 * 1024:tt2 * 1024 + 1024]
                P.op("vector", "tensor_add", out=xsbuf[:], in0=x1v, in1=ctv)
                P.op("vector", "scalar_tensor_tensor", out=sctv,
                     in0=x1v, scalar=2.0, in1=ctv, op0=ALU.mult, op1=ALU.add,
                     inc=("tv", 1))
                tro_marks.append(P.mark("tv"))
                P.wait("gpsimd", "tv", P.mark("tv"))
                P.dma("gpsimd", "qx1", out=_tm(x2_d[bb], T0 + tt2 * 128, 1),
                      in_=xsbuf[:].rearrange("p (n c) -> p n c", c=1024))
                P.dma("gpsimd", "qx1", out=_tm(ss_d[bb], T0 + tt2 * 128, 1),
                      in_=sctv.rearrange("p (n c) -> p n c", c=1024))
                x2_marks.append(P.mark("qx1"))
            x2_store_m[0] = P.mark("qx1")
            a_x1_free[0] = ("tv", P.mark("tv"))

    x2_done = P.mark("qx1")
    es3.close()

    if stop_after == "attn":
        P.wait("gpsimd", "qx1", x2_done)
        P.wait("gpsimd", "qdbg")
        return P, dr, es0

    TCF = 1024
    NTF = TCF // 128
    es4 = ExitStack()
    A4 = Alloc(nc, es4, "moe")
    wg_sb = A4.sb("wg_sb", [128, 8 * DFF], MMDT)
    wu_sb = A4.sb("wu_sb", [128, 8 * DFF], MMDT)
    wd_sb = A4.sb("wd_sb", [128, 16 * 1024], MMDT)
    h2_cm = A4.sb("h2_cm", [128, 8 * TCF], MMDT)
    act_sb = A4.sb("act_sb", [128, 16 * TCF], MMDT)
    sg_sb = A4.sb("sg_sb", [128, 2 * 512], MMDT)
    spec_sb = A4.sb("spec_sb", [128, 8 * TCF], F16)
    x2cf = A4.sb("x2cf", [128, 8 * 1024], F16)
    ce_all = A4.sb("ce_all", [128, 8 * 8], F16)
    ce_b = A4.sb("ce_b", [128, TCF], F16)
    rw_sb = A4.sb("rw_sb", [128, 8 * 8], MMDT)
    wldf = A4.sb("wldf", [128, 2 * DFF], F32)
    stats4 = A4.sb("stats4", [128, 16], F32)
    assert MMDT == F16
    h2tm = x2cf
    prb = A4.sb("prb", [128, 8], F32)
    prm = A4.sb("prm", [128, 8], F32)
    scf = A4.sb("scf", [128, 8], F32)
    moet1 = A4.sb("moet1", [128, 1024], F32)
    outc = A4.sb("outc", [128, 1024], F32)

    p_g0 = A4.ps("p_g0", [128, 512], F32)
    p_g1 = A4.ps("p_g1", [128, 512], F32)
    p_u0 = A4.ps("p_u0", [128, 512], F32)
    p_u1 = A4.ps("p_u1", [128, 512], F32)
    p_d0 = A4.ps("p_d0", [128, 512], F32)
    p_d1 = A4.ps("p_d1", [128, 512], F32)
    p_tr5 = A4.ps("p_tr5", [128, 128], MMDT)
    p_r = p_d0
    p_tr6 = A4.ps("p_tr6", [128, 128], F16)

    P.wait("sync", "qx1", x2_done)
    P.dma("sync", "qw", out=wldf[:, 0:64].rearrange("p (k e) -> p k e", e=NE),
          in_=rw_in.rearrange("(k p) e -> p k e", p=128), allow_slow_non_contiguous=True)
    P.wait("vector", "qw")
    for kt in range(8):
        P.op("vector", "tensor_scalar", out=rw_sb[:, kt * 8:kt * 8 + 8],
             in0=wldf[:, kt * 8:kt * 8 + 8], scalar1=n2w_sb[:, kt:kt + 1],
             scalar2=None, op0=ALU.mult)
    P.inc_last("vector", "tv")
    rw_ready = P.mark("tv")

    def stream_weight(wdr, wsb, nkt, ncols, fold_n2w, e_idx=None):
        for kt in range(nkt):
            par = len(wc_marks) % 2
            if len(wc_marks) >= 2:
                P.wait("sync", wc_marks[-2][0], wc_marks[-2][1])
            src2 = wdr if e_idx is None else wdr[e_idx]
            P.dma("sync", "qw", out=wldf[:, par * DFF:par * DFF + ncols],
                  in_=src2[kt * 128:kt * 128 + 128, :])
            P.wait("vector", "qw")
            if fold_n2w:
                P.op("vector", "tensor_scalar",
                     out=wsb[:, kt * ncols:kt * ncols + ncols],
                     in0=wldf[:, par * DFF:par * DFF + ncols],
                     scalar1=n2w_sb[:, kt:kt + 1], scalar2=None, op0=ALU.mult,
                     inc=("tv", 1))
            else:
                P.op("vector", "tensor_copy",
                     out=wsb[:, kt * ncols:kt * ncols + ncols],
                     in_=wldf[:, par * DFF:par * DFF + ncols], inc=("tv", 1))
            wc_marks.append(("tv", P.mark("tv")))

    wc_marks = []
    wbm = [0]
    pass2_first = [True]
    for bb in range(BPC):
        for sc in range(2):
            first_pass = (bb == 0 and sc == 0)
            T0 = sc * TCF
            P.wait("sync", f_x2_free[0][0], f_x2_free[0][1])
            P.dma("sync", "qx", out=x2cf[:].rearrange("p (n c) -> p n c", c=1024),
                  in_=_tm(x2_d[bb], T0, NTF))
            P.wait("scalar", "qx")
            for tt2 in range(NTF):
                P.op("scalar", "activation", out=act_sb[:, 0:1024],
                     in_=x2cf[:, tt2 * 1024:tt2 * 1024 + 1024], func=ACT.Square,
                     bias=0.0, scale=1.0, accum_out=stats4[:, tt2:tt2 + 1])
                P.op("scalar", "activation", out=stats4[:, 8 + tt2:9 + tt2],
                     in_=stats4[:, tt2:tt2 + 1], func=ACT.Sqrt, bias=eps6_sb[:],
                     scale=1.0 / 1024)
            P.inc_last("scalar", "tsl")
            P.wait("vector", "tsl", P.mark("tsl"))
            for tt2 in range(NTF):
                P.op("vector", "reciprocal", out=stats4[:, tt2:tt2 + 1],
                     in_=stats4[:, 8 + tt2:9 + tt2])
                P.op("vector", "tensor_scalar", out=h2tm[:, tt2 * 1024:tt2 * 1024 + 1024],
                     in0=x2cf[:, tt2 * 1024:tt2 * 1024 + 1024],
                     scalar1=stats4[:, tt2:tt2 + 1], scalar2=None, op0=ALU.mult)
            P.inc_last("vector", "tv")
            h2_ready = P.mark("tv")
            P.wait("tensor", "tv", h2_ready)
            trf_marks = []
            for idx in range(NTF * 8):
                tt2, kt = divmod(idx, 8)
                if len(trf_marks) >= 1:
                    P.wait("tensor", "tsl", trf_marks[-1])
                P.op("tensor", "transpose", p_tr5[:],
                     h2tm[:, tt2 * 1024 + kt * 128: tt2 * 1024 + kt * 128 + 128],
                     ident_b[:], inc=("tt", 1))
                P.wait("scalar", "tt", P.mark("tt"))
                P.op("scalar", "copy",
                     out=h2_cm[:, kt * TCF + tt2 * 128: kt * TCF + tt2 * 128 + 128],
                     in_=p_tr5[:], inc=("tsl", 1))
                trf_marks.append(P.mark("tsl"))
            h2cm_ready = P.mark("tsl")
            f_x2_free[0] = ("tsl", h2cm_ready)
            P.wait("tensor", "tsl", h2cm_ready)
            P.wait("tensor", "tv", rw_ready)
            for tt2 in range(NTF):
                if rt_free[0][1] > 0:
                    P.wait("tensor", rt_free[0][0], rt_free[0][1])
                    P.wait("tensor", rt_free[1][0], rt_free[1][1])
                for kt in range(8):
                    P.op("tensor", "matmul", p_r[:, 0:8],
                         h2_cm[:, kt * TCF + tt2 * 128: kt * TCF + tt2 * 128 + 128],
                         rw_sb[:, kt * 8:kt * 8 + 8],
                         start=(kt == 0), stop=(kt == 7))
                P.inc_last("tensor", "tt")
                P.wait("vector", "tt", P.mark("tt"))
                P.op("vector", "reduce_max", out=scf[:, 0:1], in_=p_r[:, 0:8], axis=AX,
                     negate=True)
                P.wait("scalar", "tv", P.mark("tv"))
                P.op("scalar", "activation", out=prb[:], in_=p_r[:, 0:8], func=ACT.Exp,
                     bias=scf[:, 0:1], scale=1.0, accum_out=scf[:, 1:2], inc=("tsl", 1))
                rt_free[0] = ("tv", P.mark("tv"))
                rt_free[1] = ("tsl", P.mark("tsl"))
                P.wait("vector", "tsl", P.mark("tsl"))
                P.op("vector", "reciprocal", out=scf[:, 2:3], in_=scf[:, 1:2])
                P.op("vector", "tensor_scalar", out=prb[:], in0=prb[:],
                     scalar1=scf[:, 2:3], scalar2=None, op0=ALU.mult)
                P.op("vector", "reduce_max", out=scf[:, 3:4], in_=prb[:], axis=AX)
                P.op("vector", "tensor_scalar", out=prm[:], in0=prb[:],
                     scalar1=scf[:, 3:4], scalar2=None, op0=ALU.is_ge)
                P.op("vector", "scalar_tensor_tensor", out=prm[:], in0=prm[:],
                     scalar=-1e30, in1=prb[:], op0=ALU.mult, op1=ALU.add)
                P.op("vector", "reduce_max", out=scf[:, 4:5], in_=prm[:], axis=AX)
                P.inc_last("vector", "tv")
                P.wait("scalar", "tv", P.mark("tv"))
                P.op("scalar", "activation", out=scf[:, 5:6], in_=scf[:, 3:4],
                     func=ACT.Exp, bias=0.0, scale=1.0)
                P.op("scalar", "activation", out=scf[:, 6:7], in_=scf[:, 4:5],
                     func=ACT.Exp, bias=0.0, scale=1.0, inc=("tsl", 1))
                P.wait("vector", "tsl", P.mark("tsl"))
                P.op("vector", "tensor_add", out=scf[:, 7:8], in0=scf[:, 5:6],
                     in1=scf[:, 6:7])
                P.op("vector", "reciprocal", out=scf[:, 7:8], in_=scf[:, 7:8])
                P.op("vector", "tensor_mul", out=scf[:, 5:6], in0=scf[:, 5:6],
                     in1=scf[:, 7:8])
                P.op("vector", "tensor_mul", out=scf[:, 6:7], in0=scf[:, 6:7],
                     in1=scf[:, 7:8])
                P.op("vector", "tensor_scalar", out=ce_all[:, tt2 * 8:tt2 * 8 + 8],
                     in0=prb[:], scalar1=scf[:, 3:4], scalar2=None, op0=ALU.is_ge)
                P.op("vector", "tensor_scalar", out=ce_all[:, tt2 * 8:tt2 * 8 + 8],
                     in0=ce_all[:, tt2 * 8:tt2 * 8 + 8], scalar1=scf[:, 5:6],
                     scalar2=None, op0=ALU.mult)
                P.op("vector", "tensor_scalar", out=prm[:], in0=prm[:],
                     scalar1=scf[:, 4:5], scalar2=None, op0=ALU.is_ge)
                P.op("vector", "tensor_scalar", out=prm[:], in0=prm[:],
                     scalar1=scf[:, 6:7], scalar2=None, op0=ALU.mult)
                P.op("vector", "tensor_add", out=ce_all[:, tt2 * 8:tt2 * 8 + 8],
                     in0=ce_all[:, tt2 * 8:tt2 * 8 + 8], in1=prm[:], inc=("tv", 1))
                if dbg and bb == 1 and sc == 1 and tt2 == 7:
                    P.wait("gpsimd", "tv", P.mark("tv"))
                    P.dma("gpsimd", "qdbg", out=bass.AP(dbg_d["d_rt"].tensor, 0,
                                                        [[64, 128], [1, 8]]),
                          in_=prb[:])
                    P.dma("gpsimd", "qdbg", out=bass.AP(dbg_d["d_rt"].tensor, 8,
                                                        [[64, 128], [1, 8]]),
                          in_=prm[:])
                    P.dma("gpsimd", "qdbg", out=bass.AP(dbg_d["d_rt"].tensor, 16,
                                                        [[64, 128], [1, 8]]),
                          in_=scf[:, 0:8])
                    P.dma("gpsimd", "qdbg", out=bass.AP(dbg_d["d_rt"].tensor, 24,
                                                        [[64, 128], [1, 8]]),
                          in_=ce_all[:, tt2 * 8:tt2 * 8 + 8])
                    P.wait("vector", "qdbg")
                    P.wait("scalar", "qdbg")
            P.wait("gpsimd", "tv", P.mark("tv"))
            P.wait("gpsimd", "qyl")
            for e in range(NE):
                P.dma("gpsimd", "qys",
                      out=bass.AP(ce_d.tensor, e * 2048 + T0, [[1, 128], [128, NTF]]),
                      in_=ce_all[:].rearrange("p (t e) -> p t e", e=NE)[:, :, e],
                      allow_slow_non_contiguous=True)
            P.wait("sync", "qys")
            P.wait("vector", "qo", out_store_m[0])
            P.op("vector", "memset", spec_sb[:], 0.0, inc=("tv", 1))
            for e in range(NE + 1):
                shared = (e == NE)
                e9 = NE if shared else e
                P.wait("sync", "tt", wgu_free[0])
                if first_pass:
                    if wbm[0] > 0:
                        P.wait("vector", "qg", wbm[0])
                    if shared:
                        stream_weight(shg_in, wg_sb, 8, DFF, True)
                        stream_weight(shu_in, wu_sb, 8, DFF, True)
                    else:
                        stream_weight(eg_in, wg_sb, 8, DFF, True, e)
                        stream_weight(eu_in, wu_sb, 8, DFF, True, e)
                    gu_ready = ("tv", P.mark("tv"))
                else:
                    if pass2_first[0]:
                        P.wait("sync", "qg")
                        pass2_first[0] = False
                    P.dma("sync", "qw", out=wg_sb[:], in_=wfg_d[e9])
                    P.dma("sync", "qw", out=wu_sb[:], in_=wfu_d[e9])
                    gu_ready = ("qw", P.mark("qw"))
                if not shared:
                    P.wait("sync", "tv", ceb_free[0])
                    P.dma("sync", "qyl", out=ce_b[:],
                          in_=bass.AP(ce_d.tensor, e * 2048 + T0, [[0, 128], [1, TCF]]))
                if first_pass:
                    P.wait("gpsimd", "tv", gu_ready[1])
                    P.dma("gpsimd", "qg", out=wfg_d[e9], in_=wg_sb[:])
                    P.dma("gpsimd", "qg", out=wfu_d[e9], in_=wu_sb[:])
                    wbm[0] = P.mark("qg")
                P.wait("tensor", gu_ready[0], gu_ready[1])
                for ft in range(16):
                    for tn in range(TCF // 512):
                        sl = (ft * 2 + tn) % 2
                        pg = (p_g0, p_g1)[sl]
                        pu = (p_u0, p_u1)[sl]
                        if len(act_marks) >= 2:
                            P.wait("tensor", "tv", act_marks[-2])
                        for kt in range(8):
                            P.op("tensor", "matmul", pg[:],
                                 wg_sb[:, kt * DFF + ft * 128: kt * DFF + ft * 128 + 128],
                                 h2_cm[:, kt * TCF + tn * 512: kt * TCF + tn * 512 + 512],
                                 start=(kt == 0), stop=(kt == 7))
                        for kt in range(8):
                            P.op("tensor", "matmul", pu[:],
                                 wu_sb[:, kt * DFF + ft * 128: kt * DFF + ft * 128 + 128],
                                 h2_cm[:, kt * TCF + tn * 512: kt * TCF + tn * 512 + 512],
                                 start=(kt == 0), stop=(kt == 7))
                        P.inc_last("tensor", "tt")
                        P.wait("scalar", "tt", P.mark("tt"))
                        sgv = sg_sb[:, tn * 512:tn * 512 + 512]
                        P.op("scalar", "activation", out=sgv, in_=pg[:], func=ACT.Silu,
                             bias=0.0, scale=1.0, inc=("tsl", 1))
                        P.wait("vector", "tsl", P.mark("tsl"))
                        dst = act_sb[:, ft * TCF + tn * 512: ft * TCF + tn * 512 + 512]
                        P.op("vector", "tensor_mul", out=dst, in0=sgv, in1=pu[:])
                        if not shared:
                            P.wait("vector", "qyl")
                            P.op("vector", "tensor_mul", out=dst, in0=dst,
                                 in1=ce_b[:, tn * 512:tn * 512 + 512])
                        P.inc_last("vector", "tv")
                        act_marks.append(P.mark("tv"))
                act_ready = P.mark("tv")
                ceb_free[0] = act_ready
                wgu_free[0] = P.mark("tt")
                P.wait("sync", "tt", wd_free[0])
                if first_pass:
                    if wbm[0] > 0:
                        P.wait("vector", "qg", wbm[0])
                    stream_weight(shd_in if shared else ed_in, wd_sb, 16, 1024, False,
                                  None if shared else e)
                    wd_ready = ("tv", P.mark("tv"))
                    P.wait("gpsimd", "tv", wd_ready[1])
                    P.dma("gpsimd", "qg", out=wfd_d[e9], in_=wd_sb[:])
                    wbm[0] = P.mark("qg")
                else:
                    P.dma("sync", "qw", out=wd_sb[:], in_=wfd_d[e9])
                    wd_ready = ("qw", P.mark("qw"))
                P.wait("tensor", wd_ready[0], wd_ready[1])
                P.wait("tensor", "tv", act_ready)
                for co in range(8):
                    for tn in range(TCF // 512):
                        pd = (p_d0, p_d1)[(co * 2 + tn) % 2]
                        if len(dn_marks) >= 2:
                            P.wait("tensor", "tv", dn_marks[-2])
                        for ft in range(16):
                            P.op("tensor", "matmul", pd[:],
                                 wd_sb[:, ft * 1024 + co * 128: ft * 1024 + co * 128 + 128],
                                 act_sb[:, ft * TCF + tn * 512: ft * TCF + tn * 512 + 512],
                                 start=(ft == 0), stop=(ft == 15))
                        P.inc_last("tensor", "tt")
                        P.wait("vector", "tt", P.mark("tt"))
                        dsts = spec_sb[:, co * TCF + tn * 512: co * TCF + tn * 512 + 512]
                        P.op("vector", "tensor_add", out=dsts, in0=dsts, in1=pd[:],
                             inc=("tv", 1))
                        dn_marks.append(P.mark("tv"))
                wd_free[0] = P.mark("tt")
            spec_ready = P.mark("tv")
            P.wait("sync", f_x2_free[0][0], f_x2_free[0][1])
            P.dma("sync", "qx", out=x2cf[:].rearrange("p (n c) -> p n c", c=1024),
                  in_=_tm(ss_d[bb], T0, NTF))
            P.wait("tensor", "tv", spec_ready)
            for tt2 in range(NTF):
                if moet_free[0] > 0:
                    P.wait("tensor", "tv", moet_free[0])
                for kt in range(8):
                    P.op("tensor", "transpose", p_tr6[:],
                         spec_sb[:, kt * TCF + tt2 * 128: kt * TCF + tt2 * 128 + 128],
                         ident_h[:], inc=("tt", 1))
                    P.wait("vector", "tt", P.mark("tt"))
                    P.op("vector", "tensor_copy", out=moet1[:, kt * 128:kt * 128 + 128],
                         in_=p_tr6[:], inc=("tv", 1))
                    P.wait("tensor", "tv", P.mark("tv"))
                if tt2 == 0:
                    P.wait("vector", "qx")
                P.op("vector", "tensor_add", out=moet1[:], in0=moet1[:],
                     in1=x2cf[:, tt2 * 1024:tt2 * 1024 + 1024], inc=("tv", 1))
                moet_free[0] = P.mark("tv")
                f_x2_free[0] = ("tv", P.mark("tv"))
                P.wait("scalar", "tv", P.mark("tv"))
                P.op("scalar", "activation", out=act_sb[:, 0:1024], in_=moet1[:],
                     func=ACT.Copy, bias=0.0, scale=1.0, accum_out=stats4[:, 0:1],
                     inc=("tsl", 1))
                P.wait("vector", "tsl", P.mark("tsl"))
                P.op("vector", "tensor_scalar", out=stats4[:, 1:2], in0=stats4[:, 0:1],
                     scalar1=1.0 / 1024, scalar2=None, op0=ALU.mult)
                P.op("vector", "tensor_scalar", out=moet1[:], in0=moet1[:],
                     scalar1=stats4[:, 1:2], scalar2=None, op0=ALU.subtract,
                     inc=("tv", 1))
                P.wait("scalar", "tv", P.mark("tv"))
                P.op("scalar", "activation", out=act_sb[:, 0:1024], in_=moet1[:],
                     func=ACT.Square, bias=0.0, scale=1.0, accum_out=stats4[:, 2:3])
                P.op("scalar", "activation", out=stats4[:, 3:4], in_=stats4[:, 2:3],
                     func=ACT.Sqrt, bias=eps5_sb[:], scale=1.0 / 1024, inc=("tsl", 1))
                P.wait("vector", "tsl", P.mark("tsl"))
                P.op("vector", "reciprocal", out=stats4[:, 4:5], in_=stats4[:, 3:4])
                ov = outc[:, 0:1024]
                if len(out_t_marks) >= 1:
                    P.wait("vector", "qo", out_t_marks[-1])
                P.op("vector", "tensor_scalar", out=ov, in0=moet1[:],
                     scalar1=stats4[:, 4:5], scalar2=None, op0=ALU.mult,
                     inc=("tv", 1))
                P.wait("gpsimd", "tv", P.mark("tv"))
                P.dma("gpsimd", "qo", out=_tm(out_d[bb], T0 + tt2 * 128, 1),
                      in_=ov.rearrange("p (n c) -> p n c", c=1024))
                out_t_marks.append(P.mark("qo"))
            out_store_m[0] = P.mark("qo")

    es4.close()
    P.wait("gpsimd", "qo", P.mark("qo"))
    P.wait("gpsimd", "qdbg")
    return P, dr, es0


def host_inputs(inputs):
    x = np.ascontiguousarray(inputs["L0_attn_x"])
    maps = []
    w_in_t = np.ascontiguousarray(inputs["L0_attn_W_in"].T)
    w_x_t = np.ascontiguousarray(inputs["L0_attn_W_x"].T)
    w_dt_t = np.ascontiguousarray(inputs["L0_attn_W_dt"].T)
    w_out_t = np.ascontiguousarray(inputs["L0_attn_W_out"].T)
    convw = np.ascontiguousarray(inputs["L0_attn_conv_w"][:, 0, :])
    ident = np.eye(128, dtype=np.float32)
    common = dict(
        w_in_t=w_in_t, w_x_t=w_x_t, w_dt_t=w_dt_t, w_out_t=w_out_t,
        gamma=inputs["L0_attn_gamma"], step=inputs["L0_attn_step_size"],
        convw=convw, convb=inputs["L0_attn_conv_b"], bdt=inputs["L0_attn_b_dt"],
        ident=ident,
        n1w=inputs["L0_ffn_norm1_w"], n2w=inputs["L0_ffn_norm2_w"],
        wq=inputs["L0_ffn_wq"], wk=inputs["L0_ffn_wk"], wv=inputs["L0_ffn_wv"],
        wo=inputs["L0_ffn_wo"], rw=inputs["L0_ffn_router_w"],
        shg=inputs["L0_ffn_sh_gate"], shu=inputs["L0_ffn_sh_up"],
        shd=inputs["L0_ffn_sh_down"], eg=inputs["L0_ffn_e_gate"],
        eu=inputs["L0_ffn_e_up"], ed=inputs["L0_ffn_e_down"],
    )
    for c in range(NCORES):
        m = dict(common)
        m["x"] = np.ascontiguousarray(x[c * BPC:(c + 1) * BPC])
        maps.append(m)
    return maps


def build_nc(dbg=False, stop_after=None):
    nc = bass.Bass("TRN2", target_bir_lowering=False, detect_race_conditions=False)
    P, dr, es0 = build_core(nc, dbg=dbg, stop_after=stop_after)
    with nc.Block() as block:
        P.build(block)
    print("instr counts:", P.nops())
    return nc


def kernel(**inputs):
    nc = build_nc(dbg=False)
    maps = host_inputs(inputs)
    res = run_bass_kernel_spmd(nc, maps, core_ids=list(range(NCORES)))
    return np.concatenate([np.asarray(r["out"]) for r in res.results], axis=0)


# version 05
# speedup vs baseline: 1.1552x; 1.0275x over previous
import numpy as np
import ml_dtypes
import concourse.bass as bass
import concourse.mybir as mybir
from concourse.bass_utils import run_bass_kernel_spmd
from contextlib import ExitStack

F32 = mybir.dt.float32
F16 = mybir.dt.float16
BF16 = mybir.dt.bfloat16
ALU = mybir.AluOpType
ACT = mybir.ActivationFunctionType
AX = mybir.AxisListType.X

NCORES = 8
DBG_CHUNK = -1
PARANOID = False
B, L, D = 16, 2048, 1024
BPC = B // NCORES
DI = 2048
DTR = 64
NH, DH = 16, 64
NE, DFF = 8, 2048
TC1 = 256
TC2 = 512
MMDT = F16

ENGINES = ("sync", "vector", "scalar", "tensor", "gpsimd")
DTB = {F32: 4, F16: 2, BF16: 2, mybir.dt.int32: 4}


class Prog:
    def __init__(self, nc, es):
        self.nc = nc
        self.es = es
        self.items = {e: [] for e in ENGINES}
        self.cnt = {}
        self.sems = {}

    def sem(self, name):
        if name not in self.sems:
            self.sems[name] = self.es.enter_context(self.nc.semaphore(name))
            self.cnt[name] = 0
        return name

    HAZARD_OPS = ("reduce_sum", "reduce_max", "tensor_reduce",
                  "tensor_tensor_scan", "reciprocal", "bn_stats", "bn_aggr")
    ENG_SEM = {"vector": "tv", "gpsimd": "tg", "scalar": "tsl"}

    def _small_out(self, meth, a, kw):
        if kw.get("accum_out") is not None:
            return True
        obj = kw.get("out")
        if obj is None and a:
            obj = a[0]
        shp = getattr(obj, "shape", None)
        if not shp or len(shp) < 2:
            return False
        free = 1
        for d in list(shp)[1:]:
            free *= int(d)
        return free < 64

    def op(self, eng, meth, *a, inc=None, **kw):
        hazard = eng in ("vector", "gpsimd", "scalar") and (
            meth in self.HAZARD_OPS or self._small_out(meth, a, kw))
        if hazard and inc is None:
            inc = (self.ENG_SEM[eng], 1)
        if inc is not None:
            self.cnt[inc[0]] += inc[1]
            semname = inc[0]
            inc = (self.sems[inc[0]], inc[1])
        self.items[eng].append((meth, a, kw, inc))
        if hazard:
            self.items[eng].append(("wait_ge", (inc[0], self.cnt[semname]), {}, None))

    def inc_last(self, eng, semname, n=1):
        idx = len(self.items[eng]) - 1
        while idx >= 0 and self.items[eng][idx][0] == "wait_ge":
            idx -= 1
        meth, a, kw, inc = self.items[eng][idx]
        if inc is not None:
            assert inc[0] is self.sems[semname], f"{meth} inc on other sem"
            return
        self.cnt[semname] += n
        self.items[eng][idx] = (meth, a, kw, (self.sems[semname], n))

    def mark(self, semname):
        return self.cnt[semname]

    def ensure_inc(self, eng, semname):
        self.inc_last(eng, semname)

    def wait(self, eng, semname, val=None):
        v = self.cnt[semname] if val is None else val
        if v > 0:
            self.items[eng].append(("wait_ge", (self.sems[semname], v), {}, None))

    def dma(self, eng, qsem, out, in_, **kw):
        self.op(eng, "dma_start", inc=(qsem, 16), out=out, in_=in_, **kw)

    def build(self, block):
        for eng in ENGINES:
            if not self.items[eng]:
                continue
            items = self.items[eng]

            def body(h, items=items):
                for meth, a, kw, inc in items:
                    r = getattr(h, meth)(*a, **kw)
                    if inc is not None:
                        r.then_inc(inc[0], inc[1])
            getattr(block, eng)(body)

    def nops(self):
        return {e: len(v) for e, v in self.items.items()}

    def fence(self, engines=ENGINES, sems=("tv", "tt", "tsl", "qs", "qg")):
        for eng in engines:
            if not self.items[eng]:
                continue
            meth, a, kw, inc = self.items[eng][-1]
            if inc is None and meth != "wait_ge":
                sm = {"sync": "qs", "gpsimd": "tg", "vector": "tv",
                      "scalar": "tsl", "tensor": "tt"}[eng]
                if meth == "dma_start":
                    sm = "qs" if eng == "sync" else "qg"
                    self.inc_last(eng, sm, 16)
                else:
                    self.inc_last(eng, sm)
        vals = {s: self.cnt[s] for s in list(sems) + ["tg"]}
        for eng in engines:
            for s, v in vals.items():
                if v > 0:
                    self.items[eng].append(("wait_ge", (self.sems[s], v), {}, None))


class Alloc:
    def __init__(self, nc, es, label=""):
        self.nc = nc
        self.es = es
        self.used = 0
        self.label = label

    def sb(self, name, shape, dtype):
        t = self.es.enter_context(self.nc.sbuf_tensor(name, shape, dtype))
        self.used += int(np.prod(shape[1:])) * DTB[dtype]
        return t

    def ps(self, name, shape, dtype=F32):
        return self.es.enter_context(self.nc.psum_tensor(name, shape, dtype))


def _tm(dram2d, t0, nt, w=D):
    return dram2d[t0:t0 + nt * 128, :].rearrange("(n p) c -> p n c", p=128)


def build_core(nc, dbg=False, stop_after=None):
    dr = {}

    def din(name, shape, dtype=F32):
        dr[name] = nc.dram_tensor(name, list(shape), dtype, kind="ExternalInput").ap()
        return dr[name]

    x_in = din("x", (BPC, L, D))
    w_in_t = din("w_in_t", (D, 2 * DI))
    w_x_t = din("w_x_t", (DI, 96))
    w_dt_t = din("w_dt_t", (DTR, DI))
    w_out_t = din("w_out_t", (DI, D))
    gamma_in = din("gamma", (D,))
    step_in = din("step", (1,))
    convw_in = din("convw", (DI, 4))
    convb_in = din("convb", (DI,))
    bdt_in = din("bdt", (DI,))
    ident_in = din("ident", (128, 128))
    n1w_in = din("n1w", (D,))
    n2w_in = din("n2w", (D,))
    wq_in = din("wq", (D, D)); wk_in = din("wk", (D, D))
    wv_in = din("wv", (D, D)); wo_in = din("wo", (D, D))
    rw_in = din("rw", (D, NE))
    shg_in = din("shg", (D, DFF)); shu_in = din("shu", (D, DFF))
    shd_in = din("shd", (DFF, D))
    eg_in = din("eg", (NE, D, DFF)); eu_in = din("eu", (NE, D, DFF))
    ed_in = din("ed", (NE, DFF, D))

    out_d = nc.dram_tensor("out", [BPC, L, D], F32, kind="ExternalOutput").ap()
    sk = "ExternalOutput" if dbg else "Internal"
    ys_d = nc.dram_tensor("ys", [BPC, DI, L], MMDT, kind=sk).ap()
    dbg_d = {}
    if dbg:
        for nm, sh, dt_ in [("d_xn", [128, 2048], MMDT), ("d_xc", [128, 4096], F32),
                            ("d_z", [128, 4096], F32), ("d_cv", [128, 4096], F32),
                            ("d_u", [128, 4096], MMDT), ("d_spz", [128, 4096], MMDT),
                            ("d_dec", [128, 4096], F32), ("d_h", [128, 4096], F32),
                            ("d_spy", [128, 4096], F32), ("d_spr", [128, 4096], F32),
                            ("d_sq", [128, 4096], F32), ("d_sqin", [128, 4096], F32),
                            ("d_wq", [128, 512], MMDT), ("d_smc", [128, 16], F32),
                            ("d_rt", [128, 64], F32), ("d_h2", [128, 128], MMDT),
                            ("d_ctx", [128, 512], F32), ("d_den", [128, 512], F32),
                            ("d_pr", [128, 512], F32), ("d_ctxn", [128, 512], F32),
                            ("d_sc2", [128, 512], F32), ("d_qs", [128, 512], MMDT),
                            ("d_kc", [128, 512], MMDT),
                            ("d_xncm", [128, 2048], MMDT)]:
            dbg_d[nm] = nc.dram_tensor(nm, sh, dt_, kind="ExternalOutput").ap()
    x1_d = nc.dram_tensor("x1s", [BPC, L, D], F16, kind=sk).ap()
    x2_d = nc.dram_tensor("x2s", [BPC, L, D], F16, kind=sk).ap()
    ss_d = nc.dram_tensor("sss", [BPC, L, D], F16, kind=sk).ap()
    den_d = nc.dram_tensor("dens", [NH, TC2], F32, kind="Internal").ap()
    ce_d = nc.dram_tensor("ces", [NE, 2048], F16, kind="Internal").ap()
    wfg_d = nc.dram_tensor("wfg", [NE + 1, 128, 8 * DFF], F16, kind="Internal").ap()
    wfu_d = nc.dram_tensor("wfu", [NE + 1, 128, 8 * DFF], F16, kind="Internal").ap()
    wfd_d = nc.dram_tensor("wfd", [NE + 1, 128, 16 * 1024], F16, kind="Internal").ap()

    es0 = ExitStack()
    nc._mk_es0 = es0
    P = Prog(nc, es0)
    A0 = Alloc(nc, es0)

    for s in ("qs", "qg", "tv", "tt", "tsl", "tg", "qx", "qw", "qyl",
              "qys", "qx1", "qo", "qdbg"):
        P.sem(s)

    ident_b = A0.sb("ident_b", [128, 128], MMDT)
    ident_f = A0.sb("ident_f", [128, 128], F32)
    gamma_sb = A0.sb("gamma_sb", [128, 8], F32)
    n1w_sb = A0.sb("n1w_sb", [128, 8], F32)
    n2w_sb = A0.sb("n2w_sb", [128, 8], F32)
    convw_sb = A0.sb("convw_sb", [128, 64], F32)
    convb_sb = A0.sb("convb_sb", [128, 16], F32)
    bdt_sb = A0.sb("bdt_sb", [128, 16], F32)
    step_sb = A0.sb("step_sb", [1, 1], F32)
    ones_sb = A0.sb("ones_sb", [128, 1], F32)
    ones_mat = A0.sb("ones_mat", [128, 128], MMDT)
    stepv = A0.sb("stepv", [128, 1], F32)
    stepv16 = A0.sb("stepv16", [128, 1], MMDT)
    acc16 = A0.sb("acc16", [128, 1], MMDT)
    ones_b = A0.sb("ones_b", [128, 1], MMDT)
    flag_sb = A0.sb("flag_sb", [128, 1], F32)
    eps5_sb = A0.sb("eps5_sb", [128, 1], F32)
    eps6_sb = A0.sb("eps6_sb", [128, 1], F32)
    smc = A0.sb("smc", [128, 16], F32)
    ident_h = A0.sb("ident_h", [128, 128], F16)

    P.dma("sync", "qs", out=ident_f[:], in_=ident_in)
    P.dma("sync", "qs", out=gamma_sb[:], in_=gamma_in.rearrange("(k p) -> p k", p=128), allow_slow_non_contiguous=True)
    P.dma("sync", "qs", out=n1w_sb[:], in_=n1w_in.rearrange("(k p) -> p k", p=128), allow_slow_non_contiguous=True)
    P.dma("sync", "qs", out=n2w_sb[:], in_=n2w_in.rearrange("(k p) -> p k", p=128), allow_slow_non_contiguous=True)
    P.dma("sync", "qs", out=convw_sb[:].rearrange("p (c k) -> p c k", k=4),
          in_=convw_in.rearrange("(c p) k -> p c k", p=128), allow_slow_non_contiguous=True)
    P.dma("sync", "qs", out=convb_sb[:], in_=convb_in.rearrange("(c p) -> p c", p=128), allow_slow_non_contiguous=True)
    P.dma("sync", "qs", out=bdt_sb[:], in_=bdt_in.rearrange("(c p) -> p c", p=128), allow_slow_non_contiguous=True)
    P.dma("sync", "qs", out=step_sb[:, 0:1], in_=step_in.rearrange("(a o) -> a o", a=1))
    ld0 = P.mark("qs")

    P.wait("vector", "qs", ld0)
    P.op("vector", "memset", ones_sb[:], 1.0)
    P.op("vector", "memset", ones_mat[:], 1.0)
    P.op("vector", "memset", smc[:], 0.0)
    P.op("vector", "memset", ones_b[:], 1.0)
    P.op("vector", "memset", eps5_sb[:], 1e-5)
    P.op("vector", "memset", eps6_sb[:], 1e-6)
    P.op("vector", "tensor_copy", out=ident_b[:], in_=ident_f[:])
    P.op("vector", "tensor_copy", out=ident_h[:], in_=ident_f[:])
    P.op("vector", "tensor_scalar", out=flag_sb[:], in0=ones_sb[:],
         scalar1=2.0 ** -30, scalar2=1.0, op0=ALU.add, op1=ALU.subtract)
    P.op("vector", "tensor_scalar", out=flag_sb[:], in0=flag_sb[:],
         scalar1=0.0, scalar2=None, op0=ALU.is_gt, inc=("tv", 1))
    P.op("vector", "memset", stepv[:], 0.0)
    P.op("vector", "tensor_copy", out=stepv[0:1, 0:1], in_=step_sb[:, 0:1])
    P.op("vector", "tensor_copy", out=stepv16[:], in_=stepv[:], inc=("tv", 1))

    def quantize(w_dram, K, M, wq_sb, fold_const, fold_gamma, wld, wmask, ps_sm, scol):
        nkt = (K + 127) // 128
        kp = min(K, 128)
        acc = smc[:, scol:scol + 1]
        red = smc[:, scol + 1:scol + 2]
        for kt in range(nkt):
            P.wait("sync", "tv", P.mark("tv"))
            P.dma("sync", "qw", out=wld[:kp, :M], in_=w_dram[kt * 128:kt * 128 + kp, :])
            P.wait("vector", "qw")
            P.op("vector", "reduce_sum", out=red[:kp, :], in_=wld[:kp, :M], axis=AX,
                 apply_absolute_value=True)
            if kt == 0:
                P.op("vector", "memset", acc[:], 0.0)
            P.op("vector", "tensor_add", out=acc[:kp, :], in0=acc[:kp, :],
                 in1=red[:kp, :], inc=("tv", 1))
        P.op("vector", "tensor_copy", out=acc16[:], in_=acc[:], inc=("tv", 1))
        P.wait("tensor", "tv", P.mark("tv"))
        P.op("tensor", "matmul", ps_sm[:, 0:1], ones_mat[:], acc16[:],
             start=True, stop=True, inc=("tt", 1))
        P.wait("vector", "tt", P.mark("tt"))
        thr = smc[:, scol + 2:scol + 3]
        fold = smc[:, scol + 3:scol + 4]
        nthr = smc[:, scol + 4:scol + 5]
        P.op("vector", "tensor_scalar", out=thr, in0=ps_sm[:, 0:1],
             scalar1=1.0 / (K * M), scalar2=1e-8, op0=ALU.mult, op1=ALU.add)
        P.op("vector", "tensor_scalar", out=fold, in0=thr,
             scalar1=fold_const, scalar2=None, op0=ALU.mult)
        if fold_gamma:
            P.op("vector", "tensor_scalar", out=fold, in0=fold,
                 scalar1=smc[:, 15:16], scalar2=None, op0=ALU.mult)
        P.op("vector", "tensor_scalar", out=thr, in0=thr,
             scalar1=0.5, scalar2=None, op0=ALU.mult)
        P.op("vector", "tensor_scalar", out=nthr, in0=thr, scalar1=-1.0, scalar2=None,
             op0=ALU.mult, inc=("tv", 1))
        for kt in range(nkt):
            P.wait("sync", "tv", P.mark("tv"))
            P.dma("sync", "qw", out=wld[:kp, :M], in_=w_dram[kt * 128:kt * 128 + kp, :])
            P.wait("vector", "qw")
            P.op("vector", "tensor_scalar", out=wmask[:kp, :M], in0=wld[:kp, :M],
                 scalar1=thr[:kp, :], scalar2=None, op0=ALU.is_gt)
            P.op("vector", "scalar_tensor_tensor", out=wmask[:kp, :M],
                 in0=wld[:kp, :M], scalar=nthr[:kp, :], in1=wmask[:kp, :M],
                 op0=ALU.is_lt, op1=ALU.subtract)
            if fold_gamma:
                gsc = smc[:, scol + 5:scol + 6]
                P.op("vector", "tensor_scalar", out=gsc, in0=gamma_sb[:, kt:kt + 1],
                     scalar1=fold, scalar2=-1.0, op0=ALU.mult, op1=ALU.mult)
                P.op("vector", "tensor_scalar", out=wq_sb[:kp, kt * M:kt * M + M],
                     in0=wmask[:kp, :M], scalar1=gsc[:kp, :], scalar2=None,
                     op0=ALU.mult, inc=("tv", 1))
            else:
                P.op("vector", "tensor_scalar", out=wq_sb[:kp, kt * M:kt * M + M],
                     in0=wmask[:kp, :M], scalar1=fold[:kp, :], scalar2=-1.0,
                     op0=ALU.mult, op1=ALU.mult, inc=("tv", 1))

    es1 = ExitStack()
    A1 = Alloc(nc, es1, "m")
    wqin = A1.sb("wqin", [128, 8 * 4096], MMDT)
    wqx = A1.sb("wqx", [128, 16 * 96], MMDT)
    wqdt = A1.sb("wqdt", [128, DI], MMDT)
    x0b = A1.sb("x0b", [128, 2 * 2048], F32)
    y32 = A1.sb("y32", [128, 16 * TC1], F32)
    r16 = A1.sb("r16", [128, 16 * TC1], BF16)
    t16 = A1.sb("t16", [128, 16 * TC1], BF16)
    stats = A1.sb("stats", [128, 16], F32)
    carry = A1.sb("carry", [128, 16], F32)
    dump = A1.sb("dump", [128, 1024], F32)
    ps_sm = A1.ps("ps_sm", [128, 16], F32)

    es1a = ExitStack()
    A1a = Alloc(nc, es1a, "m1only")
    xc0 = A1a.sb("xc0", [128, 2048], F32)
    xn_tm = A1a.sb("xn_tm", [128, 2048], MMDT)
    xn_cm = A1a.sb("xn_cm", [128, 8 * TC1], MMDT)
    xc_sb = A1a.sb("xc_sb", [128, 16 * TC1], F32)
    z_sb = A1a.sb("z_sb", [128, 16 * TC1], F32)
    spz = A1a.sb("spz", [128, 16 * TC1], MMDT)
    u16 = A1a.sb("u16", [128, 16 * TC1], MMDT)
    b16 = A1a.sb("b16", [128, 16 * TC1], F16)
    dbl_sb = A1a.sb("dbl_sb", [128, TC1], MMDT)
    hal = A1a.sb("hal", [128, 48], F32)
    ps_xz = [A1a.ps(f"ps_xz{i}", [128, TC1], F32) for i in range(2)]
    ps_tr = [A1a.ps(f"ps_tr{i}", [128, 128], MMDT) for i in range(2)]
    ps_dt = [A1a.ps(f"ps_dt{i}", [128, TC1], F32) for i in range(2)]
    dt_sb = xc_sb
    h_sb = y32

    P.wait("tensor", "tv", P.mark("tv"))
    P.op("tensor", "matmul", ps_sm[:, 0:1], ones_mat[:], stepv16[:],
         start=True, stop=True, inc=("tt", 1))
    P.wait("vector", "tt", P.mark("tt"))
    P.op("vector", "tensor_copy", out=smc[:, 15:16], in_=ps_sm[:, 0:1],
         inc=("tv", 1))

    wcast_marks = []
    a_x1_free = [("tv", 0)]
    sc_free_marks = []
    ctx_free_marks = []
    wo_ev_marks = []
    x2_store_m = [0]
    x2_marks = []
    tro_marks = []
    ctv_ev = []
    q_free = [0]
    den_prev = [0]
    dgrp_flush = [0]
    tro_free = [0]
    f_x2_free = [("tv", 0)]
    wgu_free = [0]
    wd_free = [0]
    ceb_free = [0]
    act_marks = []
    dn_marks = []
    out_store_m = [0]
    out_t_marks = []
    moet_free = [0]
    rt_free = [("tv", 0), ("tsl", 0)]
    quantize(w_in_t, D, 2 * DI, wqin, 1.0, True, y32[:, 0:4096], r16[:, 0:4096],
             ps_sm, 0)
    if dbg:
        P.ensure_inc("vector", "tv")
        P.wait("gpsimd", "tv", P.mark("tv"))
        P.dma("gpsimd", "qdbg", out=dbg_d["d_wq"], in_=wqin[:, 0:512])
        P.dma("gpsimd", "qdbg", out=dbg_d["d_smc"], in_=smc[:])
        P.wait("vector", "qdbg")
    quantize(w_x_t, DI, 96, wqx, 0.5, False, y32[:, 0:96], r16[:, 0:96], ps_sm, 6)
    quantize(w_dt_t, DTR, DI, wqdt, 1.0, False, y32[:, 0:DI], r16[:, 0:DI], ps_sm, 0)

    def sp_emit(v_in, out_f16, W, dump_init=False):
        y = y32[:, 0:W]; r = r16[:, 0:W]; t = t16[:, 0:W]
        y16 = out_f16.bitcast(BF16)
        P.ensure_inc("vector", "tv")
        P.wait("scalar", "tv", P.mark("tv"))
        P.op("scalar", "activation", out=y, in_=v_in, func=ACT.Square,
             bias=0.0, scale=1.0, inc=("tsl", 1))
        P.wait("vector", "tsl", P.mark("tsl"))
        if dump_init:
            P.wait("gpsimd", "tsl", P.mark("tsl"))
            P.dma("gpsimd", "qdbg", out=dbg_d["d_sq"], in_=y)
            P.dma("gpsimd", "qdbg", out=dbg_d["d_sqin"], in_=v_in)
            P.wait("vector", "qdbg")
            P.wait("scalar", "qdbg")
        P.op("vector", "tensor_scalar", out=y, in0=y, scalar1=4.0, scalar2=1e6,
             op0=ALU.add, op1=ALU.min)
        P.op("vector", "tensor_copy", out=y16, in_=y)
        P.op("vector", "tensor_scalar", out=r, in0=y, scalar1=16.0, scalar2=-0.375,
             op0=ALU.is_gt, op1=ALU.mult)
        P.op("vector", "tensor_scalar", out=t, in0=y, scalar1=64.0, scalar2=-0.0625,
             op0=ALU.is_gt, op1=ALU.mult)
        P.op("vector", "tensor_add", out=r, in0=r, in1=t)
        P.op("vector", "tensor_scalar", out=t, in0=y, scalar1=256.0, scalar2=-0.03125,
             op0=ALU.is_gt, op1=ALU.mult)
        P.op("vector", "tensor_add", out=r, in0=r, in1=t)
        P.op("vector", "tensor_scalar", out=r, in0=r, scalar1=0.5, scalar2=None,
             op0=ALU.add)
        if dump_init:
            P.inc_last("vector", "tv")
            P.wait("gpsimd", "tv", P.mark("tv"))
            P.dma("gpsimd", "qdbg", out=dbg_d["d_spy"], in_=y)
            P.dma("gpsimd", "qdbg", out=dbg_d["d_spr"], in_=r)
            P.wait("vector", "qdbg")
        for _ in range(3):
            P.op("vector", "tensor_mul", out=t, in0=r, in1=r)
            P.op("vector", "tensor_mul", out=t, in0=t, in1=y16)
            P.op("vector", "tensor_scalar", out=t, in0=t, scalar1=-0.5, scalar2=1.5,
                 op0=ALU.mult, op1=ALU.add)
            P.op("vector", "tensor_mul", out=r, in0=r, in1=t)
            P.op("vector", "tensor_scalar", out=r, in0=r, scalar1=1e-6, scalar2=1000.0,
                 op0=ALU.max, op1=ALU.min)
        P.op("vector", "tensor_mul", out=t, in0=y16, in1=r)
        P.op("vector", "tensor_scalar", out=t, in0=t, scalar1=0.0, scalar2=1000.0,
             op0=ALU.max, op1=ALU.min)
        P.op("vector", "tensor_add", out=out_f16, in0=v_in, in1=t)

    nch1 = L // TC1
    NT1 = TC1 // 128
    x0_free = {}
    win_done = {}
    xn_ready_prev = [0]
    y_store_marks = {}
    ev_marks = []
    tr_ev_marks = []
    db_marks = []

    for bb in range(BPC):
        for ci in range(nch1):
            gi = bb * nch1 + ci
            par = gi % 2
            T0 = ci * TC1
            x0v = x0b[:, par * 2048:par * 2048 + 2048]
            if gi >= 2:
                P.wait("sync", "tv", x0_free[gi - 2])
            P.dma("sync", "qx", out=x0v.rearrange("p (n c) -> p n c", c=1024),
                  in_=_tm(x_in[bb], T0, NT1))
            P.wait("scalar", "qx")
            if gi >= 1:
                P.wait("scalar", "tv", xn_ready_prev[0])
            for tt2 in range(NT1):
                P.op("scalar", "activation", out=xc0[:, tt2 * 1024:tt2 * 1024 + 1024],
                     in_=x0v[:, tt2 * 1024:tt2 * 1024 + 1024], func=ACT.Copy,
                     bias=0.0, scale=1.0, accum_out=stats[:, tt2:tt2 + 1])
            P.inc_last("scalar", "tsl")
            P.wait("vector", "tsl", P.mark("tsl"))
            for tt2 in range(NT1):
                mean = stats[:, 4 + tt2:5 + tt2]
                P.op("vector", "tensor_scalar", out=mean, in0=stats[:, tt2:tt2 + 1],
                     scalar1=1.0 / 1024, scalar2=None, op0=ALU.mult)
                P.op("vector", "tensor_scalar", out=xc0[:, tt2 * 1024:tt2 * 1024 + 1024],
                     in0=xc0[:, tt2 * 1024:tt2 * 1024 + 1024],
                     scalar1=mean, scalar2=None, op0=ALU.subtract)
            P.inc_last("vector", "tv")
            x0_free[gi] = P.mark("tv")
            P.wait("scalar", "tv", P.mark("tv"))
            for tt2 in range(NT1):
                P.op("scalar", "activation", out=dump[:],
                     in_=xc0[:, tt2 * 1024:tt2 * 1024 + 1024], func=ACT.Square,
                     bias=0.0, scale=1.0, accum_out=stats[:, tt2:tt2 + 1])
            P.inc_last("scalar", "tsl")
            P.wait("vector", "tsl", P.mark("tsl"))
            for tt2 in range(NT1):
                var = stats[:, 4 + tt2:5 + tt2]
                inv = stats[:, 6 + tt2:7 + tt2]
                msk = stats[:, 7 + tt2:8 + tt2]
                P.op("vector", "tensor_scalar", out=var, in0=stats[:, tt2:tt2 + 1],
                     scalar1=1.0 / 1024, scalar2=1e-9, op0=ALU.mult, op1=ALU.add)
                P.op("vector", "tensor_scalar", out=inv, in0=var, scalar1=4.0,
                     scalar2=-0.5, op0=ALU.is_ge, op1=ALU.mult)
                for thv, dv in [(16.0, -0.25), (64.0, -0.125), (256.0, -0.0625),
                                (1024.0, -0.03125), (4096.0, -0.015625),
                                (16384.0, -0.0078125), (65536.0, -0.00390625)]:
                    P.op("vector", "tensor_scalar", out=msk, in0=var, scalar1=thv,
                         scalar2=dv, op0=ALU.is_ge, op1=ALU.mult)
                    P.op("vector", "tensor_add", out=inv, in0=inv, in1=msk)
                P.op("vector", "tensor_scalar", out=msk, in0=var, scalar1=0.25,
                     scalar2=1.0, op0=ALU.is_lt, op1=ALU.mult)
                P.op("vector", "tensor_add", out=inv, in0=inv, in1=msk)
                P.op("vector", "tensor_scalar", out=msk, in0=var, scalar1=0.0625,
                     scalar2=2.0, op0=ALU.is_lt, op1=ALU.mult)
                P.op("vector", "tensor_add", out=inv, in0=inv, in1=msk)
                P.op("vector", "tensor_scalar", out=inv, in0=inv, scalar1=1.0,
                     scalar2=None, op0=ALU.add)
                P.op("vector", "tensor_scalar", out=xn_tm[:, tt2 * 1024:tt2 * 1024 + 1024],
                     in0=xc0[:, tt2 * 1024:tt2 * 1024 + 1024], scalar1=inv,
                     scalar2=None, op0=ALU.mult)
            P.inc_last("vector", "tv")
            xn_ready = P.mark("tv")
            xn_ready_prev[0] = xn_ready
            if dbg and gi == DBG_CHUNK:
                P.wait("gpsimd", "tv", P.mark("tv"))
                P.dma("gpsimd", "qdbg", out=dbg_d["d_xn"], in_=xn_tm[:])
                P.wait("tensor", "qdbg")
            P.wait("tensor", "tv", xn_ready)
            if gi >= 1:
                P.wait("scalar", "tt", win_done[gi - 1])
            for idx in range(NT1 * 8):
                tt2, kt = divmod(idx, 8)
                slot = idx % 2
                if len(tr_ev_marks) >= 2:
                    P.wait("tensor", "tsl", tr_ev_marks[-2])
                P.op("tensor", "transpose", ps_tr[slot][:],
                     xn_tm[:, tt2 * 1024 + kt * 128: tt2 * 1024 + kt * 128 + 128],
                     ident_b[:], inc=("tt", 1))
                P.wait("scalar", "tt", P.mark("tt"))
                P.op("scalar", "copy",
                     out=xn_cm[:, kt * TC1 + tt2 * 128: kt * TC1 + tt2 * 128 + 128],
                     in_=ps_tr[slot][:], inc=("tsl", 1))
                tr_ev_marks.append(P.mark("tsl"))
            xncm_ready = P.mark("tsl")
            if dbg and gi == DBG_CHUNK:
                P.wait("gpsimd", "tsl", xncm_ready)
                P.dma("gpsimd", "qdbg", out=dbg_d["d_xncm"], in_=xn_cm[:])
                P.wait("tensor", "qdbg")
            P.wait("tensor", "tsl", xncm_ready)
            for jt in range(32):
                slot = jt % 2
                if len(ev_marks) >= 2:
                    P.wait("tensor", "tv", ev_marks[-2])
                for kt in range(8):
                    P.op("tensor", "matmul", ps_xz[slot][:],
                         wqin[:, kt * 4096 + jt * 128: kt * 4096 + jt * 128 + 128],
                         xn_cm[:, kt * TC1:kt * TC1 + TC1],
                         start=(kt == 0), stop=(kt == 7))
                P.inc_last("tensor", "tt")
                P.wait("vector", "tt", P.mark("tt"))
                dstt = xc_sb if jt < 16 else z_sb
                ct = jt % 16
                P.op("vector", "tensor_copy", out=dstt[:, ct * TC1:ct * TC1 + TC1],
                     in_=ps_xz[slot][:], inc=("tv", 1))
                ev_marks.append(P.mark("tv"))
            win_done[gi] = P.mark("tt")
            if dbg and gi == DBG_CHUNK:
                P.wait("gpsimd", "tv", P.mark("tv"))
                P.dma("gpsimd", "qdbg", out=dbg_d["d_xc"], in_=xc_sb[:])
                P.dma("gpsimd", "qdbg", out=dbg_d["d_z"], in_=z_sb[:])
                P.wait("vector", "qdbg")
            if gi >= 1:
                P.wait("vector", "qys", y_store_marks[gi - 1])
            sp_emit(z_sb[:], spz[:], 16 * TC1)
            if ci == 0:
                P.op("vector", "memset", hal[:], 0.0)
            for ct in range(16):
                xcv = xc_sb[:, ct * TC1:ct * TC1 + TC1]
                ov = z_sb[:, ct * TC1:ct * TC1 + TC1]
                P.op("vector", "tensor_scalar", out=ov, in0=xcv,
                     scalar1=convw_sb[:, ct * 4 + 3:ct * 4 + 4],
                     scalar2=convb_sb[:, ct:ct + 1], op0=ALU.mult, op1=ALU.add)
                for k in range(3):
                    sh = 3 - k
                    P.op("vector", "scalar_tensor_tensor",
                         out=z_sb[:, ct * TC1 + sh:ct * TC1 + TC1],
                         in0=xc_sb[:, ct * TC1:ct * TC1 + TC1 - sh],
                         scalar=convw_sb[:, ct * 4 + k:ct * 4 + k + 1],
                         in1=z_sb[:, ct * TC1 + sh:ct * TC1 + TC1],
                         op0=ALU.mult, op1=ALU.add)
                    P.op("vector", "scalar_tensor_tensor",
                         out=z_sb[:, ct * TC1:ct * TC1 + sh],
                         in0=hal[:, ct * 3 + k:ct * 3 + 3],
                         scalar=convw_sb[:, ct * 4 + k:ct * 4 + k + 1],
                         in1=z_sb[:, ct * TC1:ct * TC1 + sh],
                         op0=ALU.mult, op1=ALU.add)
            P.op("vector", "tensor_copy",
                 out=hal[:].rearrange("p (c k) -> p c k", k=3),
                 in_=xc_sb[:].rearrange("p (c t) -> p c t", t=TC1)[:, :, TC1 - 3:TC1])
            if dbg and gi == DBG_CHUNK:
                P.inc_last("vector", "tv")
                P.wait("gpsimd", "tv", P.mark("tv"))
                P.dma("gpsimd", "qdbg", out=dbg_d["d_spz"], in_=spz[:])
                P.dma("gpsimd", "qdbg", out=dbg_d["d_cv"], in_=z_sb[:])
                P.wait("vector", "qdbg")
            sp_emit(z_sb[:], u16[:], 16 * TC1, dump_init=(dbg and gi == DBG_CHUNK))
            P.inc_last("vector", "tv")
            u_ready = P.mark("tv")
            if dbg and gi == DBG_CHUNK:
                P.wait("gpsimd", "tv", P.mark("tv"))
                P.dma("gpsimd", "qdbg", out=dbg_d["d_u"], in_=u16[:])
                P.wait("vector", "qdbg")
            P.wait("tensor", "tv", u_ready)
            if len(db_marks) >= 1:
                P.wait("tensor", "tv", db_marks[-1])
            for kt in range(16):
                P.op("tensor", "matmul", ps_dt[0][0:96, :],
                     wqx[:, kt * 96:kt * 96 + 96], u16[:, kt * TC1:kt * TC1 + TC1],
                     start=(kt == 0), stop=(kt == 15))
            P.inc_last("tensor", "tt")
            P.wait("vector", "tt", P.mark("tt"))
            P.op("vector", "tensor_copy", out=dbl_sb[0:64, :],
                 in_=ps_dt[0][0:64, :], inc=("tv", 1))
            dbl_ready = P.mark("tv")
            P.wait("tensor", "tv", dbl_ready)
            for mt in range(16):
                slot = mt % 2
                if mt >= 2:
                    P.wait("tensor", "tv", db_marks[-2])
                P.op("tensor", "matmul", ps_dt[slot][:],
                     wqdt[0:64, mt * 128:mt * 128 + 128], dbl_sb[0:64, :],
                     start=True, stop=True, inc=("tt", 1))
                P.wait("vector", "tt", P.mark("tt"))
                P.op("vector", "tensor_scalar", out=dt_sb[:, mt * TC1:mt * TC1 + TC1],
                     in0=ps_dt[slot][:],
                     scalar1=bdt_sb[:, mt:mt + 1], scalar2=None, op0=ALU.add,
                     inc=("tv", 1))
                db_marks.append(P.mark("tv"))
            W = 16 * TC1
            aab = y32[:]
            P.wait("scalar", "tv", P.mark("tv"))
            P.op("scalar", "activation", out=aab, in_=dt_sb[:], func=ACT.Abs,
                 bias=0.0, scale=1.0, inc=("tsl", 1))
            P.wait("vector", "tsl", P.mark("tsl"))
            P.op("vector", "tensor_scalar", out=aab, in0=aab, scalar1=1.0,
                 scalar2=None, op0=ALU.add)
            P.op("vector", "reciprocal", out=aab, in_=aab)
            P.op("vector", "tensor_mul", out=dt_sb[:], in0=dt_sb[:], in1=aab)
            P.op("vector", "tensor_scalar", out=dt_sb[:], in0=dt_sb[:], scalar1=0.5,
                 scalar2=0.5, op0=ALU.mult, op1=ALU.add)
            P.op("vector", "tensor_scalar", out=aab, in0=dt_sb[:], scalar1=32768.0,
                 scalar2=2.0 ** 23, op0=ALU.mult, op1=ALU.add)
            P.op("vector", "tensor_scalar", out=aab, in0=aab, scalar1=2.0 ** 23,
                 scalar2=2.0 ** -15, op0=ALU.subtract, op1=ALU.mult)
            zb = z_sb[:]
            P.op("vector", "tensor_scalar", out=zb, in0=dt_sb[:], scalar1=32768.0,
                 scalar2=2.0 ** 52, op0=ALU.mult, op1=ALU.add)
            P.op("vector", "tensor_scalar", out=zb, in0=zb, scalar1=2.0 ** 52,
                 scalar2=2.0 ** -15, op0=ALU.subtract, op1=ALU.mult)
            P.op("vector", "tensor_sub", out=zb, in0=zb, in1=aab)
            P.op("vector", "scalar_tensor_tensor", out=dt_sb[:], in0=zb,
                 scalar=flag_sb[:], in1=aab, op0=ALU.mult, op1=ALU.add)
            P.op("vector", "tensor_scalar", out=aab, in0=dt_sb[:], scalar1=-1.0,
                 scalar2=1.0, op0=ALU.mult, op1=ALU.add)
            P.op("vector", "tensor_mul", out=b16[:], in0=aab, in1=u16[:])
            for ct in range(16):
                ini = 0.0 if ci == 0 else carry[:, ct:ct + 1]
                P.op("vector", "tensor_tensor_scan",
                     out=h_sb[:, ct * TC1:ct * TC1 + TC1],
                     data0=dt_sb[:, ct * TC1:ct * TC1 + TC1],
                     data1=b16[:, ct * TC1:ct * TC1 + TC1],
                     initial=ini, op0=ALU.mult, op1=ALU.add)
            if dbg and gi == DBG_CHUNK:
                P.inc_last("vector", "tv")
                P.wait("gpsimd", "tv", P.mark("tv"))
                P.dma("gpsimd", "qdbg", out=dbg_d["d_dec"], in_=dt_sb[:])
                P.dma("gpsimd", "qdbg", out=dbg_d["d_h"], in_=h_sb[:])
                P.wait("vector", "qdbg")
            P.op("vector", "tensor_copy", out=carry[:],
                 in_=h_sb[:].rearrange("p (c t) -> p c t", t=TC1)[:, :, TC1 - 1:TC1]
                 .rearrange("p c o -> p (c o)"))
            P.op("vector", "tensor_mul", out=spz[:], in0=h_sb[:], in1=spz[:])
            P.inc_last("vector", "tv")
            y_ready = P.mark("tv")
            P.wait("gpsimd", "tv", y_ready)
            P.dma("gpsimd", "qys",
                  out=ys_d[bb].rearrange("(k p) t -> p k t", p=128)[:, :, T0:T0 + TC1],
                  in_=spz[:].rearrange("p (k t) -> p k t", t=TC1))
            y_store_marks[gi] = P.mark("qys")
            if PARANOID:
                P.fence()
    ys_done = P.mark("qys")

    if stop_after == "m1":
        P.wait("gpsimd", "qys", ys_done)
        P.wait("gpsimd", "qdbg")
        return P, dr, es0

    P.wait("vector", "tt", P.mark("tt"))
    P.wait("sync", "tv", P.mark("tv"))
    es1a.close()
    quantize(w_out_t, DI, D, wqin[:, 0:16 * D], 0.25, False, y32[:, 0:1024],
             r16[:, 0:1024], ps_sm, 0)
    wqout = wqin

    NT2 = TC2 // 128
    nch2 = L // TC2
    mo_cm = None
    es2 = ExitStack()
    A2 = Alloc(nc, es2, "m2")
    ybig = A2.sb("ybig", [128, 16 * TC2], MMDT)
    mo_cm2 = A2.sb("mo_cm2", [128, 8 * TC2], F32)
    mo_tm = A2.sb("mo_tm", [128, 4 * 1024], F32)
    x1c = A2.sb("x1c", [128, 4 * 1024], F16)
    ps_mo = [A2.ps(f"ps_mo{i}", [128, TC2], F32) for i in range(2)]
    ps_tr2 = [A2.ps(f"ps_tr2{i}", [128, 128], F32) for i in range(2)]

    mo_marks = []
    tr2_marks = []
    x0_free2 = {}
    x1_vec_done = {}
    x1_store_marks = {}
    ymark_free = {}
    for bb in range(BPC):
        for ci in range(nch2):
            gi = bb * nch2 + ci
            T0 = ci * TC2
            par = gi % 2
            x0v = x0b[:, par * 2048:par * 2048 + 2048]
            if gi >= 1:
                P.wait("sync", "tv", x0_free2[gi - 1])
            else:
                P.wait("sync", "tv", P.mark("tv"))
                P.wait("sync", "qys", ys_done)
            P.dma("sync", "qx", out=x0b[:].rearrange("p (n c) -> p n c", c=1024),
                  in_=_tm(x_in[bb], T0, NT2))
            x0mark = P.mark("qx")
            if gi >= 1:
                P.wait("sync", "tv", ymark_free[gi - 1])
            P.dma("sync", "qyl",
                  out=ybig[:].rearrange("p (k t) -> p k t", t=TC2),
                  in_=ys_d[bb].rearrange("(k p) t -> p k t", p=128)[:, :, T0:T0 + TC2])
            P.wait("tensor", "qyl")
            if gi >= 1:
                P.wait("tensor", "tv", mo_marks[-1])
            for jt in range(8):
                slot = jt % 2
                if jt >= 2:
                    P.wait("tensor", "tv", mo_marks[-2])
                for kt in range(16):
                    P.op("tensor", "matmul", ps_mo[slot][:],
                         wqout[:, kt * 1024 + jt * 128: kt * 1024 + jt * 128 + 128],
                         ybig[:, kt * TC2:kt * TC2 + TC2],
                         start=(kt == 0), stop=(kt == 15))
                P.inc_last("tensor", "tt")
                P.wait("vector", "tt", P.mark("tt"))
                P.op("vector", "tensor_copy", out=mo_cm2[:, jt * TC2:jt * TC2 + TC2],
                     in_=ps_mo[slot][:], inc=("tv", 1))
                mo_marks.append(P.mark("tv"))
            ymark_free[gi] = P.mark("tv")
            P.wait("tensor", "tv", mo_marks[-1])
            if gi >= 1:
                P.wait("scalar", "tv", x1_vec_done[gi - 1])
            for idx in range(NT2 * 8):
                tt2, kt = divmod(idx, 8)
                slot = idx % 2
                if len(tr2_marks) >= 2:
                    P.wait("tensor", "tsl", tr2_marks[-2])
                P.op("tensor", "transpose", ps_tr2[slot][:],
                     mo_cm2[:, kt * TC2 + tt2 * 128: kt * TC2 + tt2 * 128 + 128],
                     ident_f[:], inc=("tt", 1))
                P.wait("scalar", "tt", P.mark("tt"))
                P.op("scalar", "copy",
                     out=mo_tm[:, tt2 * 1024 + kt * 128: tt2 * 1024 + kt * 128 + 128],
                     in_=ps_tr2[slot][:], inc=("tsl", 1))
                tr2_marks.append(P.mark("tsl"))
            P.wait("vector", "tsl", P.mark("tsl"))
            P.wait("vector", "qx", x0mark)
            P.op("vector", "scalar_tensor_tensor", out=mo_tm[:], in0=x0b[:, 0:4096],
                 scalar=2.0, in1=mo_tm[:], op0=ALU.mult, op1=ALU.add,
                 inc=("tv", 1))
            x0_free2[gi] = P.mark("tv")
            P.wait("scalar", "tv", P.mark("tv"))
            for tt2 in range(NT2):
                P.op("scalar", "activation", out=dump[:],
                     in_=mo_tm[:, tt2 * 1024:tt2 * 1024 + 1024], func=ACT.Copy,
                     bias=0.0, scale=1.0, accum_out=stats[:, tt2:tt2 + 1])
            P.inc_last("scalar", "tsl")
            P.wait("vector", "tsl", P.mark("tsl"))
            for tt2 in range(NT2):
                mean = stats[:, 4 + tt2:5 + tt2]
                P.op("vector", "tensor_scalar", out=mean, in0=stats[:, tt2:tt2 + 1],
                     scalar1=1.0 / 1024, scalar2=None, op0=ALU.mult)
                P.op("vector", "tensor_scalar",
                     out=mo_tm[:, tt2 * 1024:tt2 * 1024 + 1024],
                     in0=mo_tm[:, tt2 * 1024:tt2 * 1024 + 1024],
                     scalar1=mean, scalar2=None, op0=ALU.subtract)
            P.inc_last("vector", "tv")
            P.wait("scalar", "tv", P.mark("tv"))
            for tt2 in range(NT2):
                P.op("scalar", "activation", out=dump[:],
                     in_=mo_tm[:, tt2 * 1024:tt2 * 1024 + 1024], func=ACT.Square,
                     bias=0.0, scale=1.0, accum_out=stats[:, tt2:tt2 + 1])
            for tt2 in range(NT2):
                P.op("scalar", "activation", out=stats[:, 4 + tt2:5 + tt2],
                     in_=stats[:, tt2:tt2 + 1], func=ACT.Sqrt, bias=eps5_sb[:],
                     scale=1.0 / 1024)
            P.inc_last("scalar", "tsl")
            P.wait("vector", "tsl", P.mark("tsl"))
            if gi >= 1:
                P.wait("vector", "qx1", x1_store_marks[gi - 1])
            for tt2 in range(NT2):
                P.op("vector", "reciprocal", out=stats[:, 8 + tt2:9 + tt2],
                     in_=stats[:, 4 + tt2:5 + tt2])
                P.op("vector", "tensor_scalar", out=x1c[:, tt2 * 1024:tt2 * 1024 + 1024],
                     in0=mo_tm[:, tt2 * 1024:tt2 * 1024 + 1024],
                     scalar1=stats[:, 8 + tt2:9 + tt2], scalar2=None, op0=ALU.mult)
            P.inc_last("vector", "tv")
            x1_vec_done[gi] = P.mark("tv")
            P.wait("gpsimd", "tv", P.mark("tv"))
            P.dma("gpsimd", "qx1", out=_tm(x1_d[bb], T0, NT2),
                  in_=x1c[:].rearrange("p (n c) -> p n c", c=1024))
            x1_store_marks[gi] = P.mark("qx1")
            if PARANOID:
                P.fence()
    x1_done_q = P.mark("qx1")

    if stop_after == "m2":
        P.wait("gpsimd", "qx1", x1_done_q)
        P.wait("gpsimd", "qdbg")
        return P, dr, es0

    es2.close()
    es1.close()

    es3 = ExitStack()
    A3 = Alloc(nc, es3, "attn")
    wq_sb = A3.sb("wq_sb", [128, 8 * 1024], MMDT)
    wk_sb = A3.sb("wk_sb", [128, 8 * 1024], MMDT)
    wv_sb = A3.sb("wv_sb", [128, 8 * 1024], MMDT)
    wo_sb = A3.sb("wo_sb", [128, 8 * 1024], MMDT)
    h_cm = A3.sb("h_cm", [128, 8 * L], MMDT)
    k_cm = A3.sb("k_cm", [128, 8 * L], MMDT)
    v_buf = A3.sb("v_buf", [128, 16 * (NH * 65)], BF16)
    q_sb = A3.sb("q_sb", [128, 512], MMDT)
    x1c = A3.sb("x1c_a", [128, 4 * 1024], F16)
    pr_sb = A3.sb("pr_sb", [128, 2 * 512], BF16)
    ctx_sb = A3.sb("ctx_sb", [128, 8 * 512], F32)
    ctxn_sb = A3.sb("ctxn_sb", [128, 8 * 512], MMDT)
    ctxo_t1 = A3.sb("ctxo_t1", [128, 2 * 1024], F16)
    xsbuf = A3.sb("xsbuf", [128, 1024], F16)
    dcol = A3.sb("dcol", [128, 512], F32)
    den_b = A3.sb("den_b", [128, 512], F32)
    stats3 = A3.sb("stats3", [128, 16], F32)
    assert MMDT == F16, "in-place rmsnorm assumes x1 spill dtype == MMDT"
    htm = x1c

    p_a = A3.ps("p_a", [128, 512], F32)
    p_b = A3.ps("p_b", [128, 512], F32)
    p_c = A3.ps("p_c", [128, 512], F32)
    p_dd = A3.ps("p_dd", [128, 512], F32)
    p_e = A3.ps("p_e", [128, 512], F32)
    p_f = A3.ps("p_f", [128, 512], F32)
    p_tr3 = A3.ps("p_tr3", [128, 128], MMDT)
    p_tr4 = A3.ps("p_tr4", [128, 128], MMDT)

    wlda = ctx_sb
    P.wait("sync", "qx1", x1_done_q)
    for wi, (wdr, wsb, foldw) in enumerate([(wq_in, wq_sb, True), (wk_in, wk_sb, True),
                                            (wv_in, wv_sb, True), (wo_in, wo_sb, False)]):
        for kt in range(8):
            par = (wi * 8 + kt) % 2
            if len(wcast_marks) >= 2:
                P.wait("sync", "tsl", wcast_marks[-2])
            P.dma("sync", "qw", out=wlda[:, par * 1024:par * 1024 + 1024],
                  in_=wdr[kt * 128:kt * 128 + 128, :])
            P.wait("scalar", "qw")
            P.op("scalar", "activation", out=wsb[:, kt * 1024:kt * 1024 + 1024],
                 in_=wlda[:, par * 1024:par * 1024 + 1024], func=ACT.Copy,
                 bias=0.0, scale=(n1w_sb[:, kt:kt + 1] if foldw else 1.0),
                 inc=("tsl", 1))
            wcast_marks.append(P.mark("tsl"))

    NT2 = TC2 // 128
    for bb in range(BPC):
        trh_marks = []
        kv_marks = []
        P.op("vector", "memset", dcol[:], 1.0)
        P.op("vector", "memset",
             v_buf[:].rearrange("p (tt h d) -> p tt h d", h=NH, d=65)[:, :, :, 64:65],
             1.0, inc=("tv", 1))
        for sc in range(4):
            T0 = sc * TC2
            P.wait("sync", a_x1_free[0][0], a_x1_free[0][1])
            P.dma("sync", "qx", out=x1c[:].rearrange("p (n c) -> p n c", c=1024),
                  in_=_tm(x1_d[bb], T0, NT2))
            P.wait("scalar", "qx")
            for tt2 in range(NT2):
                P.op("scalar", "activation", out=ctx_sb[:, 0:1024],
                     in_=x1c[:, tt2 * 1024:tt2 * 1024 + 1024], func=ACT.Square,
                     bias=0.0, scale=1.0, accum_out=stats3[:, tt2:tt2 + 1])
            for tt2 in range(NT2):
                P.op("scalar", "activation", out=stats3[:, 4 + tt2:5 + tt2],
                     in_=stats3[:, tt2:tt2 + 1], func=ACT.Sqrt, bias=eps6_sb[:],
                     scale=1.0 / 1024)
            P.inc_last("scalar", "tsl")
            P.wait("vector", "tsl", P.mark("tsl"))
            for tt2 in range(NT2):
                P.op("vector", "reciprocal", out=stats3[:, 8 + tt2:9 + tt2],
                     in_=stats3[:, 4 + tt2:5 + tt2])
                P.op("vector", "tensor_scalar", out=htm[:, tt2 * 1024:tt2 * 1024 + 1024],
                     in0=x1c[:, tt2 * 1024:tt2 * 1024 + 1024],
                     scalar1=stats3[:, 8 + tt2:9 + tt2], scalar2=None, op0=ALU.mult)
            P.inc_last("vector", "tv")
            htm_ready = P.mark("tv")
            P.wait("tensor", "tv", htm_ready)
            for idx in range(NT2 * 8):
                tt2, kt = divmod(idx, 8)
                ptr = (p_tr3, p_tr4)[idx % 2]
                if len(trh_marks) >= 2:
                    P.wait("tensor", "tsl", trh_marks[-2])
                P.op("tensor", "transpose", ptr[:],
                     htm[:, tt2 * 1024 + kt * 128: tt2 * 1024 + kt * 128 + 128],
                     ident_b[:], inc=("tt", 1))
                P.wait("scalar", "tt", P.mark("tt"))
                P.op("scalar", "copy",
                     out=h_cm[:, kt * L + T0 + tt2 * 128: kt * L + T0 + tt2 * 128 + 128],
                     in_=ptr[:], inc=("tsl", 1))
                trh_marks.append(P.mark("tsl"))
            hch_ready = P.mark("tsl")
            a_x1_free[0] = ("tsl", hch_ready)
            P.wait("tensor", "tsl", hch_ready)
            for ct in range(8):
                pk = (p_a, p_b)[len(kv_marks) % 2]
                if len(kv_marks) >= 2:
                    P.wait("tensor", "tv", kv_marks[-2])
                for kt in range(8):
                    P.op("tensor", "matmul", pk[:],
                         wk_sb[:, kt * 1024 + ct * 128: kt * 1024 + ct * 128 + 128],
                         h_cm[:, kt * L + T0: kt * L + T0 + TC2],
                         start=(kt == 0), stop=(kt == 7))
                P.inc_last("tensor", "tt")
                P.wait("vector", "tt", P.mark("tt"))
                P.op("vector", "tensor_copy",
                     out=k_cm[:, ct * L + T0: ct * L + T0 + TC2],
                     in_=pk[:], inc=("tv", 1))
                kv_marks.append(P.mark("tv"))
            for tt in range(NT2):
                for half in range(2):
                    pv = (p_c, p_dd)[len(kv_marks) % 2]
                    if len(kv_marks) >= 2:
                        P.wait("tensor", "tv", kv_marks[-2])
                    for kt in range(8):
                        P.op("tensor", "matmul", pv[:],
                             h_cm[:, kt * L + T0 + tt * 128: kt * L + T0 + tt * 128 + 128],
                             wv_sb[:, kt * 1024 + half * 512: kt * 1024 + half * 512 + 512],
                             start=(kt == 0), stop=(kt == 7))
                    P.inc_last("tensor", "tt")
                    P.wait("vector", "tt", P.mark("tt"))
                    gt = sc * NT2 + tt
                    dstv = v_buf[:, gt * 1040 + half * 520: gt * 1040 + half * 520 + 520] \
                        .rearrange("p (h d) -> p h d", d=65)[:, :, 0:64]
                    P.op("vector", "tensor_copy", out=dstv,
                         in_=pv[:].rearrange("p (h d) -> p h d", d=64), inc=("tv", 1))
                    kv_marks.append(P.mark("tv"))
        for sc in range(4):
            T0 = sc * TC2
            den_row_marks = []
            if len(ctv_ev) >= 1:
                P.wait("tensor", "tv", ctv_ev[-1])
            for ct in range(8):
                P.wait("tensor", "tv", q_free[0])
                for kt in range(8):
                    P.op("tensor", "matmul", p_a[:],
                         wq_sb[:, kt * 1024 + ct * 128: kt * 1024 + ct * 128 + 128],
                         h_cm[:, kt * L + T0: kt * L + T0 + TC2],
                         start=(kt == 0), stop=(kt == 7))
                P.inc_last("tensor", "tt")
                P.wait("vector", "tt", P.mark("tt"))
                P.op("vector", "tensor_copy", out=q_sb[:], in_=p_a[:], inc=("tv", 1))
                q_ready = P.mark("tv")
                P.wait("tensor", "tv", q_ready)
                for hh in range(2):
                    h = ct * 2 + hh
                    pctx = (p_c, p_dd)[h % 2]
                    if len(ctx_free_marks) >= 2:
                        P.wait("tensor", "tv", ctx_free_marks[-2])
                    if len(den_row_marks) >= 2:
                        P.wait("tensor", "tsl", den_row_marks[-2])
                    elif den_prev[0] > 0 and len(den_row_marks) == 0:
                        P.wait("tensor", "tsl", den_prev[0])
                    exp_marks = []

                    def issue_score(tt):
                        psc = (p_b, p_e)[tt % 2]
                        P.op("tensor", "matmul", psc[:],
                             k_cm[hh * 64:hh * 64 + 64,
                                  ct * L + tt * 128: ct * L + tt * 128 + 128],
                             q_sb[hh * 64:hh * 64 + 64, 0:512],
                             start=True, stop=True, inc=("tt", 1))
                        P.wait("scalar", "tt", P.mark("tt"))
                        ppr = pr_sb[:, (tt % 2) * 512:(tt % 2) * 512 + 512]
                        P.op("scalar", "activation", out=ppr, in_=psc[:],
                             func=ACT.Exp, bias=0.0, scale=0.125, inc=("tsl", 1))
                        exp_marks.append(P.mark("tsl"))

                    issue_score(0)
                    for tt in range(16):
                        if tt + 1 < 16:
                            issue_score(tt + 1)
                        ppr = pr_sb[:, (tt % 2) * 512:(tt % 2) * 512 + 512]
                        if dbg and bb == 0 and sc == 0 and h == 0 and tt == 0:
                            psc = (p_b, p_e)[tt % 2]
                            P.wait("vector", "tsl", exp_marks[tt])
                            P.op("vector", "tensor_copy", out=ctx_sb[:, 3584:4096],
                                 in_=psc[:], inc=("tv", 1))
                            P.wait("gpsimd", "tv", P.mark("tv"))
                            P.wait("gpsimd", "tsl", exp_marks[tt])
                            P.dma("gpsimd", "qdbg", out=dbg_d["d_pr"], in_=ppr)
                            P.dma("gpsimd", "qdbg", out=dbg_d["d_sc2"],
                                  in_=ctx_sb[:, 3584:4096])
                            P.dma("gpsimd", "qdbg", out=dbg_d["d_qs"], in_=q_sb[:])
                            P.dma("gpsimd", "qdbg", out=dbg_d["d_kc"],
                                  in_=k_cm[:, 0:512])
                            P.wait("scalar", "qdbg")
                            P.wait("tensor", "qdbg")
                            P.wait("vector", "qdbg")
                        P.wait("tensor", "tsl", exp_marks[tt])
                        P.op("tensor", "matmul", pctx[0:65, :],
                             v_buf[:, tt * 1040 + h * 65: tt * 1040 + h * 65 + 65],
                             ppr, start=(tt == 0), stop=(tt == 15))
                    P.inc_last("tensor", "tt")
                    ctx_done = P.mark("tt")
                    P.wait("scalar", "tt", ctx_done)
                    if h % 4 == 0 and dgrp_flush[0] > 0:
                        P.wait("scalar", "qys", dgrp_flush[0])
                    P.op("scalar", "copy",
                         out=dcol[(h % 4) * 32:(h % 4) * 32 + 1, :],
                         in_=pctx[64:65, :], inc=("tsl", 1))
                    den_row_marks.append(P.mark("tsl"))
                    if h % 4 == 3:
                        P.wait("vector", "tsl", P.mark("tsl"))
                        P.op("vector", "reciprocal", out=dcol[:], in_=dcol[:])
                        P.inc_last("vector", "tv")
                        P.wait("gpsimd", "tv", P.mark("tv"))
                        if h == 3:
                            P.wait("gpsimd", "qyl")
                        for pp in range(4):
                            P.dma("gpsimd", "qys",
                                  out=bass.AP(den_d.tensor, (h - 3 + pp) * TC2,
                                              [[1, 1], [1, 512]]),
                                  in_=dcol[pp * 32:pp * 32 + 1, :])
                        dgrp_flush[0] = P.mark("qys")
                    P.wait("vector", "tt", ctx_done)
                    P.op("vector", "tensor_copy",
                         out=ctx_sb[:, ct * 512:ct * 512 + 512][hh * 64:hh * 64 + 64, :],
                         in_=pctx[0:64, :], inc=("tv", 1))
                    ctx_free_marks.append(P.mark("tv"))
            den_prev[0] = den_row_marks[-1]
            P.wait("sync", "qys")
            for ct in range(8):
                for hh in range(2):
                    h = ct * 2 + hh
                    P.dma("sync", "qyl",
                          out=den_b[hh * 64:hh * 64 + 64, :],
                          in_=bass.AP(den_d.tensor, h * TC2, [[0, 64], [1, 512]]))
                P.wait("vector", "qyl")
                P.op("vector", "tensor_mul", out=ctxn_sb[:, ct * 512:ct * 512 + 512],
                     in0=ctx_sb[:, ct * 512:ct * 512 + 512], in1=den_b[:],
                     inc=("tv", 1))
                if dbg and bb == 0 and sc == 0 and ct == 0:
                    P.wait("gpsimd", "tv", P.mark("tv"))
                    P.dma("gpsimd", "qdbg", out=dbg_d["d_ctx"], in_=ctx_sb[:, 0:512])
                    P.dma("gpsimd", "qdbg", out=dbg_d["d_den"], in_=den_b[:])
                    cv = ctxn_sb[:, 0:512]
                    P.dma("gpsimd", "qdbg", out=dbg_d["d_ctxn"], in_=cv)
                    P.wait("vector", "qdbg")
                    P.wait("sync", "qdbg")
                if ct < 7:
                    P.wait("sync", "tv", P.mark("tv"))
            ctxn_ready = P.mark("tv")
            P.wait("tensor", "tv", ctxn_ready)
            for co in range(8):
                pwo = (p_f, p_a)[co % 2]
                if len(wo_ev_marks) >= 2:
                    P.wait("tensor", "tv", wo_ev_marks[-2])
                for kt in range(8):
                    P.op("tensor", "matmul", pwo[:],
                         wo_sb[:, kt * 1024 + co * 128: kt * 1024 + co * 128 + 128],
                         ctxn_sb[:, kt * 512: kt * 512 + 512],
                         start=(kt == 0), stop=(kt == 7))
                P.inc_last("tensor", "tt")
                P.wait("vector", "tt", P.mark("tt"))
                P.op("vector", "tensor_copy", out=ctx_sb[:, co * 512:co * 512 + 512],
                     in_=pwo[:], inc=("tv", 1))
                wo_ev_marks.append(P.mark("tv"))
            ctxo_ready = P.mark("tv")
            q_free[0] = ctxo_ready
            P.wait("sync", a_x1_free[0][0], a_x1_free[0][1])
            P.dma("sync", "qx", out=x1c[:].rearrange("p (n c) -> p n c", c=1024),
                  in_=_tm(x1_d[bb], T0, NT2))
            P.wait("tensor", "tv", ctxo_ready)
            for tt2 in range(NT2):
                par2 = tt2 % 2
                ctv = ctxo_t1[:, par2 * 1024:par2 * 1024 + 1024]
                sctv = ctxo_t1[:, (1 - par2) * 1024:(1 - par2) * 1024 + 1024]
                if len(tro_marks) >= 2:
                    P.wait("tensor", "tv", tro_marks[-2])
                for kt in range(8):
                    pslice = (p_f, p_dd)[kt % 2][:, 0:128]
                    if len(ctv_ev) >= 2:
                        P.wait("tensor", "tv", ctv_ev[-2])
                    P.op("tensor", "transpose", pslice,
                         ctx_sb[:, kt * 512 + tt2 * 128: kt * 512 + tt2 * 128 + 128],
                         ident_f[:], inc=("tt", 1))
                    P.wait("vector", "tt", P.mark("tt"))
                    if kt == 0:
                        if len(x2_marks) >= 1:
                            P.wait("vector", "qx1", x2_marks[-1])
                    P.op("vector", "tensor_copy", out=ctv[:, kt * 128:kt * 128 + 128],
                         in_=pslice, inc=("tv", 1))
                    ctv_ev.append(P.mark("tv"))
                if tt2 == 0:
                    P.wait("vector", "qx")
                x1v = x1c[:, tt2 * 1024:tt2 * 1024 + 1024]
                P.op("vector", "tensor_add", out=xsbuf[:], in0=x1v, in1=ctv)
                P.op("vector", "scalar_tensor_tensor", out=sctv,
                     in0=x1v, scalar=2.0, in1=ctv, op0=ALU.mult, op1=ALU.add,
                     inc=("tv", 1))
                tro_marks.append(P.mark("tv"))
                P.wait("gpsimd", "tv", P.mark("tv"))
                P.dma("gpsimd", "qx1", out=_tm(x2_d[bb], T0 + tt2 * 128, 1),
                      in_=xsbuf[:].rearrange("p (n c) -> p n c", c=1024))
                P.dma("gpsimd", "qx1", out=_tm(ss_d[bb], T0 + tt2 * 128, 1),
                      in_=sctv.rearrange("p (n c) -> p n c", c=1024))
                x2_marks.append(P.mark("qx1"))
            x2_store_m[0] = P.mark("qx1")
            a_x1_free[0] = ("tv", P.mark("tv"))

    x2_done = P.mark("qx1")
    es3.close()

    if stop_after == "attn":
        P.wait("gpsimd", "qx1", x2_done)
        P.wait("gpsimd", "qdbg")
        return P, dr, es0

    TCF = 1024
    NTF = TCF // 128
    es4 = ExitStack()
    A4 = Alloc(nc, es4, "moe")
    wg_sb = A4.sb("wg_sb", [128, 8 * DFF], MMDT)
    wu_sb = A4.sb("wu_sb", [128, 8 * DFF], MMDT)
    wd_sb = A4.sb("wd_sb", [128, 16 * 1024], MMDT)
    h2_cm = A4.sb("h2_cm", [128, 8 * TCF], MMDT)
    act_sb = A4.sb("act_sb", [128, 16 * TCF], MMDT)
    sg_sb = A4.sb("sg_sb", [128, 2 * 512], MMDT)
    spec_sb = A4.sb("spec_sb", [128, 8 * TCF], F16)
    x2cf = A4.sb("x2cf", [128, 8 * 1024], F16)
    ce_all = A4.sb("ce_all", [128, 8 * 8], F16)
    ce_b = A4.sb("ce_b", [128, TCF], F16)
    rw_sb = A4.sb("rw_sb", [128, 8 * 8], MMDT)
    wldf = A4.sb("wldf", [128, 2 * DFF], F32)
    stats4 = A4.sb("stats4", [128, 16], F32)
    assert MMDT == F16
    h2tm = x2cf
    prb = A4.sb("prb", [128, 8], F32)
    prm = A4.sb("prm", [128, 8], F32)
    scf = A4.sb("scf", [128, 8], F32)
    moet1 = A4.sb("moet1", [128, 1024], F32)
    outc = A4.sb("outc", [128, 1024], F32)

    p_g0 = A4.ps("p_g0", [128, 512], F32)
    p_g1 = A4.ps("p_g1", [128, 512], F32)
    p_u0 = A4.ps("p_u0", [128, 512], F32)
    p_u1 = A4.ps("p_u1", [128, 512], F32)
    p_d0 = A4.ps("p_d0", [128, 512], F32)
    p_d1 = A4.ps("p_d1", [128, 512], F32)
    p_tr5 = A4.ps("p_tr5", [128, 128], MMDT)
    p_r = p_d0
    p_tr6 = A4.ps("p_tr6", [128, 128], F16)

    P.wait("sync", "qx1", x2_done)
    P.dma("sync", "qw", out=wldf[:, 0:64].rearrange("p (k e) -> p k e", e=NE),
          in_=rw_in.rearrange("(k p) e -> p k e", p=128), allow_slow_non_contiguous=True)
    P.wait("vector", "qw")
    for kt in range(8):
        P.op("vector", "tensor_scalar", out=rw_sb[:, kt * 8:kt * 8 + 8],
             in0=wldf[:, kt * 8:kt * 8 + 8], scalar1=n2w_sb[:, kt:kt + 1],
             scalar2=None, op0=ALU.mult)
    P.inc_last("vector", "tv")
    rw_ready = P.mark("tv")

    def stream_weight(wdr, wsb, nkt, ncols, fold_n2w, e_idx=None):
        for kt in range(nkt):
            par = len(wc_marks) % 2
            if len(wc_marks) >= 2:
                P.wait("sync", wc_marks[-2][0], wc_marks[-2][1])
            src2 = wdr if e_idx is None else wdr[e_idx]
            P.dma("sync", "qw", out=wldf[:, par * DFF:par * DFF + ncols],
                  in_=src2[kt * 128:kt * 128 + 128, :])
            P.wait("vector", "qw")
            if fold_n2w:
                P.op("vector", "tensor_scalar",
                     out=wsb[:, kt * ncols:kt * ncols + ncols],
                     in0=wldf[:, par * DFF:par * DFF + ncols],
                     scalar1=n2w_sb[:, kt:kt + 1], scalar2=None, op0=ALU.mult,
                     inc=("tv", 1))
            else:
                P.op("vector", "tensor_copy",
                     out=wsb[:, kt * ncols:kt * ncols + ncols],
                     in_=wldf[:, par * DFF:par * DFF + ncols], inc=("tv", 1))
            wc_marks.append(("tv", P.mark("tv")))

    wc_marks = []
    wbm = [0]
    pass2_first = [True]
    for bb in range(BPC):
        for sc in range(2):
            first_pass = (bb == 0 and sc == 0)
            T0 = sc * TCF
            P.wait("sync", f_x2_free[0][0], f_x2_free[0][1])
            P.dma("sync", "qx", out=x2cf[:].rearrange("p (n c) -> p n c", c=1024),
                  in_=_tm(x2_d[bb], T0, NTF))
            P.wait("scalar", "qx")
            for tt2 in range(NTF):
                P.op("scalar", "activation", out=act_sb[:, 0:1024],
                     in_=x2cf[:, tt2 * 1024:tt2 * 1024 + 1024], func=ACT.Square,
                     bias=0.0, scale=1.0, accum_out=stats4[:, tt2:tt2 + 1])
                P.op("scalar", "activation", out=stats4[:, 8 + tt2:9 + tt2],
                     in_=stats4[:, tt2:tt2 + 1], func=ACT.Sqrt, bias=eps6_sb[:],
                     scale=1.0 / 1024)
            P.inc_last("scalar", "tsl")
            P.wait("vector", "tsl", P.mark("tsl"))
            for tt2 in range(NTF):
                P.op("vector", "reciprocal", out=stats4[:, tt2:tt2 + 1],
                     in_=stats4[:, 8 + tt2:9 + tt2])
                P.op("vector", "tensor_scalar", out=h2tm[:, tt2 * 1024:tt2 * 1024 + 1024],
                     in0=x2cf[:, tt2 * 1024:tt2 * 1024 + 1024],
                     scalar1=stats4[:, tt2:tt2 + 1], scalar2=None, op0=ALU.mult)
            P.inc_last("vector", "tv")
            h2_ready = P.mark("tv")
            P.wait("tensor", "tv", h2_ready)
            trf_marks = []
            for idx in range(NTF * 8):
                tt2, kt = divmod(idx, 8)
                if len(trf_marks) >= 1:
                    P.wait("tensor", "tsl", trf_marks[-1])
                P.op("tensor", "transpose", p_tr5[:],
                     h2tm[:, tt2 * 1024 + kt * 128: tt2 * 1024 + kt * 128 + 128],
                     ident_b[:], inc=("tt", 1))
                P.wait("scalar", "tt", P.mark("tt"))
                P.op("scalar", "copy",
                     out=h2_cm[:, kt * TCF + tt2 * 128: kt * TCF + tt2 * 128 + 128],
                     in_=p_tr5[:], inc=("tsl", 1))
                trf_marks.append(P.mark("tsl"))
            h2cm_ready = P.mark("tsl")
            f_x2_free[0] = ("tsl", h2cm_ready)
            P.wait("tensor", "tsl", h2cm_ready)
            P.wait("tensor", "tv", rw_ready)
            for tt2 in range(NTF):
                if rt_free[0][1] > 0:
                    P.wait("tensor", rt_free[0][0], rt_free[0][1])
                    P.wait("tensor", rt_free[1][0], rt_free[1][1])
                for kt in range(8):
                    P.op("tensor", "matmul", p_r[:, 0:8],
                         h2_cm[:, kt * TCF + tt2 * 128: kt * TCF + tt2 * 128 + 128],
                         rw_sb[:, kt * 8:kt * 8 + 8],
                         start=(kt == 0), stop=(kt == 7))
                P.inc_last("tensor", "tt")
                P.wait("vector", "tt", P.mark("tt"))
                P.op("vector", "reduce_max", out=scf[:, 0:1], in_=p_r[:, 0:8], axis=AX,
                     negate=True)
                P.wait("scalar", "tv", P.mark("tv"))
                P.op("scalar", "activation", out=prb[:], in_=p_r[:, 0:8], func=ACT.Exp,
                     bias=scf[:, 0:1], scale=1.0, accum_out=scf[:, 1:2], inc=("tsl", 1))
                rt_free[0] = ("tv", P.mark("tv"))
                rt_free[1] = ("tsl", P.mark("tsl"))
                P.wait("vector", "tsl", P.mark("tsl"))
                P.op("vector", "reciprocal", out=scf[:, 2:3], in_=scf[:, 1:2])
                P.op("vector", "tensor_scalar", out=prb[:], in0=prb[:],
                     scalar1=scf[:, 2:3], scalar2=None, op0=ALU.mult)
                P.op("vector", "reduce_max", out=scf[:, 3:4], in_=prb[:], axis=AX)
                P.op("vector", "tensor_scalar", out=prm[:], in0=prb[:],
                     scalar1=scf[:, 3:4], scalar2=None, op0=ALU.is_ge)
                P.op("vector", "scalar_tensor_tensor", out=prm[:], in0=prm[:],
                     scalar=-1e30, in1=prb[:], op0=ALU.mult, op1=ALU.add)
                P.op("vector", "reduce_max", out=scf[:, 4:5], in_=prm[:], axis=AX)
                P.inc_last("vector", "tv")
                P.wait("scalar", "tv", P.mark("tv"))
                P.op("scalar", "activation", out=scf[:, 5:6], in_=scf[:, 3:4],
                     func=ACT.Exp, bias=0.0, scale=1.0)
                P.op("scalar", "activation", out=scf[:, 6:7], in_=scf[:, 4:5],
                     func=ACT.Exp, bias=0.0, scale=1.0, inc=("tsl", 1))
                P.wait("vector", "tsl", P.mark("tsl"))
                P.op("vector", "tensor_add", out=scf[:, 7:8], in0=scf[:, 5:6],
                     in1=scf[:, 6:7])
                P.op("vector", "reciprocal", out=scf[:, 7:8], in_=scf[:, 7:8])
                P.op("vector", "tensor_mul", out=scf[:, 5:6], in0=scf[:, 5:6],
                     in1=scf[:, 7:8])
                P.op("vector", "tensor_mul", out=scf[:, 6:7], in0=scf[:, 6:7],
                     in1=scf[:, 7:8])
                P.op("vector", "tensor_scalar", out=ce_all[:, tt2 * 8:tt2 * 8 + 8],
                     in0=prb[:], scalar1=scf[:, 3:4], scalar2=None, op0=ALU.is_ge)
                P.op("vector", "tensor_scalar", out=ce_all[:, tt2 * 8:tt2 * 8 + 8],
                     in0=ce_all[:, tt2 * 8:tt2 * 8 + 8], scalar1=scf[:, 5:6],
                     scalar2=None, op0=ALU.mult)
                P.op("vector", "tensor_scalar", out=prm[:], in0=prm[:],
                     scalar1=scf[:, 4:5], scalar2=None, op0=ALU.is_ge)
                P.op("vector", "tensor_scalar", out=prm[:], in0=prm[:],
                     scalar1=scf[:, 6:7], scalar2=None, op0=ALU.mult)
                P.op("vector", "tensor_add", out=ce_all[:, tt2 * 8:tt2 * 8 + 8],
                     in0=ce_all[:, tt2 * 8:tt2 * 8 + 8], in1=prm[:], inc=("tv", 1))
                if dbg and bb == 1 and sc == 1 and tt2 == 7:
                    P.wait("gpsimd", "tv", P.mark("tv"))
                    P.dma("gpsimd", "qdbg", out=bass.AP(dbg_d["d_rt"].tensor, 0,
                                                        [[64, 128], [1, 8]]),
                          in_=prb[:])
                    P.dma("gpsimd", "qdbg", out=bass.AP(dbg_d["d_rt"].tensor, 8,
                                                        [[64, 128], [1, 8]]),
                          in_=prm[:])
                    P.dma("gpsimd", "qdbg", out=bass.AP(dbg_d["d_rt"].tensor, 16,
                                                        [[64, 128], [1, 8]]),
                          in_=scf[:, 0:8])
                    P.dma("gpsimd", "qdbg", out=bass.AP(dbg_d["d_rt"].tensor, 24,
                                                        [[64, 128], [1, 8]]),
                          in_=ce_all[:, tt2 * 8:tt2 * 8 + 8])
                    P.wait("vector", "qdbg")
                    P.wait("scalar", "qdbg")
            P.wait("gpsimd", "tv", P.mark("tv"))
            P.wait("gpsimd", "qyl")
            for e in range(NE):
                P.dma("gpsimd", "qys",
                      out=bass.AP(ce_d.tensor, e * 2048 + T0, [[1, 128], [128, NTF]]),
                      in_=ce_all[:].rearrange("p (t e) -> p t e", e=NE)[:, :, e],
                      allow_slow_non_contiguous=True)
            P.wait("sync", "qys")
            P.wait("vector", "qo", out_store_m[0])
            P.op("vector", "memset", spec_sb[:], 0.0, inc=("tv", 1))
            for e in range(NE + 1):
                shared = (e == NE)
                e9 = NE if shared else e
                P.wait("sync", "tt", wgu_free[0])
                if first_pass:
                    if wbm[0] > 0:
                        P.wait("vector", "qg", wbm[0])
                    if shared:
                        stream_weight(shg_in, wg_sb, 8, DFF, True)
                        stream_weight(shu_in, wu_sb, 8, DFF, True)
                    else:
                        stream_weight(eg_in, wg_sb, 8, DFF, True, e)
                        stream_weight(eu_in, wu_sb, 8, DFF, True, e)
                    gu_ready = ("tv", P.mark("tv"))
                else:
                    if pass2_first[0]:
                        P.wait("sync", "qg")
                        pass2_first[0] = False
                    P.dma("sync", "qw", out=wg_sb[:], in_=wfg_d[e9])
                    P.dma("sync", "qw", out=wu_sb[:], in_=wfu_d[e9])
                    gu_ready = ("qw", P.mark("qw"))
                if not shared:
                    P.wait("sync", "tv", ceb_free[0])
                    P.dma("sync", "qyl", out=ce_b[:],
                          in_=bass.AP(ce_d.tensor, e * 2048 + T0, [[0, 128], [1, TCF]]))
                if first_pass:
                    P.wait("gpsimd", "tv", gu_ready[1])
                    P.dma("gpsimd", "qg", out=wfg_d[e9], in_=wg_sb[:])
                    P.dma("gpsimd", "qg", out=wfu_d[e9], in_=wu_sb[:])
                    wbm[0] = P.mark("qg")
                P.wait("tensor", gu_ready[0], gu_ready[1])
                for ft in range(16):
                    for tn in range(TCF // 512):
                        sl = (ft * 2 + tn) % 2
                        pg = (p_g0, p_g1)[sl]
                        pu = (p_u0, p_u1)[sl]
                        if len(act_marks) >= 2:
                            P.wait("tensor", "tv", act_marks[-2])
                        for kt in range(8):
                            P.op("tensor", "matmul", pg[:],
                                 wg_sb[:, kt * DFF + ft * 128: kt * DFF + ft * 128 + 128],
                                 h2_cm[:, kt * TCF + tn * 512: kt * TCF + tn * 512 + 512],
                                 start=(kt == 0), stop=(kt == 7))
                        for kt in range(8):
                            P.op("tensor", "matmul", pu[:],
                                 wu_sb[:, kt * DFF + ft * 128: kt * DFF + ft * 128 + 128],
                                 h2_cm[:, kt * TCF + tn * 512: kt * TCF + tn * 512 + 512],
                                 start=(kt == 0), stop=(kt == 7))
                        P.inc_last("tensor", "tt")
                        P.wait("scalar", "tt", P.mark("tt"))
                        sgv = sg_sb[:, tn * 512:tn * 512 + 512]
                        P.op("scalar", "activation", out=sgv, in_=pg[:], func=ACT.Silu,
                             bias=0.0, scale=1.0, inc=("tsl", 1))
                        P.wait("vector", "tsl", P.mark("tsl"))
                        dst = act_sb[:, ft * TCF + tn * 512: ft * TCF + tn * 512 + 512]
                        P.op("vector", "tensor_mul", out=dst, in0=sgv, in1=pu[:])
                        if not shared:
                            P.wait("vector", "qyl")
                            P.op("vector", "tensor_mul", out=dst, in0=dst,
                                 in1=ce_b[:, tn * 512:tn * 512 + 512])
                        P.inc_last("vector", "tv")
                        act_marks.append(P.mark("tv"))
                act_ready = P.mark("tv")
                ceb_free[0] = act_ready
                wgu_free[0] = P.mark("tt")
                P.wait("sync", "tt", wd_free[0])
                if first_pass:
                    if wbm[0] > 0:
                        P.wait("vector", "qg", wbm[0])
                    stream_weight(shd_in if shared else ed_in, wd_sb, 16, 1024, False,
                                  None if shared else e)
                    wd_ready = ("tv", P.mark("tv"))
                    P.wait("gpsimd", "tv", wd_ready[1])
                    P.dma("gpsimd", "qg", out=wfd_d[e9], in_=wd_sb[:])
                    wbm[0] = P.mark("qg")
                else:
                    P.dma("sync", "qw", out=wd_sb[:], in_=wfd_d[e9])
                    wd_ready = ("qw", P.mark("qw"))
                P.wait("tensor", wd_ready[0], wd_ready[1])
                P.wait("tensor", "tv", act_ready)
                for co in range(8):
                    for tn in range(TCF // 512):
                        pd = (p_d0, p_d1)[(co * 2 + tn) % 2]
                        if len(dn_marks) >= 2:
                            P.wait("tensor", "tv", dn_marks[-2])
                        for ft in range(16):
                            P.op("tensor", "matmul", pd[:],
                                 wd_sb[:, ft * 1024 + co * 128: ft * 1024 + co * 128 + 128],
                                 act_sb[:, ft * TCF + tn * 512: ft * TCF + tn * 512 + 512],
                                 start=(ft == 0), stop=(ft == 15))
                        P.inc_last("tensor", "tt")
                        P.wait("vector", "tt", P.mark("tt"))
                        dsts = spec_sb[:, co * TCF + tn * 512: co * TCF + tn * 512 + 512]
                        P.op("vector", "tensor_add", out=dsts, in0=dsts, in1=pd[:],
                             inc=("tv", 1))
                        dn_marks.append(P.mark("tv"))
                wd_free[0] = P.mark("tt")
            spec_ready = P.mark("tv")
            P.wait("sync", f_x2_free[0][0], f_x2_free[0][1])
            P.dma("sync", "qx", out=x2cf[:].rearrange("p (n c) -> p n c", c=1024),
                  in_=_tm(ss_d[bb], T0, NTF))
            P.wait("tensor", "tv", spec_ready)
            for tt2 in range(NTF):
                if moet_free[0] > 0:
                    P.wait("tensor", "tv", moet_free[0])
                for kt in range(8):
                    P.op("tensor", "transpose", p_tr6[:],
                         spec_sb[:, kt * TCF + tt2 * 128: kt * TCF + tt2 * 128 + 128],
                         ident_h[:], inc=("tt", 1))
                    P.wait("vector", "tt", P.mark("tt"))
                    P.op("vector", "tensor_copy", out=moet1[:, kt * 128:kt * 128 + 128],
                         in_=p_tr6[:], inc=("tv", 1))
                    P.wait("tensor", "tv", P.mark("tv"))
                if tt2 == 0:
                    P.wait("vector", "qx")
                P.op("vector", "tensor_add", out=moet1[:], in0=moet1[:],
                     in1=x2cf[:, tt2 * 1024:tt2 * 1024 + 1024], inc=("tv", 1))
                moet_free[0] = P.mark("tv")
                f_x2_free[0] = ("tv", P.mark("tv"))
                P.wait("scalar", "tv", P.mark("tv"))
                P.op("scalar", "activation", out=act_sb[:, 0:1024], in_=moet1[:],
                     func=ACT.Copy, bias=0.0, scale=1.0, accum_out=stats4[:, 0:1],
                     inc=("tsl", 1))
                P.wait("vector", "tsl", P.mark("tsl"))
                P.op("vector", "tensor_scalar", out=stats4[:, 1:2], in0=stats4[:, 0:1],
                     scalar1=1.0 / 1024, scalar2=None, op0=ALU.mult)
                P.op("vector", "tensor_scalar", out=moet1[:], in0=moet1[:],
                     scalar1=stats4[:, 1:2], scalar2=None, op0=ALU.subtract,
                     inc=("tv", 1))
                P.wait("scalar", "tv", P.mark("tv"))
                P.op("scalar", "activation", out=act_sb[:, 0:1024], in_=moet1[:],
                     func=ACT.Square, bias=0.0, scale=1.0, accum_out=stats4[:, 2:3])
                P.op("scalar", "activation", out=stats4[:, 3:4], in_=stats4[:, 2:3],
                     func=ACT.Sqrt, bias=eps5_sb[:], scale=1.0 / 1024, inc=("tsl", 1))
                P.wait("vector", "tsl", P.mark("tsl"))
                P.op("vector", "reciprocal", out=stats4[:, 4:5], in_=stats4[:, 3:4])
                ov = outc[:, 0:1024]
                if len(out_t_marks) >= 1:
                    P.wait("vector", "qo", out_t_marks[-1])
                P.op("vector", "tensor_scalar", out=ov, in0=moet1[:],
                     scalar1=stats4[:, 4:5], scalar2=None, op0=ALU.mult,
                     inc=("tv", 1))
                P.wait("gpsimd", "tv", P.mark("tv"))
                P.dma("gpsimd", "qo", out=_tm(out_d[bb], T0 + tt2 * 128, 1),
                      in_=ov.rearrange("p (n c) -> p n c", c=1024))
                out_t_marks.append(P.mark("qo"))
            out_store_m[0] = P.mark("qo")

    es4.close()
    P.wait("gpsimd", "qo", P.mark("qo"))
    P.wait("gpsimd", "qdbg")
    return P, dr, es0


def host_inputs(inputs):
    x = np.ascontiguousarray(inputs["L0_attn_x"])
    maps = []
    w_in_t = np.ascontiguousarray(inputs["L0_attn_W_in"].T)
    w_x_t = np.ascontiguousarray(inputs["L0_attn_W_x"].T)
    w_dt_t = np.ascontiguousarray(inputs["L0_attn_W_dt"].T)
    w_out_t = np.ascontiguousarray(inputs["L0_attn_W_out"].T)
    convw = np.ascontiguousarray(inputs["L0_attn_conv_w"][:, 0, :])
    ident = np.eye(128, dtype=np.float32)
    common = dict(
        w_in_t=w_in_t, w_x_t=w_x_t, w_dt_t=w_dt_t, w_out_t=w_out_t,
        gamma=inputs["L0_attn_gamma"], step=inputs["L0_attn_step_size"],
        convw=convw, convb=inputs["L0_attn_conv_b"], bdt=inputs["L0_attn_b_dt"],
        ident=ident,
        n1w=inputs["L0_ffn_norm1_w"], n2w=inputs["L0_ffn_norm2_w"],
        wq=inputs["L0_ffn_wq"], wk=inputs["L0_ffn_wk"], wv=inputs["L0_ffn_wv"],
        wo=inputs["L0_ffn_wo"], rw=inputs["L0_ffn_router_w"],
        shg=inputs["L0_ffn_sh_gate"], shu=inputs["L0_ffn_sh_up"],
        shd=inputs["L0_ffn_sh_down"], eg=inputs["L0_ffn_e_gate"],
        eu=inputs["L0_ffn_e_up"], ed=inputs["L0_ffn_e_down"],
    )
    for c in range(NCORES):
        m = dict(common)
        m["x"] = np.ascontiguousarray(x[c * BPC:(c + 1) * BPC])
        maps.append(m)
    return maps


def build_nc(dbg=False, stop_after=None):
    nc = bass.Bass("TRN2", target_bir_lowering=False, detect_race_conditions=False)
    P, dr, es0 = build_core(nc, dbg=dbg, stop_after=stop_after)
    with nc.Block() as block:
        P.build(block)
    print("instr counts:", P.nops())
    return nc


def kernel(**inputs):
    nc = build_nc(dbg=False)
    maps = host_inputs(inputs)
    res = run_bass_kernel_spmd(nc, maps, core_ids=list(range(NCORES)))
    return np.concatenate([np.asarray(r["out"]) for r in res.results], axis=0)
